# version 07 (final)
# speedup vs baseline: 1.4772x; 1.0932x over previous
import numpy as np
import ml_dtypes
from contextlib import ExitStack

import concourse.bass as bass
import concourse.mybir as mybir
from concourse.bass_utils import run_bass_kernel_spmd

F32 = mybir.dt.float32
BF16 = mybir.dt.bfloat16
AF = mybir.ActivationFunctionType
ALU = mybir.AluOpType
BF = ml_dtypes.bfloat16

NCORES = 8
T_OWN = 2048
T_FULL = 4096
D = 1024
NCH = 8
NW_OWN = 4
NW_FULL = 8
NEXP = 8
DH = 4096
NDH = 32
GRP = 8
NGRP = NDH // GRP

_ENGINES = ("sync", "scalar", "vector", "tensor", "gpsimd")


class _Sem:

    def __init__(self, state, name):
        self.state = state
        self.base = state[1]
        self.name = name
        self.n = 0

    @property
    def h(self):
        return self.state[0]

    def inc(self, k):
        self.n += k
        self.state[1] += k
        return self.n


class _Prog:

    def __init__(self, nc, stack):
        self.nc = nc
        self.stack = stack
        self.items = {e: [] for e in _ENGINES}
        self.prefix = ""
        self._sem_state = {}

    def sem(self, name):
        if name not in self._sem_state:
            self._sem_state[name] = [self.stack.enter_context(self.nc.semaphore(name)), 0]
        return _Sem(self._sem_state[name], self.prefix + name)

    def barrier(self, colls=()):
        if not hasattr(self, "s_ph"):
            sv = self.prefix
            self.prefix = ""
            self.s_ph = self.sem("s_ph")
            self.s_cc = self.sem("s_cc")
            self.prefix = sv
        for (kind, op, i_ap, o_ap) in colls:
            tot = self.op("gpsimd", lambda e, kind=kind, op=op, i_ap=i_ap, o_ap=o_ap:
                          e.collective_compute(kind, op,
                                               replica_groups=[[0, 1], [2, 3], [4, 5], [6, 7]],
                                               ins=[i_ap], outs=[o_ap]), inc=self.s_cc)
            self.wait("gpsimd", self.s_cc, tot)
        for eng in _ENGINES:
            self.op(eng, lambda e: e.nop(), inc=self.s_ph)
        for eng in _ENGINES:
            self.wait(eng, self.s_ph, self.s_ph.n)

    def wait(self, eng, sem, val):
        if val <= 0:
            return
        assert val <= sem.n, f"wait {sem.name}>={val} can never fire (emitted {sem.n})"
        self.items[eng].append(lambda e, s=sem, v=val + sem.base: e.wait_ge(s.h, v))

    def op(self, eng, fn, inc=None, inc_by=1):
        if inc is None:
            self.items[eng].append(fn)
            return None
        tot = inc.inc(inc_by)
        self.items[eng].append(lambda e, f=fn, s=inc, k=inc_by: f(e).then_inc(s.h, k))
        return tot

    def opw(self, eng, fn):
        if not hasattr(self, "_selfs"):
            self._selfs = {}
        if eng not in self._selfs:
            self._selfs[eng] = self.sem(f"_self_{eng}")
        sem = self._selfs[eng]
        tot = self.op(eng, fn, inc=sem)
        self.wait(eng, sem, tot)

    def dma(self, eng, out, in_, inc):
        return self.op(eng, lambda e, o=out, i=in_: e.dma_start(out=o, in_=i), inc=inc, inc_by=16)

    def replay(self, block):
        def mk(eng):
            def body(e):
                for fn in self.items[eng]:
                    fn(e)
            return body
        for eng in _ENGINES:
            if self.items[eng]:
                getattr(block, eng)(mk(eng))




class _Bufs:

    def __init__(self, nc, stack, pool=None, banks=None):
        self.nc = nc
        self.stack = stack
        self.pool = pool
        self.banks = banks
        self.off = 0
        self.bank_i = 0
        if pool is not None:
            self.views = {F32: pool.bitcast(F32), BF16: pool.bitcast(BF16)}
            self.pool_bytes = pool.shape[1]

    def reset(self):
        self.off = 0
        self.bank_i = 0

    def sb(self, name, shape, dtype):
        if self.pool is None:
            return self.stack.enter_context(self.nc.sbuf_tensor(name, shape, dtype))
        assert len(shape) == 2
        esz = 4 if dtype == F32 else 2
        self.off = (self.off + 63) // 64 * 64
        start = self.off // esz
        n = shape[1]
        self.off += n * esz
        assert self.off <= self.pool_bytes, f"pool overflow at {name}: {self.off}"
        return self.views[dtype][0:shape[0], start:start + n]

    def ps(self, name, shape, dtype):
        if self.banks is None:
            return self.stack.enter_context(self.nc.psum_tensor(name, shape, dtype))
        b = self.banks[self.bank_i]
        self.bank_i += 1
        assert self.bank_i <= 8, "psum banks exhausted"
        return b[0:shape[0], 0:shape[1]]

    def banks8(self, name):
        if self.banks is not None:
            assert self.bank_i == 0, "banks8 conflicts with ps() carving"
            self.bank_i = 8
            return list(self.banks)
        return [self.stack.enter_context(self.nc.psum_tensor(f"{name}{j}", [128, 512], F32))
                for j in range(8)]

    def mark(self):
        return self.off

    def rewind(self, off):
        assert self.pool is not None
        self.off = off

    def seek(self, off):
        assert self.pool is not None and off >= self.off - 0
        self.off = max(self.off, off)


def _tile_ffn_w(w):
    E = w.shape[0]
    return np.ascontiguousarray(
        w.reshape(E, NCH, 128, NDH, 128).transpose(0, 3, 2, 1, 4).reshape(E, NDH, 128, NCH * 128))


def _perm_pairs(dh):
    return np.concatenate([np.arange(0, dh, 2), np.arange(1, dh, 2)])


def _rope_tables(n_heads_in_tile, dhalf, npos=T_FULL, theta=10000.0):
    j = np.arange(dhalf, dtype=np.float64)
    inv = theta ** (-2.0 * j / (2.0 * dhalf))
    pos = np.arange(npos, dtype=np.float64)
    ang = pos[None, :] * inv[:, None]
    cos = np.cos(ang)
    sin = np.sin(ang)
    ta_h = np.concatenate([cos, cos], axis=0)
    tb_h = np.concatenate([-sin, sin], axis=0)
    ta = np.tile(ta_h, (n_heads_in_tile, 1))
    tb = np.tile(tb_h, (n_heads_in_tile, 1))
    return ta.astype(BF), tb.astype(BF)


def _perm_mat(n, half):
    P = np.zeros((n, n), dtype=np.float32)
    for m in range(n):
        blk = m // (2 * half)
        r = m % (2 * half)
        src = blk * 2 * half + ((r + half) % (2 * half))
        P[src, m] = 1.0
    return P.astype(BF)


def _pad128(a):
    if a.shape[0] == 128:
        return a
    out = np.zeros((128,) + a.shape[1:], dtype=a.dtype)
    out[:a.shape[0]] = a
    return out




def build_attn0():
    nc = bass.Bass("TRN2", target_bir_lowering=False)
    xnin = nc.dram_tensor("xnin", [2, D, T_OWN], BF16, kind="ExternalInput").ap()
    wq = nc.dram_tensor("wq", [D, 512], F32, kind="ExternalInput").ap()
    wk = nc.dram_tensor("wk", [D, 512], F32, kind="ExternalInput").ap()
    wv = nc.dram_tensor("wv", [D, 512], F32, kind="ExternalInput").ap()
    bq = nc.dram_tensor("bq", [128, 4], F32, kind="ExternalInput").ap()
    bk = nc.dram_tensor("bk", [128, 4], F32, kind="ExternalInput").ap()
    bv = nc.dram_tensor("bv", [1, 512], F32, kind="ExternalInput").ap()
    ta = nc.dram_tensor("ta", [128, T_FULL], BF16, kind="ExternalInput").ap()
    tb = nc.dram_tensor("tb", [128, T_FULL], BF16, kind="ExternalInput").ap()
    perm = nc.dram_tensor("perm", [128, 128], BF16, kind="ExternalInput").ap()
    attn = nc.dram_tensor("attn", [512, T_FULL], BF16, kind="ExternalOutput").ap()

    with ExitStack() as stack:
        en = stack.enter_context
        p = _Prog(nc, stack)
        xw = [en(nc.sbuf_tensor(f"xw{j}", [128, NCH * 512], BF16)) for j in range(2)]
        wst = [en(nc.sbuf_tensor(f"wst{j}", [128, 512], F32)) for j in range(2)]
        wqb = en(nc.sbuf_tensor("wqb", [128, NCH * 512], BF16))
        wkb = en(nc.sbuf_tensor("wkb", [128, NCH * 512], BF16))
        wvb = en(nc.sbuf_tensor("wvb", [128, NCH * 512], BF16))
        bqs = en(nc.sbuf_tensor("bqs", [128, 4], F32))
        bks = en(nc.sbuf_tensor("bks", [128, 4], F32))
        bvs = en(nc.sbuf_tensor("bvs", [1, 512], F32))
        bvb = en(nc.sbuf_tensor("bvb", [1, 512], BF16))
        tas = en(nc.sbuf_tensor("tas", [128, T_FULL], BF16))
        tbs = en(nc.sbuf_tensor("tbs", [128, T_FULL], BF16))
        prm = en(nc.sbuf_tensor("prm", [128, 128], BF16))
        onebf1 = en(nc.sbuf_tensor("onebf1", [1, 128], BF16))
        kbuf = en(nc.sbuf_tensor("kbuf", [128, 4 * T_FULL], BF16))
        vbuf = en(nc.sbuf_tensor("vbuf", [128, 32 * 520], BF16))
        qbuf = [en(nc.sbuf_tensor(f"qbuf{j}", [128, 4 * 512], BF16)) for j in range(2)]
        qtmp = [en(nc.sbuf_tensor(f"qtmp{j}", [128, 512], BF16)) for j in range(2)]
        rs1 = en(nc.sbuf_tensor("rs1", [128, 512], F32))
        rs2 = en(nc.sbuf_tensor("rs2", [128, 512], F32))
        pt = [en(nc.sbuf_tensor(f"pt{j}", [128, 512], BF16)) for j in range(3)]
        ubuf = en(nc.sbuf_tensor("ubuf", [64, 8 * 512], BF16))
        dbuf = en(nc.sbuf_tensor("dbuf", [1, 8 * 512], F32))
        onef1 = en(nc.sbuf_tensor("onef1", [1, 128], F32))
        anrm = [en(nc.sbuf_tensor(f"anrm{j}", [64, 512], BF16)) for j in range(2)]

        pj = [en(nc.psum_tensor(f"pj{j}", [128, 512], F32)) for j in range(2)]
        pp = en(nc.psum_tensor("pp", [128, 512], F32))
        ss = [en(nc.psum_tensor(f"ss{j}", [128, 512], F32)) for j in range(2)]
        pv = [en(nc.psum_tensor(f"pv{j}", [128, 512], F32)) for j in range(2)]
        pd = en(nc.psum_tensor("pd", [64, 512], F32))

        s_w0 = p.sem("s_w0")
        s_w1 = p.sem("s_w1")
        s_wc = p.sem("s_wc")
        s_misc = p.sem("s_misc")
        s_ic = p.sem("s_ic")
        s_xw = [p.sem("s_xw0"), p.sem("s_xw1")]
        s_pj = p.sem("s_pj")
        s_qt = p.sem("s_qt")
        s_pp = p.sem("s_pp")
        s_rope = p.sem("s_rope")
        s_v = p.sem("s_v")
        s_smm = p.sem("s_smm")
        s_exp = p.sem("s_exp")
        s_msk = p.sem("s_msk")
        s_pvm = p.sem("s_pvm")
        s_ue = p.sem("s_ue")
        s_di = p.sem("s_di")
        s_dr = p.sem("s_dr")
        s_an = p.sem("s_an")
        s_ao = [p.sem("s_ao0"), p.sem("s_ao1")]

        p.dma("scalar", bqs[:, :], bq[:, :], inc=s_misc)
        p.dma("scalar", bks[:, :], bk[:, :], inc=s_misc)
        p.dma("scalar", bvs[:, :], bv[:, :], inc=s_misc)
        p.dma("scalar", tas[:, :], ta[:, :], inc=s_misc)
        p.dma("scalar", tbs[:, :], tb[:, :], inc=s_misc)
        p.dma("scalar", prm[:, :], perm[:, :], inc=s_misc)
        nmisc = s_misc.n
        p.wait("vector", s_misc, nmisc)
        p.op("vector", lambda e: e.memset(onebf1[:, :], 1.0))
        p.op("vector", lambda e: e.memset(onef1[:, :], 1.0))
        p.op("vector", lambda e: e.tensor_copy(out=bvb[:, :], in_=bvs[:, :]))
        p.op("vector", lambda e: e.memset(vbuf[:, :].rearrange("p (t h c) -> p t h c", t=32, h=8)[:, :, :, 64:65], 1.0),
             inc=s_ic)

        wmats = [(wq, wqb), (wk, wkb), (wv, wvb)]
        wl = 0
        for (src, dst) in wmats:
            for c in range(NCH):
                sl = wl % 2
                ssem = s_w0 if sl == 0 else s_w1
                if wl >= 2:
                    p.wait("sync", s_wc, wl - 1)
                p.dma("sync", wst[sl][:, :], src[c * 128:(c + 1) * 128, :], inc=ssem)
                p.wait("vector", ssem, 16 * (wl // 2 + 1))
                p.op("vector", lambda e, sl=sl, dst=dst, c=c: e.tensor_copy(out=dst[:, c * 512:(c + 1) * 512],
                                                                            in_=wst[sl][:, :]), inc=s_wc)
                wl += 1

        n_rope = 0
        n_pj = 0
        n_qt = 0
        n_pp = 0
        n_v = 0
        n_smm = 0
        n_exp = 0
        n_msk = 0
        n_pvm = 0
        n_ue = 0
        n_an = 0

        smm_after_window = {}
        for w in range(NW_FULL):
            half, w2 = w // 4, w % 4
            for c in range(NCH):
                if w >= 2:
                    p.wait("sync", s_pj, (w - 1) * 12)
                p.dma("sync", xw[w % 2][:, c * 512:(c + 1) * 512],
                      xnin[half, c * 128:(c + 1) * 128, w2 * 512:(w2 + 1) * 512], inc=s_xw[w % 2])

            if w >= 2:
                p.wait("vector", s_smm, smm_after_window[w - 2])
            for (qi, (wb, bias_s, dstbuf)) in enumerate([(wqb, bqs, None), (wkb, bks, None)]):
                for c in range(4):
                    pjs = n_pj % 2
                    if n_pj >= 2:
                        p.wait("tensor", s_qt, n_pj - 1)
                    if c == 0 and qi == 0:
                        p.wait("tensor", s_xw[w % 2], 16 * NCH * (w // 2 + 1))
                        if w == 0:
                            p.wait("tensor", s_wc, 24)
                            p.wait("tensor", s_ic, 1)
                    for k in range(NCH):
                        p.op("tensor", lambda e, pjs=pjs, wb=wb, c=c, k=k, w=w:
                             e.matmul(pj[pjs][:, :], wb[:, k * 512 + c * 128: k * 512 + c * 128 + 128],
                                      xw[w % 2][:, k * 512:(k + 1) * 512],
                                      start=(k == 0), stop=(k == NCH - 1)),
                             inc=(s_pj if k == NCH - 1 else None))
                    n_pj += 1
                    p.wait("scalar", s_pj, n_pj)
                    gqk = n_rope
                    if gqk >= 2:
                        p.wait("scalar", s_pp, gqk - 1)
                        p.wait("scalar", s_rope, gqk - 1)
                    p.op("scalar", lambda e, pjs=pjs, bias_s=bias_s, c=c:
                         e.activation(out=qtmp[pjs][:, :], in_=pj[pjs][:, :], func=AF.Identity, bias=bias_s[:, c:c + 1]),
                         inc=s_qt)
                    n_qt += 1
                    p.wait("tensor", s_qt, n_qt)
                    if n_pp >= 1:
                        p.wait("tensor", s_rope, n_pp)
                    p.op("tensor", lambda e, pjs=pjs: e.matmul(pp[:, :], prm[:, :], qtmp[pjs][:, :],
                                                               start=True, stop=True), inc=s_pp)
                    n_pp += 1
                    p.wait("vector", s_qt, n_qt)
                    p.wait("vector", s_pp, n_pp)
                    p.opw("vector", lambda e, pjs=pjs, w=w: e.tensor_mul(out=rs1[:, :], in0=qtmp[pjs][:, :],
                                                                        in1=tas[:, w * 512:(w + 1) * 512]))
                    p.opw("vector", lambda e, w=w: e.tensor_mul(out=rs2[:, :], in0=pp[:, :],
                                                                in1=tbs[:, w * 512:(w + 1) * 512]))
                    if qi == 0:
                        p.op("vector", lambda e, c=c, w=w: e.tensor_add(out=qbuf[w % 2][:, c * 512:(c + 1) * 512],
                                                                        in0=rs1[:, :], in1=rs2[:, :]), inc=s_rope)
                    else:
                        p.op("vector", lambda e, c=c, w=w: e.tensor_add(out=kbuf[:, c * T_FULL + w * 512: c * T_FULL + w * 512 + 512],
                                                                        in0=rs1[:, :], in1=rs2[:, :]), inc=s_rope)
                    n_rope += 1

            for j in range(4):
                kt = 4 * w + j
                pjs = n_pj % 2
                if n_pj >= 2:
                    p.wait("tensor", s_qt, n_pj - 1)
                for k in range(NCH):
                    p.op("tensor", lambda e, pjs=pjs, j=j, k=k, w=w:
                         e.matmul(pj[pjs][:, :], xw[w % 2][:, k * 512 + j * 128: k * 512 + j * 128 + 128],
                                  wvb[:, k * 512:(k + 1) * 512],
                                  start=(k == 0), stop=False))
                p.op("tensor", lambda e, pjs=pjs: e.matmul(pj[pjs][:, :], onebf1[0:1, :], bvb[0:1, :],
                                                           start=False, stop=True), inc=s_pj)
                n_pj += 1
                p.wait("scalar", s_pj, n_pj)
                p.op("scalar", lambda e, pjs=pjs, kt=kt:
                     e.activation(out=vbuf[:, kt * 520: kt * 520 + 520].rearrange("p (h c) -> p h c", h=8)[:, :, 0:64],
                                  in_=pj[pjs][:, :], func=AF.Copy),
                     inc=s_qt)
                n_qt += 1
                n_v += 1

            nkt = 4 * w + 4
            p.wait("tensor", s_rope, (w + 1) * 8)
            p.wait("tensor", s_qt, n_qt)
            for h in range(8):
                hb = (h % 2) * 64
                g = h // 2
                if h == 0:
                    p.wait("vector", s_an, 8 * w)
                    p.wait("vector", s_di, w)
                for kt in range(nkt):
                    sb = n_smm % 2
                    if n_smm >= 2:
                        p.wait("tensor", s_exp, n_smm - 1)
                    p.op("tensor", lambda e, sb=sb, hb=hb, g=g, kt=kt, w=w:
                         e.matmul(ss[sb][:, :], kbuf[hb:hb + 64, g * T_FULL + kt * 128: g * T_FULL + kt * 128 + 128],
                                  qbuf[w % 2][hb:hb + 64, g * 512:(g + 1) * 512],
                                  start=True, stop=True), inc=s_smm)
                    n_smm += 1
                    p.wait("scalar", s_smm, n_smm)
                    if n_exp >= 3:
                        p.wait("scalar", s_pvm, n_exp - 2)
                    p.op("scalar", lambda e, sb=sb, slot=n_exp % 3:
                         e.activation(out=pt[slot][:, :], in_=ss[sb][:, :], func=AF.Exp, scale=0.125), inc=s_exp)
                    n_exp += 1
                    diag = kt >= 4 * w
                    if diag:
                        p.wait("gpsimd", s_exp, n_exp)
                        p.op("gpsimd", lambda e, slot=(n_exp - 1) % 3, base=w * 512 - kt * 128:
                             e.affine_select(out=pt[slot][:, :], in_=pt[slot][:, :], compare_op=ALU.is_ge,
                                             fill=0.0, base=base, pattern=[[1, 512]], channel_multiplier=-1),
                             inc=s_msk)
                        n_msk += 1
                    p.wait("tensor", s_exp, n_exp)
                    if diag:
                        p.wait("tensor", s_msk, n_msk)
                    if kt == 0:
                        gh = 8 * w + h
                        p.wait("tensor", s_ue, 2 * (gh - 2) + 2)
                    p.op("tensor", lambda e, pvs=h % 2, kt=kt, h=h, slot=(n_exp - 1) % 3, nkt=nkt:
                         e.matmul(pv[pvs][0:65, :], vbuf[:, kt * 520 + h * 65: kt * 520 + h * 65 + 65],
                                  pt[slot][:, :], start=(kt == 0), stop=(kt == nkt - 1)), inc=s_pvm)
                    n_pvm += 1
                p.wait("vector", s_pvm, n_pvm)
                p.op("vector", lambda e, pvs=h % 2, h=h: e.tensor_copy(out=ubuf[0:64, h * 512:(h + 1) * 512],
                                                                       in_=pv[pvs][0:64, :]), inc=s_ue)
                p.op("vector", lambda e, pvs=h % 2, h=h: e.tensor_copy(out=dbuf[0:1, h * 512:(h + 1) * 512],
                                                                      in_=pv[pvs][64:65, :]), inc=s_ue)
                n_ue += 2

            smm_after_window[w] = n_smm
            p.wait("scalar", s_ue, n_ue)
            p.wait("scalar", s_dr, 8 * w)
            p.opw("scalar", lambda e: e.activation(out=dbuf[:, :], in_=dbuf[:, :], func=AF.Ln))
            p.op("scalar", lambda e: e.activation(out=dbuf[:, :], in_=dbuf[:, :], func=AF.Exp, scale=-1.0), inc=s_di)
            for h in range(8):
                p.wait("tensor", s_di, w + 1)
                if n_an >= 1:
                    p.wait("tensor", s_an, n_an)
                p.op("tensor", lambda e, h=h: e.matmul(pd[:, :], onef1[0:1, 0:64], dbuf[0:1, h * 512:(h + 1) * 512],
                                                       start=True, stop=True), inc=s_dr)
                n_dr = s_dr.n
                p.wait("vector", s_dr, n_dr)
                sl = n_an % 2
                if n_an >= 2:
                    p.wait("vector", s_ao[sl], 16 * (n_an // 2))
                p.op("vector", lambda e, h=h, sl=sl: e.tensor_mul(out=anrm[sl][:, :], in0=ubuf[0:64, h * 512:(h + 1) * 512],
                                                                  in1=pd[:, :]), inc=s_an)
                n_an += 1
                p.wait("sync", s_an, n_an)
                p.dma("sync", attn[h * 64:(h + 1) * 64, w * 512:(w + 1) * 512], anrm[sl][:, :], inc=s_ao[sl])

        p.wait("gpsimd", s_ao[0], s_ao[0].n)
        p.wait("gpsimd", s_ao[1], s_ao[1].n)

        block = en(nc.Block())
        p.replay(block)
    return nc




def build_tail0():
    nc = bass.Bass("TRN2", target_bir_lowering=False)
    xT = nc.dram_tensor("xT", [D, T_OWN], F32, kind="ExternalInput").ap()
    xnin = nc.dram_tensor("xnin", [D, T_OWN], BF16, kind="ExternalInput").ap()
    attn = nc.dram_tensor("attn", [D, T_OWN], BF16, kind="ExternalInput").ap()
    mrows = nc.dram_tensor("mrows", [NEXP, T_OWN], F32, kind="ExternalInput").ap()
    w1t = nc.dram_tensor("w1t", [NEXP, NDH, 128, D], F32, kind="ExternalInput").ap()
    b1c = nc.dram_tensor("b1c", [NEXP, 128, NDH], F32, kind="ExternalInput").ap()
    w2 = nc.dram_tensor("w2", [NEXP, DH, D], F32, kind="ExternalInput").ap()
    b2 = nc.dram_tensor("b2", [NEXP, D], F32, kind="ExternalInput").ap()
    woT = nc.dram_tensor("woT", [D, D], F32, kind="ExternalInput").ap()
    boc = nc.dram_tensor("boc", [128, NCH], F32, kind="ExternalInput").ap()
    x1 = nc.dram_tensor("x1", [D, T_OWN], F32, kind="ExternalOutput").ap()

    NSW, TSW, NWS = 2, 1024, 2

    with ExitStack() as stack:
        en = stack.enter_context
        p = _Prog(nc, stack)
        xnb = en(nc.sbuf_tensor("xnb", [128, NCH * TSW], BF16))
        one1f = en(nc.sbuf_tensor("one1f", [1, 128], F32))
        onec = en(nc.sbuf_tensor("onec", [128, 1], F32))
        mr = [en(nc.sbuf_tensor(f"mr{j}", [1, TSW], F32)) for j in range(2)]
        mrep = [en(nc.sbuf_tensor(f"mrep{j}", [128, 512], F32)) for j in range(2)]
        xm = [en(nc.sbuf_tensor(f"xm{j}", [128, NCH * 512], BF16)) for j in range(2)]
        w1st = [en(nc.sbuf_tensor(f"w1st{j}", [128, D], F32)) for j in range(2)]
        w1b = [en(nc.sbuf_tensor(f"w1b{j}", [128, D], BF16)) for j in range(2)]
        w2st = [en(nc.sbuf_tensor(f"w2st{j}", [128, D], F32)) for j in range(2)]
        w2b = en(nc.sbuf_tensor("w2b", [128, GRP * D], BF16))
        hbf = en(nc.sbuf_tensor("hbf", [128, GRP * NWS * 512], BF16))
        yacc = en(nc.sbuf_tensor("yacc", [128, NCH * TSW], F32))
        b1s = [en(nc.sbuf_tensor(f"b1s{j}", [128, NDH], F32)) for j in range(2)]
        b2s = en(nc.sbuf_tensor("b2s", [NEXP, D], F32))
        sb2 = en(nc.sbuf_tensor("sb2", [128, NCH], F32))
        wos = [en(nc.sbuf_tensor(f"wos{j}", [128, 512], F32)) for j in range(2)]
        wob = en(nc.sbuf_tensor("wob", [128, NCH * D], BF16))
        bos = en(nc.sbuf_tensor("bos", [128, NCH], F32))
        atw = [en(nc.sbuf_tensor(f"atw{j}", [128, NCH * 512], BF16)) for j in range(2)]
        xst = [en(nc.sbuf_tensor(f"xst{j}", [128, 512], F32)) for j in range(2)]
        ssb = [en(nc.sbuf_tensor(f"ssb{j}", [128, 512], F32)) for j in range(2)]
        gsb = [en(nc.sbuf_tensor(f"gsb{j}", [128, 512], BF16)) for j in range(2)]
        tmul = en(nc.sbuf_tensor("tmul", [128, 512], F32))
        x1s = [en(nc.sbuf_tensor(f"x1s{j}", [128, 512], F32)) for j in range(2)]

        pu = [en(nc.psum_tensor(f"pu{j}", [128, 512], F32)) for j in range(2)]
        py = [en(nc.psum_tensor(f"py{j}", [128, 512], F32)) for j in range(2)]
        pm = en(nc.psum_tensor("pm", [128, 512], F32))
        pss = [en(nc.psum_tensor(f"pss{j}", [128, 512], F32)) for j in range(2)]
        pb = en(nc.psum_tensor("pb", [128, 1], F32))

        s_fix = p.sem("s_fix")
        s_fx = [p.sem("s_fx0"), p.sem("s_fx1")]
        s_fixc = p.sem("s_fixc")
        s_sb2 = p.sem("s_sb2")
        s_sb2e = p.sem("s_sb2e")
        s_xnb = p.sem("s_xnb")
        s_mr = [p.sem("s_mr0"), p.sem("s_mr1")]
        s_b1 = [p.sem("s_b10"), p.sem("s_b11")]
        s_mm = p.sem("s_mm")
        s_mrep = p.sem("s_mrep")
        s_xm = p.sem("s_xm")
        s_w1 = [p.sem("s_w10"), p.sem("s_w11")]
        s_w1c = p.sem("s_w1c")
        s_w2 = [p.sem("s_w20"), p.sem("s_w21")]
        s_w2c = p.sem("s_w2c")
        s_u = p.sem("s_u")
        s_h = p.sem("s_h")
        s_y = p.sem("s_y")
        s_ya = p.sem("s_ya")
        s_ini = p.sem("s_ini")
        s_at = [p.sem("s_at0"), p.sem("s_at1")]
        s_xl = [p.sem("s_xl0"), p.sem("s_xl1")]
        s_sp = p.sem("s_sp")
        s_se = p.sem("s_se")
        s_ge = p.sem("s_ge")
        s_cmb = p.sem("s_cmb")
        s_out = [p.sem("s_out0"), p.sem("s_out1")]

        p.op("vector", lambda e: e.memset(one1f[:, :], 1.0))
        p.op("vector", lambda e: e.memset(onec[:, :], 1.0))
        wl = 0
        for c in range(NCH):
            for hf in range(2):
                sl = wl % 2
                if wl >= 2:
                    p.wait("scalar", s_fixc, wl - 1)
                p.dma("scalar", wos[sl][:, :], woT[c * 128:(c + 1) * 128, hf * 512:(hf + 1) * 512], inc=s_fx[sl])
                p.wait("vector", s_fx[sl], 16 * (wl // 2 + 1))
                p.op("vector", lambda e, sl=sl, c=c, hf=hf: e.tensor_copy(out=wob[:, c * D + hf * 512: c * D + (hf + 1) * 512],
                                                                          in_=wos[sl][:, :]), inc=s_fixc)
                wl += 1
        p.dma("scalar", bos[:, :], boc[:, :], inc=s_fix)
        p.dma("scalar", b2s[:, :], b2[:, :], inc=s_fix)
        nfix = s_fix.n
        p.wait("tensor", s_fix, nfix)
        for c in range(NCH):
            if c >= 1:
                p.wait("tensor", s_sb2e, c)
            p.op("tensor", lambda e, c=c: e.matmul(pb[:, :], b2s[:, c * 128:(c + 1) * 128], onec[0:NEXP, 0:1],
                                                   start=True, stop=True), inc=s_sb2)
            p.wait("vector", s_sb2, s_sb2.n)
            p.op("vector", lambda e, c=c: e.tensor_copy(out=sb2[:, c:c + 1], in_=pb[:, :]), inc=s_sb2e)

        n_u = n_h = n_y = n_ya = n_xm = n_mm = n_mrep = 0
        w1l = w2l = 0
        n_sp = n_se = n_ge = n_cmb = 0
        n_b1 = [0, 0]
        n_mrl = [0, 0]
        n_atl = [0, 0]
        n_xll = [0, 0]
        n_outl = [0, 0]

        for sw in range(NSW):
            if sw >= 1:
                p.wait("scalar", s_xm, sw * 128)
            for c in range(NCH):
                p.dma("scalar", xnb[:, c * TSW:(c + 1) * TSW],
                      xnin[c * 128:(c + 1) * 128, sw * TSW:(sw + 1) * TSW], inc=s_xnb)
            if sw >= 1:
                p.wait("vector", s_ge, sw * 16)
            p.op("vector", lambda e: e.memset(yacc[:, :], 0.0), inc=s_ini)

            for i in range(NEXP):
                gi = sw * NEXP + i
                msl = gi % 2
                if n_mrl[msl] >= 1:
                    p.wait("scalar", s_mm, 2 * (gi - 2) + 2)
                p.dma("scalar", mr[msl][0:1, :], mrows[i:i + 1, sw * TSW:(sw + 1) * TSW], inc=s_mr[msl])
                n_mrl[msl] += 1
                if n_b1[msl] >= 1:
                    p.wait("scalar", s_h, (gi - 1) * 64)
                p.dma("scalar", b1s[msl][:, :], b1c[i, :, :], inc=s_b1[msl])
                n_b1[msl] += 1

                for w2i in range(NWS):
                    p.wait("tensor", s_mr[msl], 16 * n_mrl[msl])
                    if n_mm >= 1:
                        p.wait("tensor", s_mrep, n_mm)
                    p.op("tensor", lambda e, msl=msl, w2i=w2i: e.matmul(pm[:, :], one1f[0:1, :],
                                                                       mr[msl][0:1, w2i * 512:(w2i + 1) * 512],
                                                                       start=True, stop=True), inc=s_mm)
                    n_mm += 1
                    p.wait("vector", s_mm, n_mm)
                    if gi >= 1:
                        p.wait("vector", s_xm, gi * 16)
                    p.op("vector", lambda e, w2i=w2i: e.tensor_copy(out=mrep[w2i][:, :], in_=pm[:, :]), inc=s_mrep)
                    n_mrep += 1
                    p.wait("vector", s_mrep, n_mrep)
                    p.wait("vector", s_xnb, 16 * NCH * (sw + 1))
                    if gi >= 1:
                        p.wait("vector", s_u, gi * 64)
                    for k in range(NCH):
                        p.op("vector", lambda e, w2i=w2i, k=k: e.tensor_mul(out=xm[w2i][:, k * 512:(k + 1) * 512],
                                                                           in0=xnb[:, k * TSW + w2i * 512: k * TSW + w2i * 512 + 512],
                                                                           in1=mrep[w2i][:, :]), inc=s_xm)
                        n_xm += 1

                for g in range(NGRP):
                    gg = gi * NGRP + g
                    for d8 in range(GRP):
                        d = g * GRP + d8
                        sl1 = w1l % 2
                        if w1l >= 2:
                            p.wait("sync", s_w1c, w1l - 1)
                        p.dma("sync", w1st[sl1][:, :], w1t[i, d, :, :], inc=s_w1[sl1])
                        p.wait("vector", s_w1[sl1], 16 * (w1l // 2 + 1))
                        if w1l >= 2:
                            p.wait("vector", s_u, (w1l - 1) * 2)
                        p.op("vector", lambda e, sl1=sl1: e.tensor_copy(out=w1b[sl1][:, :], in_=w1st[sl1][:, :]),
                             inc=s_w1c)
                        w1l += 1

                        sl2 = w2l % 2
                        if w2l >= 2:
                            p.wait("gpsimd", s_w2c, w2l - 1)
                        p.dma("gpsimd", w2st[sl2][:, :], w2[i, d * 128:(d + 1) * 128, :], inc=s_w2[sl2])
                        p.wait("vector", s_w2[sl2], 16 * (w2l // 2 + 1))
                        if gg >= 1:
                            p.wait("vector", s_y, gg * 16)
                        p.op("vector", lambda e, sl2=sl2, d8=d8: e.tensor_copy(out=w2b[:, d8 * D:(d8 + 1) * D],
                                                                              in_=w2st[sl2][:, :]), inc=s_w2c)
                        w2l += 1

                        for w2i in range(NWS):
                            if d8 == 0 and w2i == 0:
                                p.wait("tensor", s_xm, (gi + 1) * 16)
                            p.wait("tensor", s_w1c, w1l)
                            if n_u >= 2:
                                p.wait("tensor", s_h, n_u - 1)
                            for k in range(NCH):
                                p.op("tensor", lambda e, us=n_u % 2, sl1=sl1, k=k, w2i=w2i:
                                     e.matmul(pu[us][:, :], w1b[sl1][:, k * 128:(k + 1) * 128],
                                              xm[w2i][:, k * 512:(k + 1) * 512],
                                              start=(k == 0), stop=(k == NCH - 1)),
                                     inc=(s_u if k == NCH - 1 else None))
                            n_u += 1
                            p.wait("scalar", s_u, n_u)
                            p.wait("scalar", s_b1[msl], 16 * n_b1[msl])
                            if gg >= 1 and d8 == 0 and w2i == 0:
                                p.wait("scalar", s_y, gg * 16)
                            p.op("scalar", lambda e, us=(n_u - 1) % 2, d=d, d8=d8, w2i=w2i, msl=msl:
                                 e.activation(out=hbf[:, (d8 * NWS + w2i) * 512:(d8 * NWS + w2i) * 512 + 512],
                                              in_=pu[us][:, :], func=AF.Relu, bias=b1s[msl][:, d:d + 1]),
                                 inc=s_h)
                            n_h += 1

                    for w2i in range(NWS):
                        for o in range(NCH):
                            if w2i == 0 and o == 0:
                                p.wait("tensor", s_h, n_h)
                                p.wait("tensor", s_w2c, w2l)
                            if n_y >= 2:
                                p.wait("tensor", s_ya, n_y - 1)
                            for d8 in range(GRP):
                                p.op("tensor", lambda e, ys=n_y % 2, d8=d8, o=o, w2i=w2i:
                                     e.matmul(py[ys][:, :], w2b[:, d8 * D + o * 128: d8 * D + o * 128 + 128],
                                              hbf[:, (d8 * NWS + w2i) * 512:(d8 * NWS + w2i) * 512 + 512],
                                              start=(d8 == 0), stop=(d8 == GRP - 1)),
                                     inc=(s_y if d8 == GRP - 1 else None))
                            n_y += 1
                            p.wait("vector", s_y, n_y)
                            if n_ya == 0 or (sw == 1 and n_ya == 512):
                                p.wait("vector", s_ini, sw + 1)
                            p.op("vector", lambda e, ys=(n_y - 1) % 2, o=o, w2i=w2i:
                                 e.tensor_add(out=yacc[:, o * TSW + w2i * 512: o * TSW + w2i * 512 + 512],
                                              in0=yacc[:, o * TSW + w2i * 512: o * TSW + w2i * 512 + 512],
                                              in1=py[ys][:, :]), inc=s_ya)
                            n_ya += 1

            for w2i in range(NWS):
                gw = sw * NWS + w2i
                asl = gw % 2
                if n_atl[asl] >= 1:
                    p.wait("sync", s_sp, (gw - 1) * 8)
                for c in range(NCH):
                    p.dma("sync", atw[asl][:, c * 512:(c + 1) * 512],
                          attn[c * 128:(c + 1) * 128, sw * TSW + w2i * 512: sw * TSW + w2i * 512 + 512],
                          inc=s_at[asl])
                n_atl[asl] += 1
                for o in range(NCH):
                    if o == 0:
                        p.wait("tensor", s_at[asl], 16 * NCH * n_atl[asl])
                        p.wait("tensor", s_fixc, 16)
                    if n_sp >= 2:
                        p.wait("tensor", s_se, n_sp - 1)
                    for c in range(NCH):
                        p.op("tensor", lambda e, sps=n_sp % 2, c=c, o=o, asl=asl:
                             e.matmul(pss[sps][:, :], wob[:, c * D + o * 128: c * D + o * 128 + 128],
                                      atw[asl][:, c * 512:(c + 1) * 512],
                                      start=(c == 0), stop=(c == NCH - 1)),
                             inc=(s_sp if c == NCH - 1 else None))
                    n_sp += 1
                    p.wait("scalar", s_sp, n_sp)
                    if n_se >= 2:
                        p.wait("scalar", s_cmb, 2 * (n_se - 2) + 1)
                    p.op("scalar", lambda e, sps=(n_sp - 1) % 2, o=o, osl=n_se % 2:
                         e.activation(out=ssb[osl][:, :], in_=pss[sps][:, :], func=AF.Identity, bias=bos[:, o:o + 1]),
                         inc=s_se)
                    n_se += 1
                    if n_ge % 16 == 0:
                        p.wait("scalar", s_ya, 512 * (sw + 1))
                    p.op("scalar", lambda e, o=o, w2i=w2i, osl=n_ge % 2:
                         e.activation(out=gsb[osl][:, :], in_=yacc[:, o * TSW + w2i * 512: o * TSW + w2i * 512 + 512],
                                      func=AF.Silu, bias=sb2[:, o:o + 1]), inc=s_ge)
                    n_ge += 1
                    xsl = (n_xll[0] + n_xll[1]) % 2
                    if n_xll[xsl] >= 1:
                        p.wait("gpsimd", s_cmb, 2 * (n_se - 2) + 2)
                    p.dma("gpsimd", xst[xsl][:, :],
                          xT[o * 128:(o + 1) * 128, sw * TSW + w2i * 512: sw * TSW + w2i * 512 + 512], inc=s_xl[xsl])
                    n_xll[xsl] += 1
                    p.wait("vector", s_se, n_se)
                    p.wait("vector", s_ge, n_ge)
                    p.wait("vector", s_xl[xsl], 16 * n_xll[xsl])
                    if n_cmb >= 1:
                        p.wait("vector", s_cmb, n_cmb)
                    csl = (n_cmb // 2) % 2
                    if n_cmb >= 4:
                        p.wait("vector", s_out[csl], 16 * (n_cmb // 4))
                    p.op("vector", lambda e, osl=(n_se - 1) % 2: e.tensor_mul(out=tmul[:, :], in0=ssb[osl][:, :],
                                                                             in1=gsb[osl][:, :]), inc=s_cmb)
                    n_cmb += 1
                    p.wait("vector", s_cmb, n_cmb)
                    p.op("vector", lambda e, xsl=xsl, csl=csl: e.tensor_add(out=x1s[csl][:, :], in0=xst[xsl][:, :],
                                                                           in1=tmul[:, :]), inc=s_cmb)
                    n_cmb += 1
                    p.wait("gpsimd", s_cmb, n_cmb)
                    p.dma("gpsimd", x1[o * 128:(o + 1) * 128, sw * TSW + w2i * 512: sw * TSW + w2i * 512 + 512],
                          x1s[csl][:, :], inc=s_out[csl])

        p.wait("gpsimd", s_out[0], s_out[0].n)
        p.wait("gpsimd", s_out[1], s_out[1].n)

        block = en(nc.Block())
        p.replay(block)
    return nc




def build_attn1():
    nc = bass.Bass("TRN2", target_bir_lowering=False)
    xnin = nc.dram_tensor("xnin", [2, D, T_OWN], BF16, kind="ExternalInput").ap()
    wdq = nc.dram_tensor("wdq", [D, 256], F32, kind="ExternalInput").ap()
    wdk = nc.dram_tensor("wdk", [D, 256], F32, kind="ExternalInput").ap()
    wuq = nc.dram_tensor("wuq", [256, 512], F32, kind="ExternalInput").ap()
    wqr = nc.dram_tensor("wqr", [256, 256], F32, kind="ExternalInput").ap()
    wuk = nc.dram_tensor("wuk", [256, 512], F32, kind="ExternalInput").ap()
    wuv = nc.dram_tensor("wuv", [256, 512], F32, kind="ExternalInput").ap()
    wkr = nc.dram_tensor("wkr", [D, 32], F32, kind="ExternalInput").ap()
    ta1 = nc.dram_tensor("ta1", [64, T_FULL], BF16, kind="ExternalInput").ap()
    tb1 = nc.dram_tensor("tb1", [64, T_FULL], BF16, kind="ExternalInput").ap()
    prm16 = nc.dram_tensor("prm16", [64, 64], BF16, kind="ExternalInput").ap()
    attn = nc.dram_tensor("attn", [512, T_FULL], BF16, kind="ExternalOutput").ap()

    CLIP_HI = float(np.exp(80.0 / np.sqrt(96.0)))
    CLIP_LO = float(np.exp(-80.0 / np.sqrt(96.0)))

    with ExitStack() as stack:
        en = stack.enter_context
        p = _Prog(nc, stack)
        xw = [en(nc.sbuf_tensor(f"xw{j}", [128, NCH * 512], BF16)) for j in range(2)]
        wst = [en(nc.sbuf_tensor(f"wst{j}", [128, 512], F32)) for j in range(2)]
        wdqb = en(nc.sbuf_tensor("wdqb", [128, NCH * 256], BF16))
        wdkb = en(nc.sbuf_tensor("wdkb", [128, NCH * 256], BF16))
        wuqb = en(nc.sbuf_tensor("wuqb", [128, 2 * 512], BF16))
        wqrb = en(nc.sbuf_tensor("wqrb", [128, 2 * 256], BF16))
        wukb = en(nc.sbuf_tensor("wukb", [128, 2 * 512], BF16))
        wuvb = en(nc.sbuf_tensor("wuvb", [128, 2 * 512], BF16))
        wkrb = en(nc.sbuf_tensor("wkrb", [128, NCH * 32], BF16))
        tas = en(nc.sbuf_tensor("tas", [64, T_FULL], BF16))
        tbs = en(nc.sbuf_tensor("tbs", [64, T_FULL], BF16))
        prm = en(nc.sbuf_tensor("prm", [64, 64], BF16))
        onebf1 = en(nc.sbuf_tensor("onebf1", [1, 128], BF16))
        cqw = en(nc.sbuf_tensor("cqw", [128, 2 * 512], BF16))
        ckw = en(nc.sbuf_tensor("ckw", [128, 2 * 512], BF16))
        kbuf = en(nc.sbuf_tensor("kbuf", [96, 8 * T_FULL], BF16))
        vbuf = en(nc.sbuf_tensor("vbuf", [128, 32 * 520], BF16))
        qbuf = [en(nc.sbuf_tensor(f"qbuf{j}", [96, 8 * 512], BF16)) for j in range(2)]
        qrtmp = [en(nc.sbuf_tensor(f"qrtmp{j}", [64, 512], BF16)) for j in range(2)]
        rs1 = en(nc.sbuf_tensor("rs1", [64, 512], F32))
        rs2 = en(nc.sbuf_tensor("rs2", [64, 512], F32))
        krw = en(nc.sbuf_tensor("krw", [32, 512], BF16))
        pt = [en(nc.sbuf_tensor(f"pt{j}", [128, 512], BF16)) for j in range(3)]
        ubuf = en(nc.sbuf_tensor("ubuf", [64, 8 * 512], BF16))
        dbuf = en(nc.sbuf_tensor("dbuf", [1, 8 * 512], F32))
        onef1 = en(nc.sbuf_tensor("onef1", [1, 128], F32))
        anrm = [en(nc.sbuf_tensor(f"anrm{j}", [64, 512], BF16)) for j in range(2)]

        pj = [en(nc.psum_tensor(f"pj{j}", [128, 512], F32)) for j in range(2)]
        pp = en(nc.psum_tensor("pp", [128, 512], F32))
        ss = [en(nc.psum_tensor(f"ss{j}", [128, 512], F32)) for j in range(2)]
        pv = [en(nc.psum_tensor(f"pv{j}", [128, 512], F32)) for j in range(2)]
        pd = en(nc.psum_tensor("pd", [64, 512], F32))

        s_w0 = p.sem("s_w0")
        s_w1s = p.sem("s_w1s")
        s_wc = p.sem("s_wc")
        s_misc = p.sem("s_misc")
        s_ic = p.sem("s_ic")
        s_xw = [p.sem("s_xw0"), p.sem("s_xw1")]
        s_pj = p.sem("s_pj")
        s_qt = p.sem("s_qt")
        s_pp = p.sem("s_pp")
        s_rope = p.sem("s_rope")
        s_kcp = p.sem("s_kcp")
        s_smm = p.sem("s_smm")
        s_exp = p.sem("s_exp")
        s_clp = p.sem("s_clp")
        s_msk = p.sem("s_msk")
        s_pvm = p.sem("s_pvm")
        s_ue = p.sem("s_ue")
        s_di = p.sem("s_di")
        s_dr = p.sem("s_dr")
        s_an = p.sem("s_an")
        s_ao = [p.sem("s_ao0"), p.sem("s_ao1")]

        p.dma("scalar", tas[:, :], ta1[:, :], inc=s_misc)
        p.dma("scalar", tbs[:, :], tb1[:, :], inc=s_misc)
        p.dma("scalar", prm[:, :], prm16[:, :], inc=s_misc)
        nmisc = s_misc.n
        p.wait("vector", s_misc, nmisc)
        p.op("vector", lambda e: e.memset(onebf1[:, :], 1.0))
        p.op("vector", lambda e: e.memset(onef1[:, :], 1.0))
        p.op("vector", lambda e: e.memset(vbuf[:, :].rearrange("p (t h c) -> p t h c", t=32, h=8)[:, :, :, 64:65], 1.0),
             inc=s_ic)

        wjobs = []
        for c in range(NCH):
            wjobs.append((wdq[c * 128:(c + 1) * 128, :], wdqb, c * 256, 256))
        for c in range(NCH):
            wjobs.append((wdk[c * 128:(c + 1) * 128, :], wdkb, c * 256, 256))
        for c in range(2):
            wjobs.append((wuq[c * 128:(c + 1) * 128, :], wuqb, c * 512, 512))
            wjobs.append((wqr[c * 128:(c + 1) * 128, :], wqrb, c * 256, 256))
            wjobs.append((wuk[c * 128:(c + 1) * 128, :], wukb, c * 512, 512))
            wjobs.append((wuv[c * 128:(c + 1) * 128, :], wuvb, c * 512, 512))
        for c in range(NCH):
            wjobs.append((wkr[c * 128:(c + 1) * 128, :], wkrb, c * 32, 32))
        wl = 0
        for (src, dst, off, width) in wjobs:
            sl = wl % 2
            ssem = s_w0 if sl == 0 else s_w1s
            if wl >= 2:
                p.wait("sync", s_wc, wl - 1)
            p.dma("sync", wst[sl][:, 0:width], src, inc=ssem)
            p.wait("vector", ssem, 16 * (wl // 2 + 1))
            p.op("vector", lambda e, sl=sl, dst=dst, off=off, width=width:
                 e.tensor_copy(out=dst[:, off:off + width], in_=wst[sl][:, 0:width]), inc=s_wc)
            wl += 1
        nwc = wl

        n_pj = n_qt = n_pp = n_rope = n_smm = n_exp = n_clp = n_msk = n_pvm = n_ue = n_an = n_cq = n_kcp = 0
        smm_after_window = {}
        rope_after_group = []
        s_rope_cnt = [0]

        for w in range(NW_FULL):
            half, w2 = w // 4, w % 4
            for c in range(NCH):
                if w >= 2:
                    p.wait("sync", s_pj, (w - 1) * 11)
                p.dma("sync", xw[w % 2][:, c * 512:(c + 1) * 512],
                      xnin[half, c * 128:(c + 1) * 128, w2 * 512:(w2 + 1) * 512], inc=s_xw[w % 2])

            if w >= 2:
                p.wait("vector", s_smm, smm_after_window[w - 2])

            for (wsrc, dstb) in [(wdqb, cqw), (wdkb, ckw)]:
                for dl in range(2):
                    pjs = n_pj % 2
                    if n_pj >= 2:
                        p.wait("tensor", s_qt, n_pj - 1)
                    if n_pj == 0 and w == 0:
                        p.wait("tensor", s_wc, nwc)
                        p.wait("tensor", s_ic, 1)
                    if dl == 0 and wsrc is wdqb:
                        p.wait("tensor", s_xw[w % 2], 16 * NCH * (w // 2 + 1))
                        if w >= 1:
                            p.wait("tensor", s_pj, n_pj)
                    for k in range(NCH):
                        p.op("tensor", lambda e, pjs=pjs, wsrc=wsrc, dl=dl, k=k, w=w:
                             e.matmul(pj[pjs][:, :], wsrc[:, k * 256 + dl * 128: k * 256 + dl * 128 + 128],
                                      xw[w % 2][:, k * 512:(k + 1) * 512],
                                      start=(k == 0), stop=(k == NCH - 1)),
                             inc=(s_pj if k == NCH - 1 else None))
                    n_pj += 1
                    p.wait("scalar", s_pj, n_pj)
                    p.op("scalar", lambda e, pjs=pjs, dstb=dstb, dl=dl:
                         e.activation(out=dstb[:, dl * 512:(dl + 1) * 512], in_=pj[pjs][:, :], func=AF.Copy),
                         inc=s_qt)
                    n_qt += 1

            for (wub, qk) in [(wuqb, 0), (wukb, 1)]:
                for oc in range(4):
                    pjs = n_pj % 2
                    if n_pj >= 2:
                        p.wait("tensor", s_qt, n_pj - 1)
                    if oc == 0:
                        p.wait("tensor", s_qt, n_qt)
                    for dl in range(2):
                        p.op("tensor", lambda e, pjs=pjs, wub=wub, oc=oc, dl=dl, qk=qk:
                             e.matmul(pj[pjs][:, :], wub[:, dl * 512 + oc * 128: dl * 512 + oc * 128 + 128],
                                      (cqw if qk == 0 else ckw)[:, dl * 512:(dl + 1) * 512],
                                      start=(dl == 0), stop=(dl == 1)),
                             inc=(s_pj if dl == 1 else None))
                    n_pj += 1
                    p.wait("scalar", s_pj, n_pj)
                    for hh in range(2):
                        h = 2 * oc + hh
                        if qk == 0:
                            p.op("scalar", lambda e, pjs=pjs, hh=hh, h=h, w=w:
                                 e.activation(out=qbuf[w % 2][0:64, h * 512:(h + 1) * 512],
                                              in_=pj[pjs][hh * 64:(hh + 1) * 64, :], func=AF.Copy),
                                 inc=(s_qt if hh == 1 else None))
                        else:
                            p.op("scalar", lambda e, pjs=pjs, hh=hh, h=h, w=w:
                                 e.activation(out=kbuf[0:64, h * T_FULL + w * 512: h * T_FULL + w * 512 + 512],
                                              in_=pj[pjs][hh * 64:(hh + 1) * 64, :], func=AF.Copy),
                                 inc=(s_qt if hh == 1 else None))
                    n_qt += 1

            for hc in range(4):
                pjs = n_pj % 2
                if n_pj >= 2:
                    p.wait("tensor", s_qt, n_pj - 1)
                for dl in range(2):
                    p.op("tensor", lambda e, pjs=pjs, hc=hc, dl=dl:
                         e.matmul(pj[pjs][0:64, :], wqrb[:, dl * 256 + hc * 64: dl * 256 + hc * 64 + 64],
                                  cqw[:, dl * 512:(dl + 1) * 512],
                                  start=(dl == 0), stop=(dl == 1)),
                         inc=(s_pj if dl == 1 else None))
                n_pj += 1
                p.wait("scalar", s_pj, n_pj)
                gq = n_rope
                if gq >= 2:
                    p.wait("scalar", s_pp, gq - 1)
                    p.wait("scalar", s_rope, rope_after_group[gq - 2])
                p.op("scalar", lambda e, pjs=pjs, qs=gq % 2:
                     e.activation(out=qrtmp[qs][:, :], in_=pj[pjs][0:64, :], func=AF.Copy), inc=s_qt)
                n_qt += 1
                p.wait("tensor", s_qt, n_qt)
                if n_pp >= 1:
                    p.wait("tensor", s_rope, rope_after_group[n_pp - 1])
                p.op("tensor", lambda e, qs=gq % 2: e.matmul(pp[0:64, :], prm[:, :], qrtmp[qs][:, :],
                                                             start=True, stop=True), inc=s_pp)
                n_pp += 1
                p.wait("vector", s_qt, n_qt)
                p.wait("vector", s_pp, n_pp)
                p.opw("vector", lambda e, qs=gq % 2, w=w: e.tensor_mul(out=rs1[:, :], in0=qrtmp[qs][:, :],
                                                                      in1=tas[:, w * 512:(w + 1) * 512]))
                p.opw("vector", lambda e, w=w: e.tensor_mul(out=rs2[:, :], in0=pp[0:64, :],
                                                            in1=tbs[:, w * 512:(w + 1) * 512]))
                for hh in range(2):
                    h = 2 * hc + hh
                    p.op("vector", lambda e, hh=hh, h=h, w=w:
                         e.tensor_add(out=qbuf[w % 2][64:96, h * 512:(h + 1) * 512],
                                      in0=rs1[hh * 32:(hh + 1) * 32, :], in1=rs2[hh * 32:(hh + 1) * 32, :]),
                         inc=s_rope)
                n_rope += 1
                rope_after_group.append(s_rope.n)

            pjs = n_pj % 2
            if n_pj >= 2:
                p.wait("tensor", s_qt, n_pj - 1)
            for k in range(NCH):
                p.op("tensor", lambda e, pjs=pjs, k=k, w=w:
                     e.matmul(pj[pjs][0:32, :], wkrb[:, k * 32:(k + 1) * 32],
                              xw[w % 2][:, k * 512:(k + 1) * 512],
                              start=(k == 0), stop=(k == NCH - 1)),
                     inc=(s_pj if k == NCH - 1 else None))
            n_pj += 1
            p.wait("scalar", s_pj, n_pj)
            gq = n_rope
            if gq >= 2:
                p.wait("scalar", s_pp, gq - 1)
                p.wait("scalar", s_rope, rope_after_group[gq - 2])
            p.op("scalar", lambda e, pjs=pjs, qs=gq % 2:
                 e.activation(out=qrtmp[qs][0:32, :], in_=pj[pjs][0:32, :], func=AF.Copy), inc=s_qt)
            n_qt += 1
            p.wait("tensor", s_qt, n_qt)
            if n_pp >= 1:
                p.wait("tensor", s_rope, rope_after_group[n_pp - 1])
            p.op("tensor", lambda e, qs=gq % 2: e.matmul(pp[0:32, :], prm[0:32, 0:32], qrtmp[qs][0:32, :],
                                                         start=True, stop=True), inc=s_pp)
            n_pp += 1
            p.wait("vector", s_qt, n_qt)
            p.wait("vector", s_pp, n_pp)
            if w >= 1:
                p.wait("vector", s_smm, smm_after_window[w - 1])
            p.opw("vector", lambda e, qs=gq % 2, w=w: e.tensor_mul(out=rs1[0:32, :], in0=qrtmp[qs][0:32, :],
                                                                  in1=tas[0:32, w * 512:(w + 1) * 512]))
            p.opw("vector", lambda e, w=w: e.tensor_mul(out=rs2[0:32, :], in0=pp[0:32, :],
                                                        in1=tbs[0:32, w * 512:(w + 1) * 512]))
            p.opw("vector", lambda e: e.tensor_add(out=krw[:, :], in0=rs1[0:32, :], in1=rs2[0:32, :]))
            for h in range(8):
                p.op("vector", lambda e, h=h, w=w: e.tensor_copy(out=kbuf[64:96, h * T_FULL + w * 512: h * T_FULL + w * 512 + 512],
                                                                 in_=krw[:, :]),
                     inc=(s_rope if h == 7 else s_kcp))
            n_rope += 1
            n_kcp += 7
            rope_after_group.append(s_rope.n)

            for j in range(4):
                kt = 4 * w + j
                pjs = n_pj % 2
                if n_pj >= 2:
                    p.wait("tensor", s_qt, n_pj - 1)
                for dl in range(2):
                    p.op("tensor", lambda e, pjs=pjs, j=j, dl=dl:
                         e.matmul(pj[pjs][:, :], ckw[:, dl * 512 + j * 128: dl * 512 + j * 128 + 128],
                                  wuvb[:, dl * 512:(dl + 1) * 512],
                                  start=(dl == 0), stop=(dl == 1)),
                         inc=(s_pj if dl == 1 else None))
                n_pj += 1
                p.wait("scalar", s_pj, n_pj)
                p.op("scalar", lambda e, pjs=pjs, kt=kt:
                     e.activation(out=vbuf[:, kt * 520: kt * 520 + 520].rearrange("p (h c) -> p h c", h=8)[:, :, 0:64],
                                  in_=pj[pjs][:, :], func=AF.Copy),
                     inc=s_qt)
                n_qt += 1

            nkt = 4 * w + 4
            p.wait("tensor", s_rope, 2 * 4 * (w + 1) + (w + 1))
            p.wait("tensor", s_kcp, n_kcp)
            p.wait("tensor", s_qt, n_qt)
            for h in range(8):
                if h == 0:
                    p.wait("vector", s_an, 8 * w)
                    p.wait("vector", s_di, w)
                for kt in range(nkt):
                    sb = n_smm % 2
                    if n_smm >= 2:
                        p.wait("tensor", s_exp, n_smm - 1)
                    p.op("tensor", lambda e, sb=sb, h=h, kt=kt, w=w:
                         e.matmul(ss[sb][:, :], kbuf[0:96, h * T_FULL + kt * 128: h * T_FULL + kt * 128 + 128],
                                  qbuf[w % 2][0:96, h * 512:(h + 1) * 512],
                                  start=True, stop=True), inc=s_smm)
                    n_smm += 1
                    p.wait("scalar", s_smm, n_smm)
                    if n_exp >= 3:
                        p.wait("scalar", s_pvm, n_exp - 2)
                    p.op("scalar", lambda e, sb=sb, slot=n_exp % 3:
                         e.activation(out=pt[slot][:, :], in_=ss[sb][:, :], func=AF.Exp,
                                      scale=float(1.0 / np.sqrt(96.0))), inc=s_exp)
                    n_exp += 1
                    p.wait("vector", s_exp, n_exp)
                    p.op("vector", lambda e, slot=(n_exp - 1) % 3:
                         e.tensor_scalar(out=pt[slot][:, :], in0=pt[slot][:, :], scalar1=CLIP_HI,
                                         scalar2=CLIP_LO, op0=ALU.min, op1=ALU.max), inc=s_clp)
                    n_clp += 1
                    diag = kt >= 4 * w
                    if diag:
                        p.wait("gpsimd", s_clp, n_clp)
                        p.op("gpsimd", lambda e, slot=(n_exp - 1) % 3, base=w * 512 - kt * 128:
                             e.affine_select(out=pt[slot][:, :], in_=pt[slot][:, :], compare_op=ALU.is_ge,
                                             fill=0.0, base=base, pattern=[[1, 512]], channel_multiplier=-1),
                             inc=s_msk)
                        n_msk += 1
                    p.wait("tensor", s_clp, n_clp)
                    if diag:
                        p.wait("tensor", s_msk, n_msk)
                    if kt == 0:
                        gh = 8 * w + h
                        p.wait("tensor", s_ue, 2 * (gh - 2) + 2)
                    p.op("tensor", lambda e, pvs=h % 2, kt=kt, h=h, slot=(n_exp - 1) % 3, nkt=nkt:
                         e.matmul(pv[pvs][0:65, :], vbuf[:, kt * 520 + h * 65: kt * 520 + h * 65 + 65],
                                  pt[slot][:, :], start=(kt == 0), stop=(kt == nkt - 1)), inc=s_pvm)
                    n_pvm += 1
                p.wait("vector", s_pvm, n_pvm)
                p.op("vector", lambda e, pvs=h % 2, h=h: e.tensor_copy(out=ubuf[0:64, h * 512:(h + 1) * 512],
                                                                       in_=pv[pvs][0:64, :]), inc=s_ue)
                p.op("vector", lambda e, pvs=h % 2, h=h: e.tensor_copy(out=dbuf[0:1, h * 512:(h + 1) * 512],
                                                                      in_=pv[pvs][64:65, :]), inc=s_ue)
                n_ue += 2
            smm_after_window[w] = n_smm

            p.wait("scalar", s_ue, n_ue)
            p.wait("scalar", s_dr, 8 * w)
            p.opw("scalar", lambda e: e.activation(out=dbuf[:, :], in_=dbuf[:, :], func=AF.Ln))
            p.op("scalar", lambda e: e.activation(out=dbuf[:, :], in_=dbuf[:, :], func=AF.Exp, scale=-1.0), inc=s_di)
            for h in range(8):
                p.wait("tensor", s_di, w + 1)
                if n_an >= 1:
                    p.wait("tensor", s_an, n_an)
                p.op("tensor", lambda e, h=h: e.matmul(pd[:, :], onef1[0:1, 0:64], dbuf[0:1, h * 512:(h + 1) * 512],
                                                       start=True, stop=True), inc=s_dr)
                p.wait("vector", s_dr, s_dr.n)
                sl = n_an % 2
                if n_an >= 2:
                    p.wait("vector", s_ao[sl], 16 * (n_an // 2))
                p.op("vector", lambda e, h=h, sl=sl: e.tensor_mul(out=anrm[sl][:, :], in0=ubuf[0:64, h * 512:(h + 1) * 512],
                                                                  in1=pd[:, :]), inc=s_an)
                n_an += 1
                p.wait("sync", s_an, n_an)
                p.dma("sync", attn[h * 64:(h + 1) * 64, w * 512:(w + 1) * 512], anrm[sl][:, :], inc=s_ao[sl])

        p.wait("gpsimd", s_ao[0], s_ao[0].n)
        p.wait("gpsimd", s_ao[1], s_ao[1].n)

        block = en(nc.Block())
        p.replay(block)
    return nc




def build_tail1():
    nc = bass.Bass("TRN2", target_bir_lowering=False)
    xT = nc.dram_tensor("xT", [D, T_OWN], F32, kind="ExternalInput").ap()
    xnin = nc.dram_tensor("xnin", [D, T_OWN], BF16, kind="ExternalInput").ap()
    attn = nc.dram_tensor("attn", [D, T_OWN], BF16, kind="ExternalInput").ap()
    wrows = nc.dram_tensor("wrows", [NEXP, T_OWN], F32, kind="ExternalInput").ap()
    w0t = nc.dram_tensor("w0t", [NEXP, NDH, 128, D], F32, kind="ExternalInput").ap()
    w1t = nc.dram_tensor("w1t", [NEXP, NDH, 128, D], F32, kind="ExternalInput").ap()
    wo = nc.dram_tensor("wo", [NEXP, DH, D], F32, kind="ExternalInput").ap()
    wO = nc.dram_tensor("wO", [D, D], F32, kind="ExternalInput").ap()
    x2 = nc.dram_tensor("x2", [D, T_OWN], F32, kind="ExternalOutput").ap()

    NSW, TSW, NWS = 2, 1024, 2

    with ExitStack() as stack:
        en = stack.enter_context
        p = _Prog(nc, stack)
        xnb = en(nc.sbuf_tensor("xnb", [128, NCH * TSW], BF16))
        one1f = en(nc.sbuf_tensor("one1f", [1, 128], F32))
        mr = [en(nc.sbuf_tensor(f"mr{j}", [1, TSW], F32)) for j in range(2)]
        mrep = [en(nc.sbuf_tensor(f"mrep{j}", [128, 512], F32)) for j in range(2)]
        xwe = [en(nc.sbuf_tensor(f"xwe{j}", [128, NCH * 512], BF16)) for j in range(2)]
        w0st = [en(nc.sbuf_tensor(f"w0st{j}", [128, D], F32)) for j in range(2)]
        w0b = [en(nc.sbuf_tensor(f"w0b{j}", [128, D], BF16)) for j in range(2)]
        w1st = [en(nc.sbuf_tensor(f"w1st{j}", [128, D], F32)) for j in range(2)]
        w1b = [en(nc.sbuf_tensor(f"w1b{j}", [128, D], BF16)) for j in range(2)]
        w2st = [en(nc.sbuf_tensor(f"w2st{j}", [128, D], F32)) for j in range(2)]
        w2b = en(nc.sbuf_tensor("w2b", [128, GRP * D], BF16))
        sil = [en(nc.sbuf_tensor(f"sil{j}", [128, 512], BF16)) for j in range(2)]
        hbf = en(nc.sbuf_tensor("hbf", [128, GRP * NWS * 512], BF16))
        yacc = en(nc.sbuf_tensor("yacc", [128, NCH * TSW], F32))
        wos = [en(nc.sbuf_tensor(f"wos{j}", [128, 512], F32)) for j in range(2)]
        wob = en(nc.sbuf_tensor("wob", [128, NCH * D], BF16))
        atw = [en(nc.sbuf_tensor(f"atw{j}", [128, NCH * 512], BF16)) for j in range(2)]
        xst = [en(nc.sbuf_tensor(f"xst{j}", [128, 512], F32)) for j in range(2)]
        ssb = [en(nc.sbuf_tensor(f"ssb{j}", [128, 512], F32)) for j in range(2)]
        gsb = [en(nc.sbuf_tensor(f"gsb{j}", [128, 512], BF16)) for j in range(2)]
        tmul = en(nc.sbuf_tensor("tmul", [128, 512], F32))
        x1s = [en(nc.sbuf_tensor(f"x1s{j}", [128, 512], F32)) for j in range(2)]

        pu = [en(nc.psum_tensor(f"pu{j}", [128, 512], F32)) for j in range(2)]
        pvv = [en(nc.psum_tensor(f"pvv{j}", [128, 512], F32)) for j in range(2)]
        py = [en(nc.psum_tensor(f"py{j}", [128, 512], F32)) for j in range(2)]
        pm = en(nc.psum_tensor("pm", [128, 512], F32))

        s_fx = [p.sem("s_fx0"), p.sem("s_fx1")]
        s_fixc = p.sem("s_fixc")
        s_xnb = p.sem("s_xnb")
        s_mr = [p.sem("s_mr0"), p.sem("s_mr1")]
        s_mm = p.sem("s_mm")
        s_mrep = p.sem("s_mrep")
        s_xm = p.sem("s_xm")
        s_w0 = [p.sem("s_w00"), p.sem("s_w01")]
        s_w0c = p.sem("s_w0c")
        s_w1 = [p.sem("s_w10"), p.sem("s_w11")]
        s_w1c = p.sem("s_w1c")
        s_w2 = [p.sem("s_w20"), p.sem("s_w21")]
        s_w2c = p.sem("s_w2c")
        s_u = p.sem("s_u")
        s_sil = p.sem("s_sil")
        s_v = p.sem("s_v")
        s_hm = p.sem("s_hm")
        s_y = p.sem("s_y")
        s_ya = p.sem("s_ya")
        s_ini = p.sem("s_ini")
        s_at = [p.sem("s_at0"), p.sem("s_at1")]
        s_xl = [p.sem("s_xl0"), p.sem("s_xl1")]
        s_sp = p.sem("s_sp")
        s_se = p.sem("s_se")
        s_ge = p.sem("s_ge")
        s_cmb = p.sem("s_cmb")
        s_out = [p.sem("s_out0"), p.sem("s_out1")]

        p.op("vector", lambda e: e.memset(one1f[:, :], 1.0))
        wl = 0
        for c in range(NCH):
            for hf in range(2):
                sl = wl % 2
                if wl >= 2:
                    p.wait("scalar", s_fixc, wl - 1)
                p.dma("scalar", wos[sl][:, :], wO[c * 128:(c + 1) * 128, hf * 512:(hf + 1) * 512], inc=s_fx[sl])
                p.wait("vector", s_fx[sl], 16 * (wl // 2 + 1))
                p.op("vector", lambda e, sl=sl, c=c, hf=hf: e.tensor_copy(out=wob[:, c * D + hf * 512: c * D + (hf + 1) * 512],
                                                                          in_=wos[sl][:, :]), inc=s_fixc)
                wl += 1

        n_u = n_sil = n_v = n_hm = n_y = n_ya = n_xm = n_mm = n_mrep = 0
        w0l = w1l = w2l = 0
        n_sp = n_se = n_ge = n_cmb = 0
        n_mrl = [0, 0]
        n_atl = [0, 0]
        n_xll = [0, 0]

        for sw in range(NSW):
            if sw >= 1:
                p.wait("scalar", s_u, sw * 512)
                p.wait("scalar", s_xm, sw * 128)
            for c in range(NCH):
                p.dma("scalar", xnb[:, c * TSW:(c + 1) * TSW],
                      xnin[c * 128:(c + 1) * 128, sw * TSW:(sw + 1) * TSW], inc=s_xnb)
            if sw >= 1:
                p.wait("vector", s_ge, sw * 16)
            p.op("vector", lambda e: e.memset(yacc[:, :], 0.0), inc=s_ini)

            for i in range(NEXP):
                gi = sw * NEXP + i
                msl = gi % 2
                if n_mrl[msl] >= 1:
                    p.wait("scalar", s_mm, 2 * (gi - 2) + 2)
                p.dma("scalar", mr[msl][0:1, :], wrows[i:i + 1, sw * TSW:(sw + 1) * TSW], inc=s_mr[msl])
                n_mrl[msl] += 1

                for w2i in range(NWS):
                    p.wait("tensor", s_mr[msl], 16 * n_mrl[msl])
                    if n_mm >= 1:
                        p.wait("tensor", s_mrep, n_mm)
                    p.op("tensor", lambda e, msl=msl, w2i=w2i: e.matmul(pm[:, :], one1f[0:1, :],
                                                                       mr[msl][0:1, w2i * 512:(w2i + 1) * 512],
                                                                       start=True, stop=True), inc=s_mm)
                    n_mm += 1
                    p.wait("vector", s_mm, n_mm)
                    if gi >= 1:
                        p.wait("vector", s_xm, gi * 16)
                    p.op("vector", lambda e, w2i=w2i: e.tensor_copy(out=mrep[w2i][:, :], in_=pm[:, :]), inc=s_mrep)
                    n_mrep += 1
                    p.wait("vector", s_mrep, n_mrep)
                    p.wait("vector", s_xnb, 16 * NCH * (sw + 1))
                    if gi >= 1:
                        p.wait("vector", s_v, gi * 64)
                    for k in range(NCH):
                        p.op("vector", lambda e, w2i=w2i, k=k: e.tensor_mul(out=xwe[w2i][:, k * 512:(k + 1) * 512],
                                                                           in0=xnb[:, k * TSW + w2i * 512: k * TSW + w2i * 512 + 512],
                                                                           in1=mrep[w2i][:, :]), inc=s_xm)
                        n_xm += 1

                for g in range(NGRP):
                    gg = gi * NGRP + g
                    for d8 in range(GRP):
                        d = g * GRP + d8
                        sl0 = w0l % 2
                        if w0l >= 2:
                            p.wait("sync", s_w0c, w0l - 1)
                        p.dma("sync", w0st[sl0][:, :], w0t[i, d, :, :], inc=s_w0[sl0])
                        p.wait("vector", s_w0[sl0], 16 * (w0l // 2 + 1))
                        if w0l >= 2:
                            p.wait("vector", s_u, (w0l - 1) * 2)
                        p.op("vector", lambda e, sl0=sl0: e.tensor_copy(out=w0b[sl0][:, :], in_=w0st[sl0][:, :]),
                             inc=s_w0c)
                        w0l += 1
                        sl1 = w1l % 2
                        if w1l >= 2:
                            p.wait("sync", s_w1c, w1l - 1)
                        p.dma("sync", w1st[sl1][:, :], w1t[i, d, :, :], inc=s_w1[sl1])
                        p.wait("vector", s_w1[sl1], 16 * (w1l // 2 + 1))
                        if w1l >= 2:
                            p.wait("vector", s_v, (w1l - 1) * 2)
                        p.op("vector", lambda e, sl1=sl1: e.tensor_copy(out=w1b[sl1][:, :], in_=w1st[sl1][:, :]),
                             inc=s_w1c)
                        w1l += 1
                        sl2 = w2l % 2
                        if w2l >= 2:
                            p.wait("gpsimd", s_w2c, w2l - 1)
                        p.dma("gpsimd", w2st[sl2][:, :], wo[i, d * 128:(d + 1) * 128, :], inc=s_w2[sl2])
                        p.wait("vector", s_w2[sl2], 16 * (w2l // 2 + 1))
                        if gg >= 1:
                            p.wait("vector", s_y, gg * 16)
                        p.op("vector", lambda e, sl2=sl2, d8=d8: e.tensor_copy(out=w2b[:, d8 * D:(d8 + 1) * D],
                                                                              in_=w2st[sl2][:, :]), inc=s_w2c)
                        w2l += 1

                        for w2i in range(NWS):
                            if d8 == 0 and w2i == 0:
                                p.wait("tensor", s_xm, (gi + 1) * 16)
                                p.wait("tensor", s_xnb, 16 * NCH * (sw + 1))
                                if sw == 0 and gi == 0 and g == 0:
                                    p.wait("tensor", s_fixc, 16)
                            p.wait("tensor", s_w0c, w0l)
                            if n_u >= 2:
                                p.wait("tensor", s_sil, n_u - 1)
                            for k in range(NCH):
                                p.op("tensor", lambda e, us=n_u % 2, sl0=sl0, k=k, w2i=w2i, sw=sw:
                                     e.matmul(pu[us][:, :], w0b[sl0][:, k * 128:(k + 1) * 128],
                                              xnb[:, k * TSW + w2i * 512: k * TSW + w2i * 512 + 512],
                                              start=(k == 0), stop=(k == NCH - 1)),
                                     inc=(s_u if k == NCH - 1 else None))
                            n_u += 1
                            p.wait("scalar", s_u, n_u)
                            if n_sil >= 2:
                                p.wait("scalar", s_hm, n_sil - 1)
                            p.op("scalar", lambda e, us=(n_u - 1) % 2, ssl=n_sil % 2:
                                 e.activation(out=sil[ssl][:, :], in_=pu[us][:, :], func=AF.Silu), inc=s_sil)
                            n_sil += 1
                            p.wait("tensor", s_w1c, w1l)
                            if n_v >= 2:
                                p.wait("tensor", s_hm, n_v - 1)
                            for k in range(NCH):
                                p.op("tensor", lambda e, vs=n_v % 2, sl1=sl1, k=k, w2i=w2i:
                                     e.matmul(pvv[vs][:, :], w1b[sl1][:, k * 128:(k + 1) * 128],
                                              xwe[w2i][:, k * 512:(k + 1) * 512],
                                              start=(k == 0), stop=(k == NCH - 1)),
                                     inc=(s_v if k == NCH - 1 else None))
                            n_v += 1
                            p.wait("vector", s_sil, n_sil)
                            p.wait("vector", s_v, n_v)
                            if gg >= 1 and d8 == 0 and w2i == 0:
                                p.wait("vector", s_y, gg * 16)
                            p.op("vector", lambda e, ssl=(n_sil - 1) % 2, vs=(n_v - 1) % 2, d8=d8, w2i=w2i:
                                 e.tensor_mul(out=hbf[:, (d8 * NWS + w2i) * 512:(d8 * NWS + w2i) * 512 + 512],
                                              in0=sil[ssl][:, :], in1=pvv[vs][:, :]), inc=s_hm)
                            n_hm += 1

                    for w2i in range(NWS):
                        for o in range(NCH):
                            if w2i == 0 and o == 0:
                                p.wait("tensor", s_hm, n_hm)
                                p.wait("tensor", s_w2c, w2l)
                            if n_y >= 2:
                                p.wait("tensor", s_ya, n_y - 1)
                            for d8 in range(GRP):
                                p.op("tensor", lambda e, ys=n_y % 2, d8=d8, o=o, w2i=w2i:
                                     e.matmul(py[ys][:, :], w2b[:, d8 * D + o * 128: d8 * D + o * 128 + 128],
                                              hbf[:, (d8 * NWS + w2i) * 512:(d8 * NWS + w2i) * 512 + 512],
                                              start=(d8 == 0), stop=(d8 == GRP - 1)),
                                     inc=(s_y if d8 == GRP - 1 else None))
                            n_y += 1
                            p.wait("vector", s_y, n_y)
                            if n_ya == 0 or (sw == 1 and n_ya == 512):
                                p.wait("vector", s_ini, sw + 1)
                            p.op("vector", lambda e, ys=(n_y - 1) % 2, o=o, w2i=w2i:
                                 e.tensor_add(out=yacc[:, o * TSW + w2i * 512: o * TSW + w2i * 512 + 512],
                                              in0=yacc[:, o * TSW + w2i * 512: o * TSW + w2i * 512 + 512],
                                              in1=py[ys][:, :]), inc=s_ya)
                            n_ya += 1

            for w2i in range(NWS):
                gw = sw * NWS + w2i
                asl = gw % 2
                if n_atl[asl] >= 1:
                    p.wait("sync", s_sp, (gw - 1) * 8)
                for c in range(NCH):
                    p.dma("sync", atw[asl][:, c * 512:(c + 1) * 512],
                          attn[c * 128:(c + 1) * 128, sw * TSW + w2i * 512: sw * TSW + w2i * 512 + 512],
                          inc=s_at[asl])
                n_atl[asl] += 1
                for o in range(NCH):
                    if o == 0:
                        p.wait("tensor", s_at[asl], 16 * NCH * n_atl[asl])
                        p.wait("tensor", s_sil, n_sil)
                    if n_sp >= 2:
                        p.wait("tensor", s_se, n_sp - 1)
                    for c in range(NCH):
                        p.op("tensor", lambda e, sps=n_sp % 2, c=c, o=o, asl=asl:
                             e.matmul(pu[sps][:, :], wob[:, c * D + o * 128: c * D + o * 128 + 128],
                                      atw[asl][:, c * 512:(c + 1) * 512],
                                      start=(c == 0), stop=(c == NCH - 1)),
                             inc=(s_sp if c == NCH - 1 else None))
                    n_sp += 1
                    p.wait("scalar", s_sp, n_sp)
                    if n_se >= 2:
                        p.wait("scalar", s_cmb, 2 * (n_se - 2) + 1)
                    p.op("scalar", lambda e, sps=(n_sp - 1) % 2, osl=n_se % 2:
                         e.activation(out=ssb[osl][:, :], in_=pu[sps][:, :], func=AF.Copy), inc=s_se)
                    n_se += 1
                    if n_ge % 16 == 0:
                        p.wait("scalar", s_ya, 512 * (sw + 1))
                    p.op("scalar", lambda e, o=o, w2i=w2i, osl=n_ge % 2:
                         e.activation(out=gsb[osl][:, :], in_=yacc[:, o * TSW + w2i * 512: o * TSW + w2i * 512 + 512],
                                      func=AF.Silu), inc=s_ge)
                    n_ge += 1
                    xsl = (n_xll[0] + n_xll[1]) % 2
                    if n_xll[xsl] >= 1:
                        p.wait("gpsimd", s_cmb, 2 * (n_se - 2) + 2)
                    p.dma("gpsimd", xst[xsl][:, :],
                          xT[o * 128:(o + 1) * 128, sw * TSW + w2i * 512: sw * TSW + w2i * 512 + 512], inc=s_xl[xsl])
                    n_xll[xsl] += 1
                    p.wait("vector", s_se, n_se)
                    p.wait("vector", s_ge, n_ge)
                    p.wait("vector", s_xl[xsl], 16 * n_xll[xsl])
                    if n_cmb >= 1:
                        p.wait("vector", s_cmb, n_cmb)
                    csl = (n_cmb // 2) % 2
                    if n_cmb >= 4:
                        p.wait("vector", s_out[csl], 16 * (n_cmb // 4))
                    p.op("vector", lambda e, osl=(n_se - 1) % 2: e.tensor_mul(out=tmul[:, :], in0=ssb[osl][:, :],
                                                                             in1=gsb[osl][:, :]), inc=s_cmb)
                    n_cmb += 1
                    p.wait("vector", s_cmb, n_cmb)
                    p.op("vector", lambda e, xsl=xsl, csl=csl: e.tensor_add(out=x1s[csl][:, :], in0=xst[xsl][:, :],
                                                                           in1=tmul[:, :]), inc=s_cmb)
                    n_cmb += 1
                    p.wait("gpsimd", s_cmb, n_cmb)
                    p.dma("gpsimd", x2[o * 128:(o + 1) * 128, sw * TSW + w2i * 512: sw * TSW + w2i * 512 + 512],
                          x1s[csl][:, :], inc=s_out[csl])

        p.wait("gpsimd", s_out[0], s_out[0].n)
        p.wait("gpsimd", s_out[1], s_out[1].n)

        block = en(nc.Block())
        p.replay(block)
    return nc




def _emit_pro(nc, p, B, topk, debug=False, **dram):
    xT = dram['xT']
    wr = dram['wr']
    brt = dram['brt']
    xn = dram['xn']
    rows = dram['rows']
    dbg_lg = dram.get('dbg_lg')
    dbg_tm = dram.get('dbg_tm')
    dbg_ms = dram.get('dbg_ms')
    xf = B.sb("xf", [128, NCH * T_OWN], F32)
    xb = B.sb("xb", [128, NCH * T_OWN], BF16)
    xq = B.sb("xq", [128, NCH * T_OWN], BF16)
    wrsb = B.sb("wrsb", [128, NCH * NEXP], F32)
    brsb = B.sb("brsb", [128, NEXP], F32)
    onebf = B.sb("onebf", [128, 1], BF16)
    epsb = B.sb("epsb", [128, 1], F32)
    onef = B.sb("onef", [128, 1], F32)
    one1f = B.sb("one1f", [1, 128], F32)
    idf = B.sb("idf", [128, 128], F32)
    mrow = B.sb("mrow", [1, T_OWN], F32)
    rrow = B.sb("rrow", [1, T_OWN], F32)
    r1 = B.sb("r1", [1, 512], F32)
    r2 = B.sb("r2", [1, 512], F32)
    mtm = B.sb("mtm", [128, 16], F32)
    rtm = B.sb("rtm", [128, 16], F32)
    t8a = B.sb("t8a", [128, NEXP], F32)
    t8b = B.sb("t8b", [128, NEXP], F32)
    lg = B.sb("lg", [128, NEXP], F32)
    csrow = B.sb("csrow", [1, NEXP], F32)
    cssb = B.sb("cssb", [NEXP, 1], F32)
    csrep = B.sb("csrep", [128, NEXP], F32)
    m1 = B.sb("m1", [128, 1], F32)
    m2 = B.sb("m2", [128, 1], F32)
    eq1 = B.sb("eq1", [128, NEXP], F32)
    l2m = B.sb("l2m", [128, NEXP], F32)
    wv1 = B.sb("wv1", [128, 1], F32)
    wv2 = B.sb("wv2", [128, 1], F32)
    ev = B.sb("ev", [128, 1], F32)
    rows_tm = B.sb("rows_tm", [128, NEXP], F32)
    rows_sb = B.sb("rows_sb", [NEXP, T_OWN], F32)
    xns = B.sb("xns", [128, 2 * 512], BF16)
    dbg_lg_sb = B.sb("dbg_lg_sb", [128, 16 * NEXP], F32) if debug else None
    dbg_tm_sb = B.sb("dbg_tm_sb", [128, 16 * NEXP], F32) if debug else None
    tsc = B.sb("tsc", [128, 512], F32)

    ps_a = B.ps("ps_a", [1, 512], F32)
    ps_b = B.ps("ps_b", [1, 512], F32)
    ps_c = B.ps("ps_c", [128, 1], F32)
    ps_d = B.ps("ps_d", [128, 1], F32)
    ps_e = B.ps("ps_e", [128, NEXP], F32)
    ps_f = B.ps("ps_f", [128, 512], F32)
    ps_g = B.ps("ps_g", [128, 512], F32)
    ps_h = B.ps("ps_h", [128, 128], F32)

    s_ld = p.sem("s_ld")
    s_cast = p.sem("s_cast")
    s_init = p.sem("s_init")
    s_row = p.sem("s_row")
    s_tm = p.sem("s_tm")
    s_mm = p.sem("s_mm")
    s_ev = p.sem("s_ev")
    s_xn = p.sem("s_xn")
    s_xno = [p.sem("s_xno0"), p.sem("s_xno1")]
    s_cs = p.sem("s_cs")
    s_lg = p.sem("s_lg")
    s_rt = p.sem("s_rt")
    s_rc = p.sem("s_rc")
    s_st = p.sem("s_st")

    for c in range(NCH):
        p.dma("sync", xf[:, c * T_OWN:(c + 1) * T_OWN], xT[c * 128:(c + 1) * 128, :], inc=s_ld)
    for k in range(NCH):
        p.dma("sync", wrsb[:, k * NEXP:(k + 1) * NEXP], wr[k * 128:(k + 1) * 128, :], inc=s_ld)
    p.dma("sync", brsb[:, :], brt[:, :], inc=s_ld)
    nld = s_ld.n

    p.op("gpsimd", lambda e: e.memset(onebf[:, :], 1.0))
    p.op("gpsimd", lambda e: e.memset(epsb[:, :], 1e-5))
    p.op("gpsimd", lambda e: e.memset(onef[:, :], 1.0))
    p.op("gpsimd", lambda e: e.memset(one1f[:, :], 1.0))
    p.opw("gpsimd", lambda e: e.memset(idf[:, :], 0.0))
    p.op("gpsimd", lambda e: e.affine_select(out=idf[:, :], in_=idf[:, :], compare_op=ALU.not_equal,
                                             fill=1.0, base=0, pattern=[[-1, 128]], channel_multiplier=1),
         inc=s_init)

    p.wait("vector", s_ld, nld)
    p.wait("scalar", s_ld, nld)
    for c in range(NCH):
        p.op("vector", lambda e, c=c: e.tensor_copy(out=xb[:, c * T_OWN:(c + 1) * T_OWN],
                                                    in_=xf[:, c * T_OWN:(c + 1) * T_OWN]), inc=s_cast)
        p.op("scalar", lambda e, c=c: e.activation(out=xq[:, c * T_OWN:(c + 1) * T_OWN],
                                                   in_=xf[:, c * T_OWN:(c + 1) * T_OWN], func=AF.Square),
             inc=s_cast)

    p.wait("tensor", s_ld, nld)
    p.wait("tensor", s_init, 1)
    for k in range(NCH):
        p.op("tensor", lambda e, k=k: e.matmul(ps_c[0:NEXP, 0:1], wrsb[:, k * NEXP:(k + 1) * NEXP],
                                               onef[:, :], start=(k == 0), stop=(k == NCH - 1)),
             inc=(s_mm if k == NCH - 1 else None))
    p.wait("vector", s_mm, 1)
    p.op("vector", lambda e: e.tensor_copy(out=cssb[:, :], in_=ps_c[0:NEXP, 0:1]), inc=s_ev)
    p.wait("tensor", s_ev, 1)
    p.op("tensor", lambda e: e.transpose(ps_h[0:1, 0:NEXP], cssb[:, :], idf[0:NEXP, 0:NEXP]), inc=s_mm)
    p.wait("vector", s_mm, 2)
    p.op("vector", lambda e: e.tensor_copy(out=csrow[:, :], in_=ps_h[0:1, 0:NEXP]), inc=s_ev)
    p.wait("tensor", s_ev, 2)
    p.op("tensor", lambda e: e.matmul(ps_h[:, 0:NEXP], one1f[0:1, :], csrow[0:1, :], start=True, stop=True),
         inc=s_mm)
    p.wait("vector", s_mm, 3)
    p.op("vector", lambda e: e.tensor_copy(out=csrep[:, :], in_=ps_h[:, 0:NEXP]), inc=s_cs)

    p.wait("tensor", s_cast, 2 * NCH)
    row_ev = {}
    tm_after_tile = {}
    for w in range(NW_OWN):
        if w >= 1:
            p.wait("tensor", s_ev, row_ev[w - 1])
        for k in range(NCH):
            p.op("tensor", lambda e, k=k, w=w: e.matmul(ps_a[0:1, :], onebf[:, :],
                                                        xb[:, k * T_OWN + w * 512: k * T_OWN + w * 512 + 512],
                                                        start=(k == 0), stop=(k == NCH - 1)))
        for k in range(NCH):
            p.op("tensor", lambda e, k=k, w=w: e.matmul(ps_b[0:1, :], onebf[:, :],
                                                        xq[:, k * T_OWN + w * 512: k * T_OWN + w * 512 + 512],
                                                        start=(k == 0), stop=(k == NCH - 1)),
                 inc=(s_mm if k == NCH - 1 else None))
        mmv = s_mm.n
        p.wait("vector", s_mm, mmv)
        p.wait("vector", s_row, w)
        p.opw("vector", lambda e, w=w: e.tensor_scalar(out=mrow[0:1, w * 512:(w + 1) * 512], in0=ps_a[0:1, :],
                                                       scalar1=1.0 / D, scalar2=None, op0=ALU.mult))
        p.op("vector", lambda e, w=w: e.tensor_scalar(out=r1[0:1, :], in0=ps_b[0:1, :],
                                                      scalar1=1.0 / D, scalar2=None, op0=ALU.mult))
        p.opw("vector", lambda e, w=w: e.tensor_mul(out=r2[0:1, :], in0=mrow[0:1, w * 512:(w + 1) * 512],
                                                    in1=mrow[0:1, w * 512:(w + 1) * 512]))
        p.op("vector", lambda e: e.tensor_sub(out=r1[0:1, :], in0=r1[0:1, :], in1=r2[0:1, :]), inc=s_ev)
        evv = s_ev.n
        row_ev[w] = evv
        p.wait("scalar", s_ev, evv)
        p.opw("scalar", lambda e: e.activation(out=r2[0:1, :], in_=r1[0:1, :], func=AF.Ln, bias=epsb[0:1, 0:1]))
        p.op("scalar", lambda e, w=w: e.activation(out=rrow[0:1, w * 512:(w + 1) * 512], in_=r2[0:1, :],
                                                   func=AF.Exp, scale=-0.5), inc=s_row)

    for w in range(NW_OWN):
        p.wait("tensor", s_row, w + 1)
        if w >= 1:
            p.wait("tensor", s_xn, w * NCH)
        p.op("tensor", lambda e, w=w: e.matmul(ps_f[:, :], one1f[0:1, :], mrow[0:1, w * 512:(w + 1) * 512],
                                               start=True, stop=True))
        p.op("tensor", lambda e, w=w: e.matmul(ps_g[:, :], one1f[0:1, :], rrow[0:1, w * 512:(w + 1) * 512],
                                               start=True, stop=True), inc=s_mm)
        mmv = s_mm.n
        p.wait("vector", s_mm, mmv)
        for c in range(NCH):
            j = (w * NCH + c) % 2
            cw = (w * NCH + c)
            if cw >= 2:
                p.wait("vector", s_xno[j], 16 * (cw // 2))
            p.wait("vector", s_xn, cw)
            p.opw("vector", lambda e, c=c, w=w: e.tensor_sub(out=tsc[:, :],
                                                             in0=xf[:, c * T_OWN + w * 512: c * T_OWN + w * 512 + 512],
                                                             in1=ps_f[:, :]))
            p.op("vector", lambda e, j=j: e.tensor_mul(out=xns[:, j * 512:(j + 1) * 512], in0=tsc[:, :],
                                                       in1=ps_g[:, :]), inc=s_xn)
            xnv = s_xn.n
            p.wait("sync", s_xn, xnv)
            p.dma("sync", xn[c * 128:(c + 1) * 128, w * 512:(w + 1) * 512], xns[:, j * 512:(j + 1) * 512], inc=s_xno[j])

    for t in range(16):
        if t >= 1:
            p.wait("tensor", s_lg, t)
        for k in range(NCH):
            p.op("tensor", lambda e, k=k, t=t: e.matmul(ps_c[:, :], xb[:, k * T_OWN + t * 128: k * T_OWN + t * 128 + 128],
                                                        onebf[:, :], start=(k == 0), stop=(k == NCH - 1)))
        for k in range(NCH):
            p.op("tensor", lambda e, k=k, t=t: e.matmul(ps_d[:, :], xq[:, k * T_OWN + t * 128: k * T_OWN + t * 128 + 128],
                                                        onebf[:, :], start=(k == 0), stop=(k == NCH - 1)))
        for k in range(NCH):
            p.op("tensor", lambda e, k=k, t=t: e.matmul(ps_e[:, :], xf[:, k * T_OWN + t * 128: k * T_OWN + t * 128 + 128],
                                                        wrsb[:, k * NEXP:(k + 1) * NEXP],
                                                        start=(k == 0), stop=(k == NCH - 1)),
                 inc=(s_mm if k == NCH - 1 else None))
        mmv = s_mm.n
        p.wait("vector", s_mm, mmv)
        p.wait("vector", s_cs, 1)
        if t >= 1:
            p.wait("vector", s_tm, tm_after_tile[t - 1])
        p.opw("vector", lambda e, t=t: e.tensor_scalar(out=mtm[:, t:t + 1], in0=ps_c[:, :],
                                                       scalar1=1.0 / D, scalar2=None, op0=ALU.mult))
        p.op("vector", lambda e, t=t: e.tensor_scalar(out=wv1[:, :], in0=ps_d[:, :],
                                                      scalar1=1.0 / D, scalar2=None, op0=ALU.mult))
        p.opw("vector", lambda e, t=t: e.tensor_mul(out=wv2[:, :], in0=mtm[:, t:t + 1], in1=mtm[:, t:t + 1]))
        p.op("vector", lambda e: e.tensor_sub(out=wv1[:, :], in0=wv1[:, :], in1=wv2[:, :]), inc=s_ev)
        evv = s_ev.n
        p.wait("scalar", s_ev, evv)
        p.opw("scalar", lambda e: e.activation(out=wv2[:, :], in_=wv1[:, :], func=AF.Ln, bias=epsb[:, 0:1]))
        p.op("scalar", lambda e, t=t: e.activation(out=rtm[:, t:t + 1], in_=wv2[:, :], func=AF.Exp, scale=-0.5),
             inc=s_tm)
        tm_stats = s_tm.n
        p.wait("vector", s_tm, tm_stats)
        p.opw("vector", lambda e, t=t: e.tensor_scalar(out=t8a[:, :], in0=csrep[:, :],
                                                       scalar1=mtm[:, t:t + 1], scalar2=None, op0=ALU.mult))
        p.opw("vector", lambda e: e.tensor_sub(out=t8b[:, :], in0=ps_e[:, :], in1=t8a[:, :]))
        p.opw("vector", lambda e, t=t: e.tensor_scalar(out=t8a[:, :], in0=t8b[:, :],
                                                       scalar1=rtm[:, t:t + 1], scalar2=None, op0=ALU.mult))
        p.opw("vector", lambda e: e.tensor_add(out=lg[:, :], in0=t8a[:, :], in1=brsb[:, :]))
        p.opw("vector", lambda e: e.reduce_max(m1[:, :], lg[:, :], axis=mybir.AxisListType.X))
        p.opw("vector", lambda e: e.tensor_scalar(out=eq1[:, :], in0=lg[:, :], scalar1=m1[:, 0:1],
                                                  scalar2=None, op0=ALU.is_equal))
        if topk == 1:
            p.op("vector", lambda e: e.tensor_copy(out=rows_tm[:, :], in_=eq1[:, :]), inc=s_lg)
            tm_after_tile[t] = s_tm.n
        else:
            p.opw("vector", lambda e: e.scalar_tensor_tensor(out=l2m[:, :], in0=eq1[:, :], scalar=-1e30,
                                                             in1=lg[:, :], op0=ALU.mult, op1=ALU.add))
            p.opw("vector", lambda e: e.reduce_max(m2[:, :], l2m[:, :], axis=mybir.AxisListType.X))
            p.op("vector", lambda e: e.tensor_sub(out=wv1[:, :], in0=m2[:, :], in1=m1[:, :]), inc=s_ev)
            evv = s_ev.n
            p.wait("scalar", s_ev, evv)
            p.op("scalar", lambda e: e.activation(out=ev[:, :], in_=wv1[:, :], func=AF.Exp), inc=s_tm)
            p.wait("vector", s_tm, s_tm.n)
            p.opw("vector", lambda e: e.tensor_scalar(out=wv1[:, :], in0=ev[:, :], scalar1=1.0,
                                                      scalar2=None, op0=ALU.add))
            p.opw("vector", lambda e: e.reciprocal(wv2[:, :], wv1[:, :]))
            p.opw("vector", lambda e: e.tensor_mul(out=wv1[:, :], in0=ev[:, :], in1=wv2[:, :]))
            p.opw("vector", lambda e: e.tensor_scalar(out=eq1[:, :], in0=eq1[:, :], scalar1=wv2[:, 0:1],
                                                      scalar2=None, op0=ALU.mult))
            p.opw("vector", lambda e: e.tensor_scalar(out=l2m[:, :], in0=l2m[:, :], scalar1=m2[:, 0:1],
                                                      scalar2=None, op0=ALU.is_equal))
            p.op("vector", lambda e: e.scalar_tensor_tensor(out=rows_tm[:, :], in0=l2m[:, :],
                                                            scalar=wv1[:, 0:1], in1=eq1[:, :],
                                                            op0=ALU.mult, op1=ALU.add), inc=s_lg)
        tm_after_tile[t] = s_tm.n
        if debug:
            p.wait("vector", s_lg, s_lg.n)
            p.op("vector", lambda e, t=t: e.tensor_copy(out=dbg_lg_sb[:, t * NEXP:(t + 1) * NEXP], in_=lg[:, :]))
            p.op("vector", lambda e, t=t: e.tensor_copy(out=dbg_tm_sb[:, t * NEXP:(t + 1) * NEXP], in_=rows_tm[:, :]))
        p.wait("tensor", s_lg, s_lg.n)
        if t == 0:
            p.wait("tensor", s_cs, 1)
        if t >= 1:
            p.wait("tensor", s_rc, t)
        p.op("tensor", lambda e: e.transpose(ps_h[0:NEXP, 0:128], rows_tm[:, :], idf[:, :]), inc=s_rt)
        p.wait("vector", s_rt, s_rt.n)
        p.op("vector", lambda e, t=t: e.tensor_copy(out=rows_sb[:, t * 128:(t + 1) * 128],
                                                    in_=ps_h[0:NEXP, 0:128]), inc=s_rc)

    p.wait("gpsimd", s_rc, 16)
    p.dma("gpsimd", rows[:, :], rows_sb[:, :], inc=s_st)
    if debug:
        p.dma("gpsimd", dbg_lg[:, :], dbg_lg_sb[:, :], inc=s_st)
        p.dma("gpsimd", dbg_tm[:, :], dbg_tm_sb[:, :], inc=s_st)
        p.dma("gpsimd", dbg_ms[:, 0:16], mtm[:, :], inc=s_st)
        p.dma("gpsimd", dbg_ms[:, 16:32], rtm[:, :], inc=s_st)
    p.wait("gpsimd", s_xno[0], s_xno[0].n)
    p.wait("gpsimd", s_xno[1], s_xno[1].n)
    p.wait("gpsimd", s_st, s_st.n)



def build_pro(topk, debug=False):
    nc = bass.Bass("TRN2", target_bir_lowering=False)
    xT = nc.dram_tensor("xT", [D, T_OWN], F32, kind="ExternalInput").ap()
    wr = nc.dram_tensor("wr", [D, NEXP], F32, kind="ExternalInput").ap()
    brt = nc.dram_tensor("brt", [128, NEXP], F32, kind="ExternalInput").ap()
    xn = nc.dram_tensor("xn", [D, T_OWN], BF16, kind="ExternalOutput").ap()
    rows = nc.dram_tensor("rows", [NEXP, T_OWN], F32, kind="ExternalOutput").ap()
    if debug:
        dbg_lg = nc.dram_tensor("dbg_lg", [128, 16 * NEXP], F32, kind="ExternalOutput").ap()
        dbg_tm = nc.dram_tensor("dbg_tm", [128, 16 * NEXP], F32, kind="ExternalOutput").ap()
        dbg_ms = nc.dram_tensor("dbg_ms", [128, 32], F32, kind="ExternalOutput").ap()
    with ExitStack() as stack:
        B = _Bufs(nc, stack)
        p = _Prog(nc, stack)
        dbg = dict(dbg_lg=dbg_lg, dbg_tm=dbg_tm, dbg_ms=dbg_ms) if debug else {}
        _emit_pro(nc, p, B, topk, debug=debug, xT=xT, wr=wr, brt=brt, xn=xn, rows=rows, **dbg)
        block = stack.enter_context(nc.Block())
        p.replay(block)
    return nc


def _emit_attn0(nc, p, B, **dram):
    xnin = dram['xnin']
    wq = dram['wq']
    wk = dram['wk']
    wv = dram['wv']
    bq = dram['bq']
    bk = dram['bk']
    bv = dram['bv']
    ta = dram['ta']
    tb = dram['tb']
    perm = dram['perm']
    attn = dram.get('attn')
    wOo = dram.get('wOo')
    sp = dram.get('sp')
    fused = sp is not None
    xw = [B.sb(f"xw{j}", [128, NCH * 512], BF16) for j in range(2)]
    wst = [B.sb(f"wst{j}", [128, 512], F32) for j in range(2)]
    wqb = B.sb("wqb", [128, NCH * 512], BF16)
    wkb = B.sb("wkb", [128, NCH * 512], BF16)
    wvb = B.sb("wvb", [128, NCH * 512], BF16)
    bqs = B.sb("bqs", [128, 4], F32)
    bks = B.sb("bks", [128, 4], F32)
    bvs = B.sb("bvs", [1, 512], F32)
    bvb = B.sb("bvb", [1, 512], BF16)
    tas = B.sb("tas", [128, T_FULL], BF16)
    tbs = B.sb("tbs", [128, T_FULL], BF16)
    prm = B.sb("prm", [128, 128], BF16)
    onebf1 = B.sb("onebf1", [1, 128], BF16)
    kbuf = B.sb("kbuf", [128, 4 * T_FULL], BF16)
    vbuf = B.sb("vbuf", [128, 32 * 520], BF16)
    qbuf = [B.sb(f"qbuf{j}", [128, 4 * 512], BF16) for j in range(2)]
    qtmp = [B.sb(f"qtmp{j}", [128, 512], BF16) for j in range(2)]
    rs1 = B.sb("rs1", [128, 512], F32)
    rs2 = B.sb("rs2", [128, 512], F32)
    pt = [B.sb(f"pt{j}", [128, 512], BF16) for j in range(3)]
    ubuf = B.sb("ubuf", [64, 8 * 512], BF16)
    dbuf = B.sb("dbuf", [1, 8 * 512], F32)
    onef1 = B.sb("onef1", [1, 128], F32)
    if fused:
        awin = B.sb("awin", [128, 2 * 4 * 512], BF16)
        wOob = B.sb("wOob", [128, 4 * 1024], BF16)
        spst = [B.sb(f"spst{j}", [128, 512], F32) for j in range(2)]
    else:
        anrm = [B.sb(f"anrm{j}", [64, 512], BF16) for j in range(2)]

    pj = [B.ps(f"pj{j}", [128, 512], F32) for j in range(2)]
    pp = B.ps("pp", [128, 512], F32)
    ss = [B.ps(f"ss{j}", [128, 512], F32) for j in range(2)]
    pv = [B.ps(f"pv{j}", [128, 512], F32) for j in range(2)]
    pd = B.ps("pd", [128, 512], F32)

    s_w0 = p.sem("s_w0")
    s_w1 = p.sem("s_w1")
    s_wc = p.sem("s_wc")
    s_misc = p.sem("s_misc")
    s_ic = p.sem("s_ic")
    s_xw = [p.sem("s_xw0"), p.sem("s_xw1")]
    s_pj = p.sem("s_pj")
    s_qt = p.sem("s_qt")
    s_pp = p.sem("s_pp")
    s_rope = p.sem("s_rope")
    s_v = p.sem("s_v")
    s_smm = p.sem("s_smm")
    s_exp = p.sem("s_exp")
    s_msk = p.sem("s_msk")
    s_pvm = p.sem("s_pvm")
    s_ue = p.sem("s_ue")
    s_di = p.sem("s_di")
    s_dr = p.sem("s_dr")
    s_an = p.sem("s_an")
    s_ao = [p.sem("s_ao0"), p.sem("s_ao1")]
    s_spe = p.sem("s_spe")
    s_spmm = p.sem("s_spmm")
    s_spo = [p.sem("s_spo0"), p.sem("s_spo1")]

    p.dma("scalar", bqs[:, :], bq[:, :], inc=s_misc)
    p.dma("scalar", bks[:, :], bk[:, :], inc=s_misc)
    p.dma("scalar", bvs[:, :], bv[:, :], inc=s_misc)
    p.dma("scalar", tas[:, :], ta[:, :], inc=s_misc)
    p.dma("scalar", tbs[:, :], tb[:, :], inc=s_misc)
    p.dma("scalar", prm[:, :], perm[:, :], inc=s_misc)
    nmisc = s_misc.n
    p.wait("vector", s_misc, nmisc)
    p.op("vector", lambda e: e.memset(onebf1[:, :], 1.0))
    p.op("vector", lambda e: e.memset(onef1[:, :], 1.0))
    p.op("vector", lambda e: e.tensor_copy(out=bvb[:, :], in_=bvs[:, :]))
    p.op("vector", lambda e: e.memset(vbuf[:, :].rearrange("p (t h c) -> p t h c", t=32, h=8)[:, :, :, 64:65], 1.0),
         inc=s_ic)

    wjobs = []
    for (src, dst) in [(wq, wqb), (wk, wkb), (wv, wvb)]:
        for c in range(NCH):
            wjobs.append((src[c * 128:(c + 1) * 128, :], dst, c * 512))
    if fused:
        for c in range(4):
            for hf in range(2):
                wjobs.append((wOo[c * 128:(c + 1) * 128, hf * 512:(hf + 1) * 512], wOob, c * 1024 + hf * 512))
    wl = 0
    nwl = len(wjobs)
    for (src, dst, off) in wjobs:
        sl = wl % 2
        ssem = s_w0 if sl == 0 else s_w1
        if wl >= 2:
            p.wait("sync", s_wc, wl - 1)
        p.dma("sync", wst[sl][:, :], src, inc=ssem)
        p.wait("vector", ssem, 16 * (wl // 2 + 1))
        p.op("vector", lambda e, sl=sl, dst=dst, off=off: e.tensor_copy(out=dst[:, off:off + 512],
                                                                        in_=wst[sl][:, :]), inc=s_wc)
        wl += 1

    n_rope = 0
    n_pj = 0
    n_qt = 0
    n_pp = 0
    n_v = 0
    n_smm = 0
    n_exp = 0
    n_msk = 0
    n_pvm = 0
    n_ue = 0
    n_an = 0

    smm_after_window = {}
    for w in range(NW_FULL):
        half, w2 = w // 4, w % 4
        for c in range(NCH):
            if w >= 2:
                p.wait("sync", s_pj, (w - 1) * 12)
            p.dma("sync", xw[w % 2][:, c * 512:(c + 1) * 512],
                  xnin[half, c * 128:(c + 1) * 128, w2 * 512:(w2 + 1) * 512], inc=s_xw[w % 2])

        if w >= 2:
            p.wait("vector", s_smm, smm_after_window[w - 2])
        for (qi, (wb, bias_s, dstbuf)) in enumerate([(wqb, bqs, None), (wkb, bks, None)]):
            for c in range(4):
                pjs = n_pj % 2
                if n_pj >= 2:
                    p.wait("tensor", s_qt, n_pj - 1)
                if c == 0 and qi == 0:
                    p.wait("tensor", s_xw[w % 2], 16 * NCH * (w // 2 + 1))
                    if w == 0:
                        p.wait("tensor", s_wc, nwl)
                        p.wait("tensor", s_ic, 1)
                for k in range(NCH):
                    p.op("tensor", lambda e, pjs=pjs, wb=wb, c=c, k=k, w=w:
                         e.matmul(pj[pjs][:, :], wb[:, k * 512 + c * 128: k * 512 + c * 128 + 128],
                                  xw[w % 2][:, k * 512:(k + 1) * 512],
                                  start=(k == 0), stop=(k == NCH - 1)),
                         inc=(s_pj if k == NCH - 1 else None))
                n_pj += 1
                p.wait("scalar", s_pj, n_pj)
                gqk = n_rope
                if gqk >= 2:
                    p.wait("scalar", s_pp, gqk - 1)
                    p.wait("scalar", s_rope, gqk - 1)
                p.op("scalar", lambda e, pjs=pjs, bias_s=bias_s, c=c:
                     e.activation(out=qtmp[pjs][:, :], in_=pj[pjs][:, :], func=AF.Identity, bias=bias_s[:, c:c + 1]),
                     inc=s_qt)
                n_qt += 1
                p.wait("tensor", s_qt, n_qt)
                if n_pp >= 1:
                    p.wait("tensor", s_rope, n_pp)
                if fused and c == 0 and qi == 0 and w >= 1:
                    p.wait("tensor", s_spe, 8 * w)
                p.op("tensor", lambda e, pjs=pjs: e.matmul(pp[:, :], prm[:, :], qtmp[pjs][:, :],
                                                           start=True, stop=True), inc=s_pp)
                n_pp += 1
                p.wait("vector", s_qt, n_qt)
                p.wait("vector", s_pp, n_pp)
                p.opw("vector", lambda e, pjs=pjs, w=w: e.tensor_mul(out=rs1[:, :], in0=qtmp[pjs][:, :],
                                                                    in1=tas[:, w * 512:(w + 1) * 512]))
                p.opw("vector", lambda e, w=w: e.tensor_mul(out=rs2[:, :], in0=pp[:, :],
                                                            in1=tbs[:, w * 512:(w + 1) * 512]))
                if qi == 0:
                    p.op("vector", lambda e, c=c, w=w: e.tensor_add(out=qbuf[w % 2][:, c * 512:(c + 1) * 512],
                                                                    in0=rs1[:, :], in1=rs2[:, :]), inc=s_rope)
                else:
                    p.op("vector", lambda e, c=c, w=w: e.tensor_add(out=kbuf[:, c * T_FULL + w * 512: c * T_FULL + w * 512 + 512],
                                                                    in0=rs1[:, :], in1=rs2[:, :]), inc=s_rope)
                n_rope += 1

        for j in range(4):
            kt = 4 * w + j
            pjs = n_pj % 2
            if n_pj >= 2:
                p.wait("tensor", s_qt, n_pj - 1)
            for k in range(NCH):
                p.op("tensor", lambda e, pjs=pjs, j=j, k=k, w=w:
                     e.matmul(pj[pjs][:, :], xw[w % 2][:, k * 512 + j * 128: k * 512 + j * 128 + 128],
                              wvb[:, k * 512:(k + 1) * 512],
                              start=(k == 0), stop=False))
            p.op("tensor", lambda e, pjs=pjs: e.matmul(pj[pjs][:, :], onebf1[0:1, :], bvb[0:1, :],
                                                       start=False, stop=True), inc=s_pj)
            n_pj += 1
            p.wait("scalar", s_pj, n_pj)
            p.op("scalar", lambda e, pjs=pjs, kt=kt:
                 e.activation(out=vbuf[:, kt * 520: kt * 520 + 520].rearrange("p (h c) -> p h c", h=8)[:, :, 0:64],
                              in_=pj[pjs][:, :], func=AF.Copy),
                 inc=s_qt)
            n_qt += 1
            n_v += 1

        nkt = 4 * w + 4
        p.wait("tensor", s_rope, (w + 1) * 8)
        p.wait("tensor", s_qt, n_qt)
        for h in range(8):
            hb = (h % 2) * 64
            g = h // 2
            if h == 0:
                p.wait("vector", s_an, 8 * w)
                p.wait("vector", s_di, w)
            for kt in range(nkt):
                sb = n_smm % 2
                if n_smm >= 2:
                    p.wait("tensor", s_exp, n_smm - 1)
                p.op("tensor", lambda e, sb=sb, hb=hb, g=g, kt=kt, w=w:
                     e.matmul(ss[sb][:, :], kbuf[hb:hb + 64, g * T_FULL + kt * 128: g * T_FULL + kt * 128 + 128],
                              qbuf[w % 2][hb:hb + 64, g * 512:(g + 1) * 512],
                              start=True, stop=True), inc=s_smm)
                n_smm += 1
                p.wait("scalar", s_smm, n_smm)
                if n_exp >= 3:
                    p.wait("scalar", s_pvm, n_exp - 2)
                p.op("scalar", lambda e, sb=sb, slot=n_exp % 3:
                     e.activation(out=pt[slot][:, :], in_=ss[sb][:, :], func=AF.Exp, scale=0.125), inc=s_exp)
                n_exp += 1
                diag = kt >= 4 * w
                if diag:
                    p.wait("gpsimd", s_exp, n_exp)
                    p.op("gpsimd", lambda e, slot=(n_exp - 1) % 3, base=w * 512 - kt * 128:
                         e.affine_select(out=pt[slot][:, :], in_=pt[slot][:, :], compare_op=ALU.is_ge,
                                         fill=0.0, base=base, pattern=[[1, 512]], channel_multiplier=-1),
                         inc=s_msk)
                    n_msk += 1
                p.wait("tensor", s_exp, n_exp)
                if diag:
                    p.wait("tensor", s_msk, n_msk)
                if kt == 0:
                    gh = 8 * w + h
                    p.wait("tensor", s_ue, 2 * (gh - 2) + 2)
                p.op("tensor", lambda e, pvs=h % 2, kt=kt, h=h, slot=(n_exp - 1) % 3, nkt=nkt:
                     e.matmul(pv[pvs][0:65, :], vbuf[:, kt * 520 + h * 65: kt * 520 + h * 65 + 65],
                              pt[slot][:, :], start=(kt == 0), stop=(kt == nkt - 1)), inc=s_pvm)
                n_pvm += 1
            p.wait("vector", s_pvm, n_pvm)
            p.op("vector", lambda e, pvs=h % 2, h=h: e.tensor_copy(out=ubuf[0:64, h * 512:(h + 1) * 512],
                                                                   in_=pv[pvs][0:64, :]), inc=s_ue)
            p.op("vector", lambda e, pvs=h % 2, h=h: e.tensor_copy(out=dbuf[0:1, h * 512:(h + 1) * 512],
                                                                  in_=pv[pvs][64:65, :]), inc=s_ue)
            n_ue += 2

        smm_after_window[w] = n_smm
        p.wait("scalar", s_ue, n_ue)
        p.wait("scalar", s_dr, 8 * w)
        p.opw("scalar", lambda e: e.activation(out=dbuf[:, :], in_=dbuf[:, :], func=AF.Ln))
        p.op("scalar", lambda e: e.activation(out=dbuf[:, :], in_=dbuf[:, :], func=AF.Exp, scale=-1.0), inc=s_di)
        for h in range(8):
            p.wait("tensor", s_di, w + 1)
            if n_an >= 1:
                p.wait("tensor", s_an, n_an)
            if fused and h == 0 and w >= 1:
                p.wait("tensor", s_spe, 8 * w)
            p.op("tensor", lambda e, h=h: e.matmul(pd[0:64, :], onef1[0:1, 0:64], dbuf[0:1, h * 512:(h + 1) * 512],
                                                   start=True, stop=True), inc=s_dr)
            n_dr = s_dr.n
            p.wait("vector", s_dr, n_dr)
            if fused:
                if h == 0 and w >= 2:
                    p.wait("vector", s_spe, 8 * (w - 1))
                p.op("vector", lambda e, h=h, w=w:
                     e.tensor_mul(out=awin[(h % 2) * 64:(h % 2) * 64 + 64,
                                           (w % 2) * 2048 + (h // 2) * 512:(w % 2) * 2048 + (h // 2) * 512 + 512],
                                  in0=ubuf[0:64, h * 512:(h + 1) * 512], in1=pd[0:64, :]), inc=s_an)
                n_an += 1
            else:
                sl = n_an % 2
                if n_an >= 2:
                    p.wait("vector", s_ao[sl], 16 * (n_an // 2))
                p.op("vector", lambda e, h=h, sl=sl: e.tensor_mul(out=anrm[sl][:, :], in0=ubuf[0:64, h * 512:(h + 1) * 512],
                                                                  in1=pd[0:64, :]), inc=s_an)
                n_an += 1
                p.wait("sync", s_an, n_an)
                p.dma("sync", attn[h * 64:(h + 1) * 64, w * 512:(w + 1) * 512], anrm[sl][:, :], inc=s_ao[sl])

        if fused:
            for o in range(8):
                if o == 0:
                    p.wait("tensor", s_an, 8 * (w + 1))
                nsp = 8 * w + o
                if nsp >= 2:
                    p.wait("tensor", s_spe, nsp - 1)
                bank = pd if o % 2 == 0 else pp
                for k in range(4):
                    p.op("tensor", lambda e, bank=bank, k=k, o=o, w=w:
                         e.matmul(bank[:, :],
                                  wOob[:, k * 1024 + o * 128: k * 1024 + o * 128 + 128],
                                  awin[:, (w % 2) * 2048 + k * 512:(w % 2) * 2048 + k * 512 + 512],
                                  start=(k == 0), stop=(k == 3)),
                         inc=(s_spmm if k == 3 else None))
                p.wait("scalar", s_spmm, s_spmm.n)
                ssl = nsp % 2
                if nsp >= 2:
                    p.wait("scalar", s_spo[ssl], 16 * (nsp // 2))
                p.op("scalar", lambda e, bank=bank, ssl=ssl:
                     e.activation(out=spst[ssl][:, :], in_=bank[:, :], func=AF.Copy),
                     inc=s_spe)
                p.wait("sync", s_spe, s_spe.n)
                p.dma("sync", sp[(w // 4) * 1024 + o * 128:(w // 4) * 1024 + o * 128 + 128,
                                 (w % 4) * 512:(w % 4) * 512 + 512], spst[ssl][:, :], inc=s_spo[ssl])

    if fused:
        p.wait("gpsimd", s_spo[0], s_spo[0].n)
        p.wait("gpsimd", s_spo[1], s_spo[1].n)
    else:
        p.wait("gpsimd", s_ao[0], s_ao[0].n)
        p.wait("gpsimd", s_ao[1], s_ao[1].n)



def build_attn0():
    nc = bass.Bass("TRN2", target_bir_lowering=False)
    xnin = nc.dram_tensor("xnin", [2, D, T_OWN], BF16, kind="ExternalInput").ap()
    wq = nc.dram_tensor("wq", [D, 512], F32, kind="ExternalInput").ap()
    wk = nc.dram_tensor("wk", [D, 512], F32, kind="ExternalInput").ap()
    wv = nc.dram_tensor("wv", [D, 512], F32, kind="ExternalInput").ap()
    bq = nc.dram_tensor("bq", [128, 4], F32, kind="ExternalInput").ap()
    bk = nc.dram_tensor("bk", [128, 4], F32, kind="ExternalInput").ap()
    bv = nc.dram_tensor("bv", [1, 512], F32, kind="ExternalInput").ap()
    ta = nc.dram_tensor("ta", [128, T_FULL], BF16, kind="ExternalInput").ap()
    tb = nc.dram_tensor("tb", [128, T_FULL], BF16, kind="ExternalInput").ap()
    perm = nc.dram_tensor("perm", [128, 128], BF16, kind="ExternalInput").ap()
    attn = nc.dram_tensor("attn", [512, T_FULL], BF16, kind="ExternalOutput").ap()
    with ExitStack() as stack:
        B = _Bufs(nc, stack)
        p = _Prog(nc, stack)
        _emit_attn0(nc, p, B, xnin=xnin, wq=wq, wk=wk, wv=wv, bq=bq, bk=bk, bv=bv, ta=ta, tb=tb, perm=perm, attn=attn)
        block = stack.enter_context(nc.Block())
        p.replay(block)
    return nc


def _emit_tail0(nc, p, B, **dram):
    xT = dram['xT']
    xnin = dram['xnin']
    attn = dram.get('attn')
    sred = dram.get('sred')
    fused = sred is not None
    mrows = dram['mrows']
    w1t = dram['w1t']
    b1c = dram['b1c']
    w2 = dram['w2']
    b2 = dram['b2']
    woT = dram['woT']
    boc = dram['boc']
    x1 = dram['x1']
    NSW, TSW, NWS = 2, 1024, 2
    xnb = B.sb("xnb", [128, NCH * TSW], BF16)
    one1f = B.sb("one1f", [1, 128], F32)
    onec = B.sb("onec", [128, 1], F32)
    mr = [B.sb(f"mr{j}", [1, TSW], F32) for j in range(2)]
    mrep = [B.sb(f"mrep{j}", [128, 512], F32) for j in range(2)]
    xm = [B.sb(f"xm{j}", [128, NCH * 512], BF16) for j in range(2)]
    w1st = [B.sb(f"w1st{j}", [128, D], F32) for j in range(2)]
    w1b = [B.sb(f"w1b{j}", [128, D], BF16) for j in range(2)]
    w2st = [B.sb(f"w2st{j}", [128, D], F32) for j in range(2)]
    w2b = B.sb("w2b", [128, GRP * D], BF16)
    hbf = B.sb("hbf", [128, GRP * NWS * 512], BF16)
    yacc = B.sb("yacc", [128, NCH * TSW], F32)
    b1s = [B.sb(f"b1s{j}", [128, NDH], F32) for j in range(2)]
    b2s = B.sb("b2s", [NEXP, D], F32)
    sb2 = B.sb("sb2", [128, NCH], F32)
    if not fused:
        wos = [B.sb(f"wos{j}", [128, 512], F32) for j in range(2)]
        wob = B.sb("wob", [128, NCH * D], BF16)
        atw = [B.sb(f"atw{j}", [128, NCH * 512], BF16) for j in range(2)]
    else:
        sst = [B.sb(f"sst{j}", [128, 512], F32) for j in range(2)]
    bos = B.sb("bos", [128, NCH], F32)
    xst = [B.sb(f"xst{j}", [128, 512], F32) for j in range(2)]
    ssb = [B.sb(f"ssb{j}", [128, 512], F32) for j in range(2)]
    gsb = [B.sb(f"gsb{j}", [128, 512], BF16) for j in range(2)]
    tmul = B.sb("tmul", [128, 512], F32)
    x1s = [B.sb(f"x1s{j}", [128, 512], F32) for j in range(2)]

    pu = [B.ps(f"pu{j}", [128, 512], F32) for j in range(2)]
    py = [B.ps(f"py{j}", [128, 512], F32) for j in range(2)]
    pm = B.ps("pm", [128, 512], F32)
    pss = [B.ps(f"pss{j}", [128, 512], F32) for j in range(2)]
    pb = B.ps("pb", [128, 1], F32)

    s_fix = p.sem("s_fix")
    s_fx = [p.sem("s_fx0"), p.sem("s_fx1")]
    s_fixc = p.sem("s_fixc")
    s_sb2 = p.sem("s_sb2")
    s_sb2e = p.sem("s_sb2e")
    s_xnb = p.sem("s_xnb")
    s_mr = [p.sem("s_mr0"), p.sem("s_mr1")]
    s_b1 = [p.sem("s_b10"), p.sem("s_b11")]
    s_mm = p.sem("s_mm")
    s_mrep = p.sem("s_mrep")
    s_xm = p.sem("s_xm")
    s_w1 = [p.sem("s_w10"), p.sem("s_w11")]
    s_w1c = p.sem("s_w1c")
    s_w2 = [p.sem("s_w20"), p.sem("s_w21")]
    s_w2c = p.sem("s_w2c")
    s_u = p.sem("s_u")
    s_h = p.sem("s_h")
    s_y = p.sem("s_y")
    s_ya = p.sem("s_ya")
    s_ini = p.sem("s_ini")
    s_at = [p.sem("s_at0"), p.sem("s_at1")]
    s_xl = [p.sem("s_xl0"), p.sem("s_xl1")]
    s_sp = p.sem("s_sp")
    s_se = p.sem("s_se")
    s_ge = p.sem("s_ge")
    s_cmb = p.sem("s_cmb")
    s_out = [p.sem("s_out0"), p.sem("s_out1")]

    s_one = p.sem("s_one")
    p.op("vector", lambda e: e.memset(one1f[:, :], 1.0))
    p.op("vector", lambda e: e.memset(onec[:, :], 1.0), inc=s_one)
    wl = 0
    if not fused:
        for c in range(NCH):
            for hf in range(2):
                sl = wl % 2
                if wl >= 2:
                    p.wait("scalar", s_fixc, wl - 1)
                p.dma("scalar", wos[sl][:, :], woT[c * 128:(c + 1) * 128, hf * 512:(hf + 1) * 512], inc=s_fx[sl])
                p.wait("vector", s_fx[sl], 16 * (wl // 2 + 1))
                p.op("vector", lambda e, sl=sl, c=c, hf=hf: e.tensor_copy(out=wob[:, c * D + hf * 512: c * D + (hf + 1) * 512],
                                                                          in_=wos[sl][:, :]), inc=s_fixc)
                wl += 1
    p.dma("scalar", bos[:, :], boc[:, :], inc=s_fix)
    p.dma("scalar", b2s[:, :], b2[:, :], inc=s_fix)
    nfix = s_fix.n
    p.wait("tensor", s_fix, nfix)
    p.wait("tensor", s_one, 1)
    for c in range(NCH):
        if c >= 1:
            p.wait("tensor", s_sb2e, c)
        p.op("tensor", lambda e, c=c: e.matmul(pb[:, :], b2s[:, c * 128:(c + 1) * 128], onec[0:NEXP, 0:1],
                                               start=True, stop=True), inc=s_sb2)
        p.wait("vector", s_sb2, s_sb2.n)
        p.op("vector", lambda e, c=c: e.tensor_copy(out=sb2[:, c:c + 1], in_=pb[:, :]), inc=s_sb2e)

    n_u = n_h = n_y = n_ya = n_xm = n_mm = n_mrep = 0
    w1l = w2l = 0
    n_sp = n_se = n_ge = n_cmb = 0
    n_b1 = [0, 0]
    n_mrl = [0, 0]
    n_atl = [0, 0]
    n_xll = [0, 0]
    n_outl = [0, 0]

    for sw in range(NSW):
        if sw >= 1:
            p.wait("scalar", s_xm, sw * 128)
        for c in range(NCH):
            p.dma("scalar", xnb[:, c * TSW:(c + 1) * TSW],
                  xnin[c * 128:(c + 1) * 128, sw * TSW:(sw + 1) * TSW], inc=s_xnb)
        if sw >= 1:
            p.wait("vector", s_ge, sw * 16)
        p.op("vector", lambda e: e.memset(yacc[:, :], 0.0), inc=s_ini)

        for i in range(NEXP):
            gi = sw * NEXP + i
            msl = gi % 2
            if n_mrl[msl] >= 1:
                p.wait("scalar", s_mm, 2 * (gi - 2) + 2)
            p.dma("scalar", mr[msl][0:1, :], mrows[i:i + 1, sw * TSW:(sw + 1) * TSW], inc=s_mr[msl])
            n_mrl[msl] += 1
            if n_b1[msl] >= 1:
                p.wait("scalar", s_h, (gi - 1) * 64)
            p.dma("scalar", b1s[msl][:, :], b1c[i, :, :], inc=s_b1[msl])
            n_b1[msl] += 1

            for w2i in range(NWS):
                p.wait("tensor", s_mr[msl], 16 * n_mrl[msl])
                if n_mm >= 1:
                    p.wait("tensor", s_mrep, n_mm)
                p.op("tensor", lambda e, msl=msl, w2i=w2i: e.matmul(pm[:, :], one1f[0:1, :],
                                                                   mr[msl][0:1, w2i * 512:(w2i + 1) * 512],
                                                                   start=True, stop=True), inc=s_mm)
                n_mm += 1
                p.wait("vector", s_mm, n_mm)
                if gi >= 1:
                    p.wait("vector", s_xm, gi * 16)
                p.op("vector", lambda e, w2i=w2i: e.tensor_copy(out=mrep[w2i][:, :], in_=pm[:, :]), inc=s_mrep)
                n_mrep += 1
                p.wait("vector", s_mrep, n_mrep)
                p.wait("vector", s_xnb, 16 * NCH * (sw + 1))
                if gi >= 1:
                    p.wait("vector", s_u, gi * 64)
                for k in range(NCH):
                    p.op("vector", lambda e, w2i=w2i, k=k: e.tensor_mul(out=xm[w2i][:, k * 512:(k + 1) * 512],
                                                                       in0=xnb[:, k * TSW + w2i * 512: k * TSW + w2i * 512 + 512],
                                                                       in1=mrep[w2i][:, :]), inc=s_xm)
                    n_xm += 1

            for g in range(NGRP):
                gg = gi * NGRP + g
                for d8 in range(GRP):
                    d = g * GRP + d8
                    sl1 = w1l % 2
                    if w1l >= 2:
                        p.wait("sync", s_w1c, w1l - 1)
                    p.dma("sync", w1st[sl1][:, :], w1t[i, d, :, :], inc=s_w1[sl1])
                    p.wait("vector", s_w1[sl1], 16 * (w1l // 2 + 1))
                    if w1l >= 2:
                        p.wait("vector", s_u, (w1l - 1) * 2)
                    p.op("vector", lambda e, sl1=sl1: e.tensor_copy(out=w1b[sl1][:, :], in_=w1st[sl1][:, :]),
                         inc=s_w1c)
                    w1l += 1

                    sl2 = w2l % 2
                    if w2l >= 2:
                        p.wait("gpsimd", s_w2c, w2l - 1)
                    p.dma("gpsimd", w2st[sl2][:, :], w2[i, d * 128:(d + 1) * 128, :], inc=s_w2[sl2])
                    p.wait("vector", s_w2[sl2], 16 * (w2l // 2 + 1))
                    if gg >= 1:
                        p.wait("vector", s_y, gg * 16)
                    p.op("vector", lambda e, sl2=sl2, d8=d8: e.tensor_copy(out=w2b[:, d8 * D:(d8 + 1) * D],
                                                                          in_=w2st[sl2][:, :]), inc=s_w2c)
                    w2l += 1

                    for w2i in range(NWS):
                        if d8 == 0 and w2i == 0:
                            p.wait("tensor", s_xm, (gi + 1) * 16)
                        p.wait("tensor", s_w1c, w1l)
                        if n_u >= 2:
                            p.wait("tensor", s_h, n_u - 1)
                        for k in range(NCH):
                            p.op("tensor", lambda e, us=n_u % 2, sl1=sl1, k=k, w2i=w2i:
                                 e.matmul(pu[us][:, :], w1b[sl1][:, k * 128:(k + 1) * 128],
                                          xm[w2i][:, k * 512:(k + 1) * 512],
                                          start=(k == 0), stop=(k == NCH - 1)),
                                 inc=(s_u if k == NCH - 1 else None))
                        n_u += 1
                        p.wait("scalar", s_u, n_u)
                        p.wait("scalar", s_b1[msl], 16 * n_b1[msl])
                        if gg >= 1 and d8 == 0 and w2i == 0:
                            p.wait("scalar", s_y, gg * 16)
                        p.op("scalar", lambda e, us=(n_u - 1) % 2, d=d, d8=d8, w2i=w2i, msl=msl:
                             e.activation(out=hbf[:, (d8 * NWS + w2i) * 512:(d8 * NWS + w2i) * 512 + 512],
                                          in_=pu[us][:, :], func=AF.Relu, bias=b1s[msl][:, d:d + 1]),
                             inc=s_h)
                        n_h += 1

                for w2i in range(NWS):
                    for o in range(NCH):
                        if w2i == 0 and o == 0:
                            p.wait("tensor", s_h, n_h)
                            p.wait("tensor", s_w2c, w2l)
                        if n_y >= 2:
                            p.wait("tensor", s_ya, n_y - 1)
                        for d8 in range(GRP):
                            p.op("tensor", lambda e, ys=n_y % 2, d8=d8, o=o, w2i=w2i:
                                 e.matmul(py[ys][:, :], w2b[:, d8 * D + o * 128: d8 * D + o * 128 + 128],
                                          hbf[:, (d8 * NWS + w2i) * 512:(d8 * NWS + w2i) * 512 + 512],
                                          start=(d8 == 0), stop=(d8 == GRP - 1)),
                                 inc=(s_y if d8 == GRP - 1 else None))
                        n_y += 1
                        p.wait("vector", s_y, n_y)
                        if n_ya == 0 or (sw == 1 and n_ya == 512):
                            p.wait("vector", s_ini, sw + 1)
                        p.op("vector", lambda e, ys=(n_y - 1) % 2, o=o, w2i=w2i:
                             e.tensor_add(out=yacc[:, o * TSW + w2i * 512: o * TSW + w2i * 512 + 512],
                                          in0=yacc[:, o * TSW + w2i * 512: o * TSW + w2i * 512 + 512],
                                          in1=py[ys][:, :]), inc=s_ya)
                        n_ya += 1

        for w2i in range(NWS):
            gw = sw * NWS + w2i
            asl = gw % 2
            if not fused:
                if n_atl[asl] >= 1:
                    p.wait("sync", s_sp, (gw - 1) * 8)
                for c in range(NCH):
                    p.dma("sync", atw[asl][:, c * 512:(c + 1) * 512],
                          attn[c * 128:(c + 1) * 128, sw * TSW + w2i * 512: sw * TSW + w2i * 512 + 512],
                          inc=s_at[asl])
                n_atl[asl] += 1
            for o in range(NCH):
                if fused:
                    ss2 = n_se % 2
                    if n_se >= 2:
                        p.wait("sync", s_se, n_se - 1)
                    p.dma("sync", sst[ss2][:, :],
                          sred[o * 128:(o + 1) * 128, sw * TSW + w2i * 512: sw * TSW + w2i * 512 + 512],
                          inc=s_at[ss2])
                    n_atl[ss2] += 1
                    p.wait("vector", s_at[ss2], 16 * n_atl[ss2])
                    p.wait("vector", s_fix, nfix)
                    if n_se >= 2:
                        p.wait("vector", s_cmb, 2 * (n_se - 2) + 1)
                    p.op("vector", lambda e, ss2=ss2, o=o, osl=n_se % 2:
                         e.tensor_scalar(out=ssb[osl][:, :], in0=sst[ss2][:, :], scalar1=bos[:, o:o + 1],
                                         scalar2=None, op0=ALU.add), inc=s_se)
                    n_se += 1
                else:
                    if o == 0:
                        p.wait("tensor", s_at[asl], 16 * NCH * n_atl[asl])
                        p.wait("tensor", s_fixc, 16)
                    if n_sp >= 2:
                        p.wait("tensor", s_se, n_sp - 1)
                    for c in range(NCH):
                        p.op("tensor", lambda e, sps=n_sp % 2, c=c, o=o, asl=asl:
                             e.matmul(pss[sps][:, :], wob[:, c * D + o * 128: c * D + o * 128 + 128],
                                      atw[asl][:, c * 512:(c + 1) * 512],
                                      start=(c == 0), stop=(c == NCH - 1)),
                             inc=(s_sp if c == NCH - 1 else None))
                    n_sp += 1
                    p.wait("scalar", s_sp, n_sp)
                    if n_se >= 2:
                        p.wait("scalar", s_cmb, 2 * (n_se - 2) + 1)
                    p.op("scalar", lambda e, sps=(n_sp - 1) % 2, o=o, osl=n_se % 2:
                         e.activation(out=ssb[osl][:, :], in_=pss[sps][:, :], func=AF.Identity, bias=bos[:, o:o + 1]),
                         inc=s_se)
                    n_se += 1
                if n_ge % 16 == 0:
                    p.wait("scalar", s_ya, 512 * (sw + 1))
                if fused and n_ge >= 2:
                    p.wait("scalar", s_cmb, 2 * (n_ge - 2) + 1)
                p.op("scalar", lambda e, o=o, w2i=w2i, osl=n_ge % 2:
                     e.activation(out=gsb[osl][:, :], in_=yacc[:, o * TSW + w2i * 512: o * TSW + w2i * 512 + 512],
                                  func=AF.Silu, bias=sb2[:, o:o + 1]), inc=s_ge)
                n_ge += 1
                xsl = (n_xll[0] + n_xll[1]) % 2
                if n_xll[xsl] >= 1:
                    p.wait("gpsimd", s_cmb, 2 * (n_se - 2) + 2)
                p.dma("gpsimd", xst[xsl][:, :],
                      xT[o * 128:(o + 1) * 128, sw * TSW + w2i * 512: sw * TSW + w2i * 512 + 512], inc=s_xl[xsl])
                n_xll[xsl] += 1
                p.wait("vector", s_se, n_se)
                p.wait("vector", s_ge, n_ge)
                p.wait("vector", s_xl[xsl], 16 * n_xll[xsl])
                if n_cmb >= 1:
                    p.wait("vector", s_cmb, n_cmb)
                csl = (n_cmb // 2) % 2
                if n_cmb >= 4:
                    p.wait("vector", s_out[csl], 16 * (n_cmb // 4))
                p.op("vector", lambda e, osl=(n_se - 1) % 2: e.tensor_mul(out=tmul[:, :], in0=ssb[osl][:, :],
                                                                         in1=gsb[osl][:, :]), inc=s_cmb)
                n_cmb += 1
                p.wait("vector", s_cmb, n_cmb)
                p.op("vector", lambda e, xsl=xsl, csl=csl: e.tensor_add(out=x1s[csl][:, :], in0=xst[xsl][:, :],
                                                                       in1=tmul[:, :]), inc=s_cmb)
                n_cmb += 1
                p.wait("gpsimd", s_cmb, n_cmb)
                p.dma("gpsimd", x1[o * 128:(o + 1) * 128, sw * TSW + w2i * 512: sw * TSW + w2i * 512 + 512],
                      x1s[csl][:, :], inc=s_out[csl])

    p.wait("gpsimd", s_out[0], s_out[0].n)
    p.wait("gpsimd", s_out[1], s_out[1].n)



def build_tail0():
    nc = bass.Bass("TRN2", target_bir_lowering=False)
    xT = nc.dram_tensor("xT", [D, T_OWN], F32, kind="ExternalInput").ap()
    xnin = nc.dram_tensor("xnin", [D, T_OWN], BF16, kind="ExternalInput").ap()
    attn = nc.dram_tensor("attn", [D, T_OWN], BF16, kind="ExternalInput").ap()
    mrows = nc.dram_tensor("mrows", [NEXP, T_OWN], F32, kind="ExternalInput").ap()
    w1t = nc.dram_tensor("w1t", [NEXP, NDH, 128, D], F32, kind="ExternalInput").ap()
    b1c = nc.dram_tensor("b1c", [NEXP, 128, NDH], F32, kind="ExternalInput").ap()
    w2 = nc.dram_tensor("w2", [NEXP, DH, D], F32, kind="ExternalInput").ap()
    b2 = nc.dram_tensor("b2", [NEXP, D], F32, kind="ExternalInput").ap()
    woT = nc.dram_tensor("woT", [D, D], F32, kind="ExternalInput").ap()
    boc = nc.dram_tensor("boc", [128, NCH], F32, kind="ExternalInput").ap()
    x1 = nc.dram_tensor("x1", [D, T_OWN], F32, kind="ExternalOutput").ap()
    with ExitStack() as stack:
        B = _Bufs(nc, stack)
        p = _Prog(nc, stack)
        _emit_tail0(nc, p, B, xT=xT, xnin=xnin, attn=attn, mrows=mrows, w1t=w1t, b1c=b1c, w2=w2, b2=b2, woT=woT, boc=boc, x1=x1)
        block = stack.enter_context(nc.Block())
        p.replay(block)
    return nc


def _emit_attn1(nc, p, B, **dram):
    xnin = dram['xnin']
    wdq = dram['wdq']
    wdk = dram['wdk']
    wuq = dram['wuq']
    wqr = dram['wqr']
    wuk = dram['wuk']
    wuv = dram['wuv']
    wkr = dram['wkr']
    ta1 = dram['ta1']
    tb1 = dram['tb1']
    prm16 = dram['prm16']
    attn = dram.get('attn')
    wOo = dram.get('wOo')
    sp = dram.get('sp')
    fused = sp is not None
    CLIP_HI = float(np.exp(80.0 / np.sqrt(96.0)))
    CLIP_LO = float(np.exp(-80.0 / np.sqrt(96.0)))
    xw = [B.sb(f"xw{j}", [128, NCH * 512], BF16) for j in range(2)]
    wst = [B.sb(f"wst{j}", [128, 512], F32) for j in range(2)]
    wdqb = B.sb("wdqb", [128, NCH * 256], BF16)
    wdkb = B.sb("wdkb", [128, NCH * 256], BF16)
    wuqb = B.sb("wuqb", [128, 2 * 512], BF16)
    wqrb = B.sb("wqrb", [128, 2 * 256], BF16)
    wukb = B.sb("wukb", [128, 2 * 512], BF16)
    wuvb = B.sb("wuvb", [128, 2 * 512], BF16)
    wkrb = B.sb("wkrb", [128, NCH * 32], BF16)
    if fused:
        ttab = [B.sb(f"ttab{j}", [64, 2 * 512], BF16) for j in range(2)]
    else:
        tas = B.sb("tas", [64, T_FULL], BF16)
        tbs = B.sb("tbs", [64, T_FULL], BF16)
    prm = B.sb("prm", [64, 64], BF16)
    onebf1 = B.sb("onebf1", [1, 128], BF16)
    cqw = B.sb("cqw", [128, 2 * 512], BF16)
    ckw = B.sb("ckw", [128, 2 * 512], BF16)
    kbuf = B.sb("kbuf", [96, 8 * T_FULL], BF16)
    vbuf = B.sb("vbuf", [128, 32 * 520], BF16)
    qbuf = [B.sb("qbuf0", [96, 8 * 512], BF16)]
    qrtmp = [B.sb(f"qrtmp{j}", [64, 512], BF16) for j in range(2)]
    rs1 = B.sb("rs1", [64, 512], F32)
    rs2 = B.sb("rs2", [64, 512], F32)
    krw = B.sb("krw", [32, 512], BF16)
    pt = [B.sb(f"pt{j}", [128, 512], BF16) for j in range(3)]
    ubuf = B.sb("ubuf", [64, 8 * 512], BF16)
    dbuf = B.sb("dbuf", [1, 8 * 512], BF16)
    onef1 = B.sb("onef1", [1, 128], F32)
    if fused:
        awin = B.sb("awin", [128, 2 * 4 * 512], BF16)
        wOob = B.sb("wOob", [128, 4 * 1024], BF16)
        spst = [B.sb(f"spst{j}", [128, 512], F32) for j in range(2)]
    else:
        anrm = [B.sb(f"anrm{j}", [64, 512], BF16) for j in range(2)]

    pj = [B.ps(f"pj{j}", [128, 512], F32) for j in range(2)]
    pp = B.ps("pp", [128, 512], F32)
    ss = [B.ps(f"ss{j}", [128, 512], F32) for j in range(2)]
    pv = [B.ps(f"pv{j}", [128, 512], F32) for j in range(2)]
    pd = B.ps("pd", [128, 512], F32)

    s_w0 = p.sem("s_w0")
    s_w1s = p.sem("s_w1s")
    s_wc = p.sem("s_wc")
    s_misc = p.sem("s_misc")
    s_ic = p.sem("s_ic")
    s_xw = [p.sem("s_xw0"), p.sem("s_xw1")]
    s_pj = p.sem("s_pj")
    s_qt = p.sem("s_qt")
    s_pp = p.sem("s_pp")
    s_rope = p.sem("s_rope")
    s_kcp = p.sem("s_kcp")
    s_smm = p.sem("s_smm")
    s_exp = p.sem("s_exp")
    s_clp = p.sem("s_clp")
    s_msk = p.sem("s_msk")
    s_pvm = p.sem("s_pvm")
    s_ue = p.sem("s_ue")
    s_di = p.sem("s_di")
    s_dr = p.sem("s_dr")
    s_an = p.sem("s_an")
    s_ao = [p.sem("s_ao0"), p.sem("s_ao1")]
    s_spe = p.sem("s_spe")
    s_spmm = p.sem("s_spmm")
    s_spo = [p.sem("s_spo0"), p.sem("s_spo1")]
    s_tt = [p.sem("s_tt0"), p.sem("s_tt1")]

    if not fused:
        p.dma("scalar", tas[:, :], ta1[:, :], inc=s_misc)
        p.dma("scalar", tbs[:, :], tb1[:, :], inc=s_misc)
    p.dma("scalar", prm[:, :], prm16[:, :], inc=s_misc)
    nmisc = s_misc.n
    p.wait("vector", s_misc, nmisc)
    p.op("vector", lambda e: e.memset(onebf1[:, :], 1.0))
    p.op("vector", lambda e: e.memset(onef1[:, :], 1.0))
    p.op("vector", lambda e: e.memset(vbuf[:, :].rearrange("p (t h c) -> p t h c", t=32, h=8)[:, :, :, 64:65], 1.0),
         inc=s_ic)

    wjobs = []
    for c in range(NCH):
        wjobs.append((wdq[c * 128:(c + 1) * 128, :], wdqb, c * 256, 256))
    for c in range(NCH):
        wjobs.append((wdk[c * 128:(c + 1) * 128, :], wdkb, c * 256, 256))
    for c in range(2):
        wjobs.append((wuq[c * 128:(c + 1) * 128, :], wuqb, c * 512, 512))
        wjobs.append((wqr[c * 128:(c + 1) * 128, :], wqrb, c * 256, 256))
        wjobs.append((wuk[c * 128:(c + 1) * 128, :], wukb, c * 512, 512))
        wjobs.append((wuv[c * 128:(c + 1) * 128, :], wuvb, c * 512, 512))
    for c in range(NCH):
        wjobs.append((wkr[c * 128:(c + 1) * 128, :], wkrb, c * 32, 32))
    if fused:
        for c in range(4):
            for hf in range(2):
                wjobs.append((wOo[c * 128:(c + 1) * 128, hf * 512:(hf + 1) * 512], wOob, c * 1024 + hf * 512, 512))
    wl = 0
    for (src, dst, off, width) in wjobs:
        sl = wl % 2
        ssem = s_w0 if sl == 0 else s_w1s
        if wl >= 2:
            p.wait("sync", s_wc, wl - 1)
        p.dma("sync", wst[sl][:, 0:width], src, inc=ssem)
        p.wait("vector", ssem, 16 * (wl // 2 + 1))
        p.op("vector", lambda e, sl=sl, dst=dst, off=off, width=width:
             e.tensor_copy(out=dst[:, off:off + width], in_=wst[sl][:, 0:width]), inc=s_wc)
        wl += 1
    nwc = wl

    n_pj = n_qt = n_pp = n_rope = n_smm = n_exp = n_clp = n_msk = n_pvm = n_ue = n_an = n_cq = n_kcp = 0
    smm_after_window = {}
    rope_after_group = []
    s_rope_cnt = [0]

    for w in range(NW_FULL):
        half, w2 = w // 4, w % 4
        for c in range(NCH):
            if w >= 2:
                p.wait("sync", s_pj, (w - 1) * 11)
            p.dma("sync", xw[w % 2][:, c * 512:(c + 1) * 512],
                  xnin[half, c * 128:(c + 1) * 128, w2 * 512:(w2 + 1) * 512], inc=s_xw[w % 2])
        if fused:
            if w >= 2:
                p.wait("scalar", s_rope, rope_after_group[5 * (w - 1) - 1])
            p.dma("scalar", ttab[w % 2][:, 0:512], ta1[:, w * 512:(w + 1) * 512], inc=s_tt[w % 2])
            p.dma("scalar", ttab[w % 2][:, 512:1024], tb1[:, w * 512:(w + 1) * 512], inc=s_tt[w % 2])
            tav = ttab[w % 2][:, 0:512]
            tbv = ttab[w % 2][:, 512:1024]
        else:
            tav = tas[:, w * 512:(w + 1) * 512]
            tbv = tbs[:, w * 512:(w + 1) * 512]

        if w >= 1:
            p.wait("vector", s_smm, smm_after_window[w - 1])

        for (wsrc, dstb) in [(wdqb, cqw), (wdkb, ckw)]:
            for dl in range(2):
                pjs = n_pj % 2
                if n_pj >= 2:
                    p.wait("tensor", s_qt, n_pj - 1)
                if n_pj == 0 and w == 0:
                    p.wait("tensor", s_wc, nwc)
                    p.wait("tensor", s_ic, 1)
                if dl == 0 and wsrc is wdqb:
                    p.wait("tensor", s_xw[w % 2], 16 * NCH * (w // 2 + 1))
                    if w >= 1:
                        p.wait("tensor", s_pj, n_pj)
                for k in range(NCH):
                    p.op("tensor", lambda e, pjs=pjs, wsrc=wsrc, dl=dl, k=k, w=w:
                         e.matmul(pj[pjs][:, :], wsrc[:, k * 256 + dl * 128: k * 256 + dl * 128 + 128],
                                  xw[w % 2][:, k * 512:(k + 1) * 512],
                                  start=(k == 0), stop=(k == NCH - 1)),
                         inc=(s_pj if k == NCH - 1 else None))
                n_pj += 1
                p.wait("scalar", s_pj, n_pj)
                p.op("scalar", lambda e, pjs=pjs, dstb=dstb, dl=dl:
                     e.activation(out=dstb[:, dl * 512:(dl + 1) * 512], in_=pj[pjs][:, :], func=AF.Copy),
                     inc=s_qt)
                n_qt += 1

        for (wub, qk) in [(wuqb, 0), (wukb, 1)]:
            for oc in range(4):
                pjs = n_pj % 2
                if n_pj >= 2:
                    p.wait("tensor", s_qt, n_pj - 1)
                if oc == 0:
                    p.wait("tensor", s_qt, n_qt)
                for dl in range(2):
                    p.op("tensor", lambda e, pjs=pjs, wub=wub, oc=oc, dl=dl, qk=qk:
                         e.matmul(pj[pjs][:, :], wub[:, dl * 512 + oc * 128: dl * 512 + oc * 128 + 128],
                                  (cqw if qk == 0 else ckw)[:, dl * 512:(dl + 1) * 512],
                                  start=(dl == 0), stop=(dl == 1)),
                         inc=(s_pj if dl == 1 else None))
                n_pj += 1
                p.wait("scalar", s_pj, n_pj)
                for hh in range(2):
                    h = 2 * oc + hh
                    if qk == 0:
                        p.op("scalar", lambda e, pjs=pjs, hh=hh, h=h, w=w:
                             e.activation(out=qbuf[0][0:64, h * 512:(h + 1) * 512],
                                          in_=pj[pjs][hh * 64:(hh + 1) * 64, :], func=AF.Copy),
                             inc=(s_qt if hh == 1 else None))
                    else:
                        p.op("scalar", lambda e, pjs=pjs, hh=hh, h=h, w=w:
                             e.activation(out=kbuf[0:64, h * T_FULL + w * 512: h * T_FULL + w * 512 + 512],
                                          in_=pj[pjs][hh * 64:(hh + 1) * 64, :], func=AF.Copy),
                             inc=(s_qt if hh == 1 else None))
                n_qt += 1

        for hc in range(4):
            pjs = n_pj % 2
            if n_pj >= 2:
                p.wait("tensor", s_qt, n_pj - 1)
            for dl in range(2):
                p.op("tensor", lambda e, pjs=pjs, hc=hc, dl=dl:
                     e.matmul(pj[pjs][0:64, :], wqrb[:, dl * 256 + hc * 64: dl * 256 + hc * 64 + 64],
                              cqw[:, dl * 512:(dl + 1) * 512],
                              start=(dl == 0), stop=(dl == 1)),
                     inc=(s_pj if dl == 1 else None))
            n_pj += 1
            p.wait("scalar", s_pj, n_pj)
            gq = n_rope
            if gq >= 2:
                p.wait("scalar", s_pp, gq - 1)
                p.wait("scalar", s_rope, rope_after_group[gq - 2])
            p.op("scalar", lambda e, pjs=pjs, qs=gq % 2:
                 e.activation(out=qrtmp[qs][:, :], in_=pj[pjs][0:64, :], func=AF.Copy), inc=s_qt)
            n_qt += 1
            p.wait("tensor", s_qt, n_qt)
            if n_pp >= 1:
                p.wait("tensor", s_rope, rope_after_group[n_pp - 1])
            if fused and hc == 0 and w >= 1:
                p.wait("tensor", s_spe, 8 * w)
            p.op("tensor", lambda e, qs=gq % 2: e.matmul(pp[0:64, :], prm[:, :], qrtmp[qs][:, :],
                                                         start=True, stop=True), inc=s_pp)
            n_pp += 1
            p.wait("vector", s_qt, n_qt)
            p.wait("vector", s_pp, n_pp)
            if fused:
                p.wait("vector", s_tt[w % 2], 32 * (w // 2 + 1))
            p.opw("vector", lambda e, qs=gq % 2, tav=tav: e.tensor_mul(out=rs1[:, :], in0=qrtmp[qs][:, :],
                                                                      in1=tav))
            p.opw("vector", lambda e, tbv=tbv: e.tensor_mul(out=rs2[:, :], in0=pp[0:64, :],
                                                            in1=tbv))
            for hh in range(2):
                h = 2 * hc + hh
                p.op("vector", lambda e, hh=hh, h=h, w=w:
                     e.tensor_add(out=qbuf[0][64:96, h * 512:(h + 1) * 512],
                                  in0=rs1[hh * 32:(hh + 1) * 32, :], in1=rs2[hh * 32:(hh + 1) * 32, :]),
                     inc=s_rope)
            n_rope += 1
            rope_after_group.append(s_rope.n)

        pjs = n_pj % 2
        if n_pj >= 2:
            p.wait("tensor", s_qt, n_pj - 1)
        for k in range(NCH):
            p.op("tensor", lambda e, pjs=pjs, k=k, w=w:
                 e.matmul(pj[pjs][0:32, :], wkrb[:, k * 32:(k + 1) * 32],
                          xw[w % 2][:, k * 512:(k + 1) * 512],
                          start=(k == 0), stop=(k == NCH - 1)),
                 inc=(s_pj if k == NCH - 1 else None))
        n_pj += 1
        p.wait("scalar", s_pj, n_pj)
        gq = n_rope
        if gq >= 2:
            p.wait("scalar", s_pp, gq - 1)
            p.wait("scalar", s_rope, rope_after_group[gq - 2])
        p.op("scalar", lambda e, pjs=pjs, qs=gq % 2:
             e.activation(out=qrtmp[qs][0:32, :], in_=pj[pjs][0:32, :], func=AF.Copy), inc=s_qt)
        n_qt += 1
        p.wait("tensor", s_qt, n_qt)
        if n_pp >= 1:
            p.wait("tensor", s_rope, rope_after_group[n_pp - 1])
        p.op("tensor", lambda e, qs=gq % 2: e.matmul(pp[0:32, :], prm[0:32, 0:32], qrtmp[qs][0:32, :],
                                                     start=True, stop=True), inc=s_pp)
        n_pp += 1
        p.wait("vector", s_qt, n_qt)
        p.wait("vector", s_pp, n_pp)
        if w >= 1:
            p.wait("vector", s_smm, smm_after_window[w - 1])
        if fused:
            p.wait("vector", s_tt[w % 2], 32 * (w // 2 + 1))
        p.opw("vector", lambda e, qs=gq % 2, tav=tav: e.tensor_mul(out=rs1[0:32, :], in0=qrtmp[qs][0:32, :],
                                                                  in1=tav[0:32, :]))
        p.opw("vector", lambda e, tbv=tbv: e.tensor_mul(out=rs2[0:32, :], in0=pp[0:32, :],
                                                        in1=tbv[0:32, :]))
        p.opw("vector", lambda e: e.tensor_add(out=krw[:, :], in0=rs1[0:32, :], in1=rs2[0:32, :]))
        for h in range(8):
            p.op("vector", lambda e, h=h, w=w: e.tensor_copy(out=kbuf[64:96, h * T_FULL + w * 512: h * T_FULL + w * 512 + 512],
                                                             in_=krw[:, :]),
                 inc=(s_rope if h == 7 else s_kcp))
        n_rope += 1
        n_kcp += 7
        rope_after_group.append(s_rope.n)

        for j in range(4):
            kt = 4 * w + j
            pjs = n_pj % 2
            if n_pj >= 2:
                p.wait("tensor", s_qt, n_pj - 1)
            for dl in range(2):
                p.op("tensor", lambda e, pjs=pjs, j=j, dl=dl:
                     e.matmul(pj[pjs][:, :], ckw[:, dl * 512 + j * 128: dl * 512 + j * 128 + 128],
                              wuvb[:, dl * 512:(dl + 1) * 512],
                              start=(dl == 0), stop=(dl == 1)),
                     inc=(s_pj if dl == 1 else None))
            n_pj += 1
            p.wait("scalar", s_pj, n_pj)
            p.op("scalar", lambda e, pjs=pjs, kt=kt:
                 e.activation(out=vbuf[:, kt * 520: kt * 520 + 520].rearrange("p (h c) -> p h c", h=8)[:, :, 0:64],
                              in_=pj[pjs][:, :], func=AF.Copy),
                 inc=s_qt)
            n_qt += 1

        nkt = 4 * w + 4
        p.wait("tensor", s_rope, 2 * 4 * (w + 1) + (w + 1))
        p.wait("tensor", s_kcp, n_kcp)
        p.wait("tensor", s_qt, n_qt)
        for h in range(8):
            if h == 0:
                p.wait("vector", s_an, 8 * w)
                p.wait("vector", s_di, w)
            for kt in range(nkt):
                sb = n_smm % 2
                if n_smm >= 2:
                    p.wait("tensor", s_exp, n_smm - 1)
                p.op("tensor", lambda e, sb=sb, h=h, kt=kt, w=w:
                     e.matmul(ss[sb][:, :], kbuf[0:96, h * T_FULL + kt * 128: h * T_FULL + kt * 128 + 128],
                              qbuf[0][0:96, h * 512:(h + 1) * 512],
                              start=True, stop=True), inc=s_smm)
                n_smm += 1
                p.wait("scalar", s_smm, n_smm)
                if n_exp >= 3:
                    p.wait("scalar", s_pvm, n_exp - 2)
                p.op("scalar", lambda e, sb=sb, slot=n_exp % 3:
                     e.activation(out=pt[slot][:, :], in_=ss[sb][:, :], func=AF.Exp,
                                  scale=float(1.0 / np.sqrt(96.0))), inc=s_exp)
                n_exp += 1
                p.wait("vector", s_exp, n_exp)
                p.op("vector", lambda e, slot=(n_exp - 1) % 3:
                     e.tensor_scalar(out=pt[slot][:, :], in0=pt[slot][:, :], scalar1=CLIP_HI,
                                     scalar2=CLIP_LO, op0=ALU.min, op1=ALU.max), inc=s_clp)
                n_clp += 1
                diag = kt >= 4 * w
                if diag:
                    p.wait("gpsimd", s_clp, n_clp)
                    p.op("gpsimd", lambda e, slot=(n_exp - 1) % 3, base=w * 512 - kt * 128:
                         e.affine_select(out=pt[slot][:, :], in_=pt[slot][:, :], compare_op=ALU.is_ge,
                                         fill=0.0, base=base, pattern=[[1, 512]], channel_multiplier=-1),
                         inc=s_msk)
                    n_msk += 1
                p.wait("tensor", s_clp, n_clp)
                if diag:
                    p.wait("tensor", s_msk, n_msk)
                if kt == 0:
                    gh = 8 * w + h
                    p.wait("tensor", s_ue, 2 * (gh - 2) + 2)
                p.op("tensor", lambda e, pvs=h % 2, kt=kt, h=h, slot=(n_exp - 1) % 3, nkt=nkt:
                     e.matmul(pv[pvs][0:65, :], vbuf[:, kt * 520 + h * 65: kt * 520 + h * 65 + 65],
                              pt[slot][:, :], start=(kt == 0), stop=(kt == nkt - 1)), inc=s_pvm)
                n_pvm += 1
            p.wait("vector", s_pvm, n_pvm)
            p.op("vector", lambda e, pvs=h % 2, h=h: e.tensor_copy(out=ubuf[0:64, h * 512:(h + 1) * 512],
                                                                   in_=pv[pvs][0:64, :]), inc=s_ue)
            p.op("vector", lambda e, pvs=h % 2, h=h: e.tensor_copy(out=dbuf[0:1, h * 512:(h + 1) * 512],
                                                                  in_=pv[pvs][64:65, :]), inc=s_ue)
            n_ue += 2
        smm_after_window[w] = n_smm

        p.wait("scalar", s_ue, n_ue)
        p.wait("scalar", s_dr, 8 * w)
        p.opw("scalar", lambda e: e.activation(out=dbuf[:, :], in_=dbuf[:, :], func=AF.Ln))
        p.op("scalar", lambda e: e.activation(out=dbuf[:, :], in_=dbuf[:, :], func=AF.Exp, scale=-1.0), inc=s_di)
        for h in range(8):
            p.wait("tensor", s_di, w + 1)
            if n_an >= 1:
                p.wait("tensor", s_an, n_an)
            if fused and h == 0 and w >= 1:
                p.wait("tensor", s_spe, 8 * w)
            p.op("tensor", lambda e, h=h: e.matmul(pd[0:64, :], onebf1[0:1, 0:64], dbuf[0:1, h * 512:(h + 1) * 512],
                                                   start=True, stop=True), inc=s_dr)
            p.wait("vector", s_dr, s_dr.n)
            if fused:
                if h == 0 and w >= 2:
                    p.wait("vector", s_spe, 8 * (w - 1))
                p.op("vector", lambda e, h=h, w=w:
                     e.tensor_mul(out=awin[(h % 2) * 64:(h % 2) * 64 + 64,
                                           (w % 2) * 2048 + (h // 2) * 512:(w % 2) * 2048 + (h // 2) * 512 + 512],
                                  in0=ubuf[0:64, h * 512:(h + 1) * 512], in1=pd[0:64, :]), inc=s_an)
                n_an += 1
            else:
                sl = n_an % 2
                if n_an >= 2:
                    p.wait("vector", s_ao[sl], 16 * (n_an // 2))
                p.op("vector", lambda e, h=h, sl=sl: e.tensor_mul(out=anrm[sl][:, :], in0=ubuf[0:64, h * 512:(h + 1) * 512],
                                                                  in1=pd[0:64, :]), inc=s_an)
                n_an += 1
                p.wait("sync", s_an, n_an)
                p.dma("sync", attn[h * 64:(h + 1) * 64, w * 512:(w + 1) * 512], anrm[sl][:, :], inc=s_ao[sl])

        if fused:
            for o in range(8):
                if o == 0:
                    p.wait("tensor", s_an, 8 * (w + 1))
                nsp = 8 * w + o
                if nsp >= 2:
                    p.wait("tensor", s_spe, nsp - 1)
                bank = pd if o % 2 == 0 else pp
                for k in range(4):
                    p.op("tensor", lambda e, bank=bank, k=k, o=o, w=w:
                         e.matmul(bank[:, :],
                                  wOob[:, k * 1024 + o * 128: k * 1024 + o * 128 + 128],
                                  awin[:, (w % 2) * 2048 + k * 512:(w % 2) * 2048 + k * 512 + 512],
                                  start=(k == 0), stop=(k == 3)),
                         inc=(s_spmm if k == 3 else None))
                p.wait("scalar", s_spmm, s_spmm.n)
                ssl = nsp % 2
                if nsp >= 2:
                    p.wait("scalar", s_spo[ssl], 16 * (nsp // 2))
                p.op("scalar", lambda e, bank=bank, ssl=ssl:
                     e.activation(out=spst[ssl][:, :], in_=bank[:, :], func=AF.Copy),
                     inc=s_spe)
                p.wait("sync", s_spe, s_spe.n)
                p.dma("sync", sp[(w // 4) * 1024 + o * 128:(w // 4) * 1024 + o * 128 + 128,
                                 (w % 4) * 512:(w % 4) * 512 + 512], spst[ssl][:, :], inc=s_spo[ssl])

    if fused:
        p.wait("gpsimd", s_spo[0], s_spo[0].n)
        p.wait("gpsimd", s_spo[1], s_spo[1].n)
    else:
        p.wait("gpsimd", s_ao[0], s_ao[0].n)
        p.wait("gpsimd", s_ao[1], s_ao[1].n)



def build_attn1():
    nc = bass.Bass("TRN2", target_bir_lowering=False)
    xnin = nc.dram_tensor("xnin", [2, D, T_OWN], BF16, kind="ExternalInput").ap()
    wdq = nc.dram_tensor("wdq", [D, 256], F32, kind="ExternalInput").ap()
    wdk = nc.dram_tensor("wdk", [D, 256], F32, kind="ExternalInput").ap()
    wuq = nc.dram_tensor("wuq", [256, 512], F32, kind="ExternalInput").ap()
    wqr = nc.dram_tensor("wqr", [256, 256], F32, kind="ExternalInput").ap()
    wuk = nc.dram_tensor("wuk", [256, 512], F32, kind="ExternalInput").ap()
    wuv = nc.dram_tensor("wuv", [256, 512], F32, kind="ExternalInput").ap()
    wkr = nc.dram_tensor("wkr", [D, 32], F32, kind="ExternalInput").ap()
    ta1 = nc.dram_tensor("ta1", [64, T_FULL], BF16, kind="ExternalInput").ap()
    tb1 = nc.dram_tensor("tb1", [64, T_FULL], BF16, kind="ExternalInput").ap()
    prm16 = nc.dram_tensor("prm16", [64, 64], BF16, kind="ExternalInput").ap()
    attn = nc.dram_tensor("attn", [512, T_FULL], BF16, kind="ExternalOutput").ap()
    with ExitStack() as stack:
        B = _Bufs(nc, stack)
        p = _Prog(nc, stack)
        _emit_attn1(nc, p, B, xnin=xnin, wdq=wdq, wdk=wdk, wuq=wuq, wqr=wqr, wuk=wuk, wuv=wuv, wkr=wkr, ta1=ta1, tb1=tb1, prm16=prm16, attn=attn)
        block = stack.enter_context(nc.Block())
        p.replay(block)
    return nc


def _emit_tail1(nc, p, B, **dram):
    xT = dram['xT']
    xnin = dram['xnin']
    attn = dram.get('attn')
    sred = dram.get('sred')
    fused = sred is not None
    wrows = dram['wrows']
    w0t = dram['w0t']
    w1t = dram['w1t']
    wo = dram['wo']
    wO = dram['wO']
    x2 = dram['x2']
    NSW, TSW, NWS = 2, 1024, 2
    xnb = B.sb("xnb", [128, NCH * TSW], BF16)
    one1f = B.sb("one1f", [1, 128], F32)
    mr = [B.sb(f"mr{j}", [1, TSW], F32) for j in range(2)]
    mrep = [B.sb(f"mrep{j}", [128, 512], F32) for j in range(2)]
    xwe = [B.sb(f"xwe{j}", [128, NCH * 512], BF16) for j in range(2)]
    w0st = [B.sb(f"w0st{j}", [128, D], F32) for j in range(2)]
    w0b = [B.sb(f"w0b{j}", [128, D], BF16) for j in range(2)]
    w1st = [B.sb(f"w1st{j}", [128, D], F32) for j in range(2)]
    w1b = [B.sb(f"w1b{j}", [128, D], BF16) for j in range(2)]
    w2st = [B.sb(f"w2st{j}", [128, D], F32) for j in range(2)]
    w2b = B.sb("w2b", [128, GRP * D], BF16)
    sil = [B.sb(f"sil{j}", [128, 512], BF16) for j in range(2)]
    hbf = B.sb("hbf", [128, GRP * NWS * 512], BF16)
    yacc = B.sb("yacc", [128, NCH * TSW], F32)
    if not fused:
        wos = [B.sb(f"wos{j}", [128, 512], F32) for j in range(2)]
        wob = B.sb("wob", [128, NCH * D], BF16)
        atw = [B.sb(f"atw{j}", [128, NCH * 512], BF16) for j in range(2)]
    else:
        sst = [B.sb(f"sst{j}", [128, 512], F32) for j in range(2)]
    xst = [B.sb(f"xst{j}", [128, 512], F32) for j in range(2)]
    ssb = [B.sb(f"ssb{j}", [128, 512], F32) for j in range(2)]
    gsb = [B.sb(f"gsb{j}", [128, 512], BF16) for j in range(2)]
    tmul = B.sb("tmul", [128, 512], F32)
    x1s = [B.sb(f"x1s{j}", [128, 512], F32) for j in range(2)]

    pu = [B.ps(f"pu{j}", [128, 512], F32) for j in range(2)]
    pvv = [B.ps(f"pvv{j}", [128, 512], F32) for j in range(2)]
    py = [B.ps(f"py{j}", [128, 512], F32) for j in range(2)]
    pm = B.ps("pm", [128, 512], F32)

    s_fx = [p.sem("s_fx0"), p.sem("s_fx1")]
    s_fixc = p.sem("s_fixc")
    s_xnb = p.sem("s_xnb")
    s_mr = [p.sem("s_mr0"), p.sem("s_mr1")]
    s_mm = p.sem("s_mm")
    s_mrep = p.sem("s_mrep")
    s_xm = p.sem("s_xm")
    s_w0 = [p.sem("s_w00"), p.sem("s_w01")]
    s_w0c = p.sem("s_w0c")
    s_w1 = [p.sem("s_w10"), p.sem("s_w11")]
    s_w1c = p.sem("s_w1c")
    s_w2 = [p.sem("s_w20"), p.sem("s_w21")]
    s_w2c = p.sem("s_w2c")
    s_u = p.sem("s_u")
    s_sil = p.sem("s_sil")
    s_v = p.sem("s_v")
    s_hm = p.sem("s_hm")
    s_y = p.sem("s_y")
    s_ya = p.sem("s_ya")
    s_ini = p.sem("s_ini")
    s_at = [p.sem("s_at0"), p.sem("s_at1")]
    s_xl = [p.sem("s_xl0"), p.sem("s_xl1")]
    s_sp = p.sem("s_sp")
    s_se = p.sem("s_se")
    s_ge = p.sem("s_ge")
    s_cmb = p.sem("s_cmb")
    s_out = [p.sem("s_out0"), p.sem("s_out1")]

    s_one = p.sem("s_one")
    p.op("vector", lambda e: e.memset(one1f[:, :], 1.0), inc=s_one)
    wl = 0
    if not fused:
        for c in range(NCH):
            for hf in range(2):
                sl = wl % 2
                if wl >= 2:
                    p.wait("scalar", s_fixc, wl - 1)
                p.dma("scalar", wos[sl][:, :], wO[c * 128:(c + 1) * 128, hf * 512:(hf + 1) * 512], inc=s_fx[sl])
                p.wait("vector", s_fx[sl], 16 * (wl // 2 + 1))
                p.op("vector", lambda e, sl=sl, c=c, hf=hf: e.tensor_copy(out=wob[:, c * D + hf * 512: c * D + (hf + 1) * 512],
                                                                          in_=wos[sl][:, :]), inc=s_fixc)
                wl += 1

    n_u = n_sil = n_v = n_hm = n_y = n_ya = n_xm = n_mm = n_mrep = 0
    w0l = w1l = w2l = 0
    n_sp = n_se = n_ge = n_cmb = 0
    n_mrl = [0, 0]
    n_atl = [0, 0]
    n_xll = [0, 0]

    for sw in range(NSW):
        if sw >= 1:
            p.wait("scalar", s_u, sw * 512)
            p.wait("scalar", s_xm, sw * 128)
        for c in range(NCH):
            p.dma("scalar", xnb[:, c * TSW:(c + 1) * TSW],
                  xnin[c * 128:(c + 1) * 128, sw * TSW:(sw + 1) * TSW], inc=s_xnb)
        if sw >= 1:
            p.wait("vector", s_ge, sw * 16)
        p.op("vector", lambda e: e.memset(yacc[:, :], 0.0), inc=s_ini)

        for i in range(NEXP):
            gi = sw * NEXP + i
            msl = gi % 2
            if n_mrl[msl] >= 1:
                p.wait("scalar", s_mm, 2 * (gi - 2) + 2)
            p.dma("scalar", mr[msl][0:1, :], wrows[i:i + 1, sw * TSW:(sw + 1) * TSW], inc=s_mr[msl])
            n_mrl[msl] += 1

            for w2i in range(NWS):
                p.wait("tensor", s_mr[msl], 16 * n_mrl[msl])
                p.wait("tensor", s_one, 1)
                if n_mm >= 1:
                    p.wait("tensor", s_mrep, n_mm)
                p.op("tensor", lambda e, msl=msl, w2i=w2i: e.matmul(pm[:, :], one1f[0:1, :],
                                                                   mr[msl][0:1, w2i * 512:(w2i + 1) * 512],
                                                                   start=True, stop=True), inc=s_mm)
                n_mm += 1
                p.wait("vector", s_mm, n_mm)
                if gi >= 1:
                    p.wait("vector", s_xm, gi * 16)
                p.op("vector", lambda e, w2i=w2i: e.tensor_copy(out=mrep[w2i][:, :], in_=pm[:, :]), inc=s_mrep)
                n_mrep += 1
                p.wait("vector", s_mrep, n_mrep)
                p.wait("vector", s_xnb, 16 * NCH * (sw + 1))
                if gi >= 1:
                    p.wait("vector", s_v, gi * 64)
                for k in range(NCH):
                    p.op("vector", lambda e, w2i=w2i, k=k: e.tensor_mul(out=xwe[w2i][:, k * 512:(k + 1) * 512],
                                                                       in0=xnb[:, k * TSW + w2i * 512: k * TSW + w2i * 512 + 512],
                                                                       in1=mrep[w2i][:, :]), inc=s_xm)
                    n_xm += 1

            for g in range(NGRP):
                gg = gi * NGRP + g
                for d8 in range(GRP):
                    d = g * GRP + d8
                    sl0 = w0l % 2
                    if w0l >= 2:
                        p.wait("sync", s_w0c, w0l - 1)
                    p.dma("sync", w0st[sl0][:, :], w0t[i, d, :, :], inc=s_w0[sl0])
                    p.wait("vector", s_w0[sl0], 16 * (w0l // 2 + 1))
                    if w0l >= 2:
                        p.wait("vector", s_u, (w0l - 1) * 2)
                    p.op("vector", lambda e, sl0=sl0: e.tensor_copy(out=w0b[sl0][:, :], in_=w0st[sl0][:, :]),
                         inc=s_w0c)
                    w0l += 1
                    sl1 = w1l % 2
                    if w1l >= 2:
                        p.wait("sync", s_w1c, w1l - 1)
                    p.dma("sync", w1st[sl1][:, :], w1t[i, d, :, :], inc=s_w1[sl1])
                    p.wait("vector", s_w1[sl1], 16 * (w1l // 2 + 1))
                    if w1l >= 2:
                        p.wait("vector", s_v, (w1l - 1) * 2)
                    p.op("vector", lambda e, sl1=sl1: e.tensor_copy(out=w1b[sl1][:, :], in_=w1st[sl1][:, :]),
                         inc=s_w1c)
                    w1l += 1
                    sl2 = w2l % 2
                    if w2l >= 2:
                        p.wait("gpsimd", s_w2c, w2l - 1)
                    p.dma("gpsimd", w2st[sl2][:, :], wo[i, d * 128:(d + 1) * 128, :], inc=s_w2[sl2])
                    p.wait("vector", s_w2[sl2], 16 * (w2l // 2 + 1))
                    if gg >= 1:
                        p.wait("vector", s_y, gg * 16)
                    p.op("vector", lambda e, sl2=sl2, d8=d8: e.tensor_copy(out=w2b[:, d8 * D:(d8 + 1) * D],
                                                                          in_=w2st[sl2][:, :]), inc=s_w2c)
                    w2l += 1

                    for w2i in range(NWS):
                        if d8 == 0 and w2i == 0:
                            p.wait("tensor", s_xm, (gi + 1) * 16)
                            p.wait("tensor", s_xnb, 16 * NCH * (sw + 1))
                            if (not fused) and sw == 0 and gi == 0 and g == 0:
                                p.wait("tensor", s_fixc, 16)
                        p.wait("tensor", s_w0c, w0l)
                        if n_u >= 2:
                            p.wait("tensor", s_sil, n_u - 1)
                        for k in range(NCH):
                            p.op("tensor", lambda e, us=n_u % 2, sl0=sl0, k=k, w2i=w2i, sw=sw:
                                 e.matmul(pu[us][:, :], w0b[sl0][:, k * 128:(k + 1) * 128],
                                          xnb[:, k * TSW + w2i * 512: k * TSW + w2i * 512 + 512],
                                          start=(k == 0), stop=(k == NCH - 1)),
                                 inc=(s_u if k == NCH - 1 else None))
                        n_u += 1
                        p.wait("scalar", s_u, n_u)
                        if n_sil >= 2:
                            p.wait("scalar", s_hm, n_sil - 1)
                        p.op("scalar", lambda e, us=(n_u - 1) % 2, ssl=n_sil % 2:
                             e.activation(out=sil[ssl][:, :], in_=pu[us][:, :], func=AF.Silu), inc=s_sil)
                        n_sil += 1
                        p.wait("tensor", s_w1c, w1l)
                        if n_v >= 2:
                            p.wait("tensor", s_hm, n_v - 1)
                        for k in range(NCH):
                            p.op("tensor", lambda e, vs=n_v % 2, sl1=sl1, k=k, w2i=w2i:
                                 e.matmul(pvv[vs][:, :], w1b[sl1][:, k * 128:(k + 1) * 128],
                                          xwe[w2i][:, k * 512:(k + 1) * 512],
                                          start=(k == 0), stop=(k == NCH - 1)),
                                 inc=(s_v if k == NCH - 1 else None))
                        n_v += 1
                        p.wait("vector", s_sil, n_sil)
                        p.wait("vector", s_v, n_v)
                        if gg >= 1 and d8 == 0 and w2i == 0:
                            p.wait("vector", s_y, gg * 16)
                        p.op("vector", lambda e, ssl=(n_sil - 1) % 2, vs=(n_v - 1) % 2, d8=d8, w2i=w2i:
                             e.tensor_mul(out=hbf[:, (d8 * NWS + w2i) * 512:(d8 * NWS + w2i) * 512 + 512],
                                          in0=sil[ssl][:, :], in1=pvv[vs][:, :]), inc=s_hm)
                        n_hm += 1

                for w2i in range(NWS):
                    for o in range(NCH):
                        if w2i == 0 and o == 0:
                            p.wait("tensor", s_hm, n_hm)
                            p.wait("tensor", s_w2c, w2l)
                        if n_y >= 2:
                            p.wait("tensor", s_ya, n_y - 1)
                        for d8 in range(GRP):
                            p.op("tensor", lambda e, ys=n_y % 2, d8=d8, o=o, w2i=w2i:
                                 e.matmul(py[ys][:, :], w2b[:, d8 * D + o * 128: d8 * D + o * 128 + 128],
                                          hbf[:, (d8 * NWS + w2i) * 512:(d8 * NWS + w2i) * 512 + 512],
                                          start=(d8 == 0), stop=(d8 == GRP - 1)),
                                 inc=(s_y if d8 == GRP - 1 else None))
                        n_y += 1
                        p.wait("vector", s_y, n_y)
                        if n_ya == 0 or (sw == 1 and n_ya == 512):
                            p.wait("vector", s_ini, sw + 1)
                        p.op("vector", lambda e, ys=(n_y - 1) % 2, o=o, w2i=w2i:
                             e.tensor_add(out=yacc[:, o * TSW + w2i * 512: o * TSW + w2i * 512 + 512],
                                          in0=yacc[:, o * TSW + w2i * 512: o * TSW + w2i * 512 + 512],
                                          in1=py[ys][:, :]), inc=s_ya)
                        n_ya += 1

        for w2i in range(NWS):
            gw = sw * NWS + w2i
            asl = gw % 2
            if not fused:
                if n_atl[asl] >= 1:
                    p.wait("sync", s_sp, (gw - 1) * 8)
                for c in range(NCH):
                    p.dma("sync", atw[asl][:, c * 512:(c + 1) * 512],
                          attn[c * 128:(c + 1) * 128, sw * TSW + w2i * 512: sw * TSW + w2i * 512 + 512],
                          inc=s_at[asl])
                n_atl[asl] += 1
            for o in range(NCH):
                if fused:
                    ss2 = n_se % 2
                    if n_se >= 2:
                        p.wait("sync", s_se, n_se - 1)
                    p.dma("sync", sst[ss2][:, :],
                          sred[o * 128:(o + 1) * 128, sw * TSW + w2i * 512: sw * TSW + w2i * 512 + 512],
                          inc=s_at[ss2])
                    n_atl[ss2] += 1
                    p.wait("vector", s_at[ss2], 16 * n_atl[ss2])
                    if n_se >= 2:
                        p.wait("vector", s_cmb, 2 * (n_se - 2) + 1)
                    p.op("vector", lambda e, ss2=ss2, osl=n_se % 2:
                         e.tensor_copy(out=ssb[osl][:, :], in_=sst[ss2][:, :]), inc=s_se)
                    n_se += 1
                else:
                    if o == 0:
                        p.wait("tensor", s_at[asl], 16 * NCH * n_atl[asl])
                        p.wait("tensor", s_sil, n_sil)
                    if n_sp >= 2:
                        p.wait("tensor", s_se, n_sp - 1)
                    for c in range(NCH):
                        p.op("tensor", lambda e, sps=n_sp % 2, c=c, o=o, asl=asl:
                             e.matmul(pu[sps][:, :], wob[:, c * D + o * 128: c * D + o * 128 + 128],
                                      atw[asl][:, c * 512:(c + 1) * 512],
                                      start=(c == 0), stop=(c == NCH - 1)),
                             inc=(s_sp if c == NCH - 1 else None))
                    n_sp += 1
                    p.wait("scalar", s_sp, n_sp)
                    if n_se >= 2:
                        p.wait("scalar", s_cmb, 2 * (n_se - 2) + 1)
                    p.op("scalar", lambda e, sps=(n_sp - 1) % 2, osl=n_se % 2:
                         e.activation(out=ssb[osl][:, :], in_=pu[sps][:, :], func=AF.Copy), inc=s_se)
                    n_se += 1
                if n_ge % 16 == 0:
                    p.wait("scalar", s_ya, 512 * (sw + 1))
                if fused and n_ge >= 2:
                    p.wait("scalar", s_cmb, 2 * (n_ge - 2) + 1)
                p.op("scalar", lambda e, o=o, w2i=w2i, osl=n_ge % 2:
                     e.activation(out=gsb[osl][:, :], in_=yacc[:, o * TSW + w2i * 512: o * TSW + w2i * 512 + 512],
                                  func=AF.Silu), inc=s_ge)
                n_ge += 1
                xsl = (n_xll[0] + n_xll[1]) % 2
                if n_xll[xsl] >= 1:
                    p.wait("gpsimd", s_cmb, 2 * (n_se - 2) + 2)
                p.dma("gpsimd", xst[xsl][:, :],
                      xT[o * 128:(o + 1) * 128, sw * TSW + w2i * 512: sw * TSW + w2i * 512 + 512], inc=s_xl[xsl])
                n_xll[xsl] += 1
                p.wait("vector", s_se, n_se)
                p.wait("vector", s_ge, n_ge)
                p.wait("vector", s_xl[xsl], 16 * n_xll[xsl])
                if n_cmb >= 1:
                    p.wait("vector", s_cmb, n_cmb)
                csl = (n_cmb // 2) % 2
                if n_cmb >= 4:
                    p.wait("vector", s_out[csl], 16 * (n_cmb // 4))
                p.op("vector", lambda e, osl=(n_se - 1) % 2: e.tensor_mul(out=tmul[:, :], in0=ssb[osl][:, :],
                                                                         in1=gsb[osl][:, :]), inc=s_cmb)
                n_cmb += 1
                p.wait("vector", s_cmb, n_cmb)
                p.op("vector", lambda e, xsl=xsl, csl=csl: e.tensor_add(out=x1s[csl][:, :], in0=xst[xsl][:, :],
                                                                       in1=tmul[:, :]), inc=s_cmb)
                n_cmb += 1
                p.wait("gpsimd", s_cmb, n_cmb)
                p.dma("gpsimd", x2[o * 128:(o + 1) * 128, sw * TSW + w2i * 512: sw * TSW + w2i * 512 + 512],
                      x1s[csl][:, :], inc=s_out[csl])

    p.wait("gpsimd", s_out[0], s_out[0].n)
    p.wait("gpsimd", s_out[1], s_out[1].n)



def build_tail1():
    nc = bass.Bass("TRN2", target_bir_lowering=False)
    xT = nc.dram_tensor("xT", [D, T_OWN], F32, kind="ExternalInput").ap()
    xnin = nc.dram_tensor("xnin", [D, T_OWN], BF16, kind="ExternalInput").ap()
    attn = nc.dram_tensor("attn", [D, T_OWN], BF16, kind="ExternalInput").ap()
    wrows = nc.dram_tensor("wrows", [NEXP, T_OWN], F32, kind="ExternalInput").ap()
    w0t = nc.dram_tensor("w0t", [NEXP, NDH, 128, D], F32, kind="ExternalInput").ap()
    w1t = nc.dram_tensor("w1t", [NEXP, NDH, 128, D], F32, kind="ExternalInput").ap()
    wo = nc.dram_tensor("wo", [NEXP, DH, D], F32, kind="ExternalInput").ap()
    wO = nc.dram_tensor("wO", [D, D], F32, kind="ExternalInput").ap()
    x2 = nc.dram_tensor("x2", [D, T_OWN], F32, kind="ExternalOutput").ap()
    with ExitStack() as stack:
        B = _Bufs(nc, stack)
        p = _Prog(nc, stack)
        _emit_tail1(nc, p, B, xT=xT, xnin=xnin, attn=attn, wrows=wrows, w0t=w0t, w1t=w1t, wo=wo, wO=wO, x2=x2)
        block = stack.enter_context(nc.Block())
        p.replay(block)
    return nc


C1 = 768
JW1 = 384
NJT1 = C1 // 128
FG = 2


def _emit_tail1s(nc, p, B, **dram):
    xT = dram['xT']
    xnin = dram['xnin']
    sred = dram['sred']
    wrows = dram['wrows']
    w0t = dram['w0t']
    w1t = dram['w1t']
    wo = dram['wo']
    x2 = dram['x2']
    sut = dram['sut']
    su16 = dram['su16']
    iot = dram['iot']
    idf = dram['idf']
    idb = dram['idb']
    C, JW, NJT = C1, JW1, NJT1
    NFG = NDH // FG

    idbs = B.sb("idbs", [128, 128], BF16)
    idfs = B.sb("idfs", [128, 128], F32)
    suts = B.sb("suts", [128, 128], F32)
    su16s = B.sb("su16s", [16, 16], F32)
    iots = B.sb("iots", [128, C], F32)
    one1f = B.sb("one1f", [1, 128], F32)
    onec = B.sb("onec", [128, 1], F32)
    mtok = B.sb("mtok", [128, 16 * NEXP], F32)
    mbin = B.sb("mbin", [128, 16], F32)
    pcols = B.sb("pcols", [128, 16], F32)
    pm1 = B.sb("pm1", [128, 16], F32)
    csb = B.sb("csb", [16, 1], F32)
    offsb = B.sb("offsb", [16, 1], F32)
    offr = B.sb("offr", [1, 16], F32)
    sil = [B.sb(f"sil{j}", [128, JW], BF16) for j in range(2)]
    u0 = B.mark()
    rsb = B.sb("rsb", [NEXP, T_OWN], F32)
    xnb = [B.sb(f"xnb{j}", [128, T_OWN], BF16) for j in range(2)]
    u_end = B.mark()
    B.rewind(u0)
    gbuf = B.sb("gbuf", [128, 16 * JW], BF16)
    u_end = max(u_end, B.mark())
    B.rewind(u0)
    ssb = [B.sb(f"ssb{j}", [128, 512], F32) for j in range(2)]
    gsb = [B.sb(f"gsb{j}", [128, 512], BF16) for j in range(2)]
    xst = [B.sb(f"xst{j}", [128, 512], F32) for j in range(2)]
    tmul = B.sb("tmul", [128, 512], F32)
    x1s = [B.sb(f"x1s{j}", [128, 512], F32) for j in range(2)]
    u_end = max(u_end, B.mark())
    B.seek(u_end)
    gtb = B.sb("gtb", [128, NJT * T_OWN], BF16)
    xy = B.sb("xy", [128, 16 * JW], BF16)
    hall = B.sb("hall", [128, NDH * C], BF16)
    w0b = [B.sb(f"w0b{j}", [128, FG * D], BF16) for j in range(2)]
    w1b = [B.sb(f"w1b{j}", [128, FG * D], BF16) for j in range(2)]
    wob = [B.sb(f"wob{j}", [128, FG * 512], BF16) for j in range(2)]
    yaccb = B.sb("yaccb", [128, NCH * T_OWN], BF16)
    xn_tok = B.sb("xn_tok", [128, 16 * NCH * 128], BF16)

    bkh = B.banks8("bk")
    bkb = [h.bitcast(BF16) for h in bkh]

    s_ld = p.sem("s_ld")
    s_cn = p.sem("s_ic")
    s_xnl = [p.sem("s_xw0"), p.sem("s_xw1")]
    s_mt = p.sem("s_rt")
    s_mte = p.sem("s_rc")
    s_xt = p.sem("s_pp")
    s_xte = p.sem("s_qt")
    s_xts = p.sem("s_h")
    s_pfm = p.sem("s_mm")
    s_pfe = p.sem("s_ev")
    s_g = p.sem("s_lg")
    s_gt = p.sem("s_rope")
    s_gm = p.sem("s_pj")
    s_ge = p.sem("s_ge")
    s_gs = p.sem("s_cs")
    s_tr = p.sem("s_smm")
    s_tre = p.sem("s_exp")
    s_trs = p.sem("s_msk")
    s_w0 = [p.sem("s_w20"), p.sem("s_w21")]
    s_w1 = [p.sem("s_w10"), p.sem("s_w11")]
    s_wo = [p.sem("s_fx0"), p.sem("s_fx1")]
    s_u = p.sem("s_u")
    s_sil = p.sem("s_v")
    s_hm = p.sem("s_pvm")
    s_y = p.sem("s_y")
    s_p2 = p.sem("s_tm")
    s_ye = p.sem("s_ue")
    s_sc = p.sem("s_di")
    s_sa = p.sem("s_dr")
    s_sl = [p.sem("s_at0"), p.sem("s_at1")]
    s_xl = [p.sem("s_xl0"), p.sem("s_xl1")]
    s_gsb = p.sem("s_an")
    s_cmb = p.sem("s_cmb")
    s_out = [p.sem("s_out0"), p.sem("s_out1")]

    p.dma("sync", idbs[:, :], idb[:, :], inc=s_ld)
    p.dma("sync", idfs[:, :], idf[:, :], inc=s_ld)
    p.dma("sync", suts[:, :], sut[:, :], inc=s_ld)
    p.dma("sync", su16s[:, :], su16[:, :], inc=s_ld)
    p.dma("sync", iots[:, :], iot[:, 0:C], inc=s_ld)
    p.dma("sync", rsb[:, :], wrows[:, :], inc=s_ld)
    nld = s_ld.n
    p.op("vector", lambda e: e.memset(one1f[:, :], 1.0), inc=s_cn)
    p.op("vector", lambda e: e.memset(onec[:, :], 1.0), inc=s_cn)
    p.op("vector", lambda e: e.memset(yaccb[:, :], 0.0), inc=s_cn)

    p.wait("tensor", s_ld, nld)
    for tt in range(16):
        b = tt % 4
        if tt >= 4:
            p.wait("tensor", s_mte, tt - 3)
        p.op("tensor", lambda e, b=b, tt=tt: e.transpose(bkh[b][0:128, 0:NEXP],
                                                         rsb[0:NEXP, tt * 128:(tt + 1) * 128],
                                                         idfs[0:NEXP, 0:NEXP]), inc=s_mt)
        p.wait("vector", s_mt, s_mt.n)
        p.op("vector", lambda e, b=b, tt=tt: e.tensor_copy(out=mtok[:, tt * NEXP:(tt + 1) * NEXP],
                                                           in_=bkh[b][0:128, 0:NEXP]), inc=s_mte)

    p.wait("tensor", s_mte, 16)
    for c in range(NCH):
        slot = c % 2
        if c >= 2:
            p.wait("sync", s_xt, 16 * (c - 1))
        p.dma("sync", xnb[slot][:, :], xnin[c * 128:(c + 1) * 128, :], inc=s_xnl[slot])
        p.wait("tensor", s_xnl[slot], 16 * (c // 2 + 1))
        for tt in range(16):
            i = c * 16 + tt
            b = i % 4
            if i >= 4:
                if i % 2 == 0:
                    p.wait("tensor", s_xte, (i - 4) // 2 + 1)
                else:
                    p.wait("tensor", s_xts, (i - 4) // 2 + 1)
            p.op("tensor", lambda e, b=b, slot=slot, tt=tt: e.transpose(
                bkb[b][0:128, 0:128], xnb[slot][:, tt * 128:(tt + 1) * 128], idbs[:, :]), inc=s_xt)
            eng = "vector" if i % 2 == 0 else "scalar"
            p.wait(eng, s_xt, s_xt.n)
            if eng == "vector":
                p.op(eng, lambda e, b=b, tt=tt, c=c: e.tensor_copy(
                    out=xn_tok[:, tt * 1024 + c * 128: tt * 1024 + c * 128 + 128],
                    in_=bkb[b][0:128, 0:128]), inc=s_xte)
            else:
                p.op(eng, lambda e, b=b, tt=tt, c=c: e.activation(
                    out=xn_tok[:, tt * 1024 + c * 128: tt * 1024 + c * 128 + 128],
                    in_=bkb[b][0:128, 0:128], func=AF.Copy), inc=s_xts)
    n_xt_total = s_xt.n
    n_xte_total = s_xte.n
    n_xts_total = s_xts.n

    mtok_v = mtok.rearrange("p (t e8) -> p e8 t", e8=NEXP)

    hm_done = {}
    u_done = {}
    ye_done = {}
    sa_done = {-1: 0}
    sc_done = {-1: 0}
    tre_done = {}
    ge_done = {}

    def emit_prefix(ex):
        if ex == 0:
            p.wait("vector", s_mte, 16)
        p.op("vector", lambda e, ex=ex: e.tensor_scalar(out=mbin[:, :], in0=mtok_v[:, ex, :],
                                                        scalar1=0.0, scalar2=None, op0=ALU.is_gt),
             inc=s_pfe)
        p.wait("tensor", s_pfe, s_pfe.n)
        p.wait("tensor", s_cn, 2)
        if ex > 0:
            p.wait("tensor", s_hm, hm_done[ex - 1])
        p.op("tensor", lambda e: e.matmul(bkh[7][0:16, 0:1], mbin[:, :], onec[:, :],
                                          start=True, stop=True), inc=s_pfm)
        p.wait("vector", s_pfm, s_pfm.n)
        p.op("vector", lambda e: e.tensor_copy(out=csb[:, :], in_=bkh[7][0:16, 0:1]), inc=s_pfe)
        p.wait("tensor", s_pfe, s_pfe.n)
        p.op("tensor", lambda e: e.matmul(bkh[7][0:16, 0:1], su16s[:, :], csb[:, :],
                                          start=True, stop=True), inc=s_pfm)
        p.wait("vector", s_pfm, s_pfm.n)
        p.op("vector", lambda e: e.tensor_copy(out=offsb[:, :], in_=bkh[7][0:16, 0:1]), inc=s_pfe)
        p.wait("tensor", s_pfe, s_pfe.n)
        p.op("tensor", lambda e: e.transpose(bkh[7][0:1, 0:16], offsb[:, :], idfs[0:16, 0:16]),
             inc=s_pfm)
        p.wait("vector", s_pfm, s_pfm.n)
        p.op("vector", lambda e: e.tensor_copy(out=offr[:, :], in_=bkh[7][0:1, 0:16]), inc=s_pfe)
        p.wait("tensor", s_pfe, s_pfe.n)
        p.op("tensor", lambda e: e.matmul(bkh[7][0:128, 0:16], suts[:, :], mbin[:, :],
                                          start=True, stop=False))
        p.op("tensor", lambda e: e.matmul(bkh[7][0:128, 0:16], one1f[:, :], offr[:, :],
                                          start=False, stop=True), inc=s_pfm)
        p.wait("vector", s_pfm, s_pfm.n)
        p.op("vector", lambda e: e.tensor_copy(out=pcols[:, :], in_=bkh[7][0:128, 0:16]), inc=s_pfe)
        p.wait("vector", s_pfe, s_pfe.n)
        p.opw("vector", lambda e: e.tensor_mul(out=pm1[:, :], in0=pcols[:, :], in1=mbin[:, :]))
        p.op("vector", lambda e: e.tensor_scalar(out=pm1[:, :], in0=pm1[:, :], scalar1=1.0,
                                                 scalar2=None, op0=ALU.subtract), inc=s_pfe)

    def emit_gbuild(ex, jw):
        if ex == 0 and jw == 0:
            p.wait("vector", s_ld, nld)
            p.wait("vector", s_xt, n_xt_total)
            p.wait("vector", s_mt, 16)
        p.wait("vector", s_tr, s_tr.n)
        p.wait("vector", s_pfe, s_pfe.n)
        for tt in range(16):
            p.op("vector", lambda e, tt=tt, jw=jw: e.tensor_scalar(
                out=gbuf[:, tt * JW:(tt + 1) * JW], in0=iots[:, jw * JW:(jw + 1) * JW],
                scalar1=pm1[:, tt:tt + 1], scalar2=None, op0=ALU.is_equal), inc=s_g)

    def emit_gather(ex, jw):
        gbase_t = s_gt.n
        gbase_m = s_gm.n
        gebase = s_ge.n
        for tt in range(16):
            p.wait("tensor", s_g, s_g.n - 16 + tt + 1)
            if tt == 0 and jw == 0:
                p.wait("tensor", s_ye, ye_done.get(ex - 1, 0))
                p.wait("tensor", s_sa, sa_done[ex - 1])
                p.wait("tensor", s_pfe, s_pfe.n)
                if ex == 0:
                    p.wait("tensor", s_xte, n_xte_total)
                    p.wait("tensor", s_xts, n_xts_total)
            if tt == 0 and jw == 1:
                p.wait("tensor", s_ge, gebase)
            for k in range(NCH):
                p.op("tensor", lambda e, k=k, tt=tt, jw=jw: e.matmul(
                    bkh[k][0:128, 0:JW],
                    xn_tok[:, tt * 1024 + k * 128: tt * 1024 + k * 128 + 128],
                    gbuf[:, tt * JW:(tt + 1) * JW],
                    start=(tt == 0), stop=(tt == 15)),
                    inc=(s_gt if k == NCH - 1 else (s_gm if tt == 15 else None)))
        if dbg and ex == 1 and jw == 0:
            p.wait("scalar", s_dbg, s_dbg.n)
        for k in range(NCH):
            if k < NCH - 1:
                p.wait("scalar", s_gm, gbase_m + k + 1)
            else:
                p.wait("scalar", s_gt, gbase_t + 16)
            if ex > 0 and jw == 0 and k == 0:
                p.wait("scalar", s_sc, sc_done[ex - 1])
            p.op("scalar", lambda e, k=k, jw=jw: e.activation(
                out=xy[:, (jw * NCH + k) * JW: (jw * NCH + k + 1) * JW],
                in_=bkh[k][0:128, 0:JW], func=AF.Copy), inc=s_ge)
        ge_done[(ex, jw)] = s_ge.n
        for tt in range(16):
            p.wait("vector", s_gt, gbase_t + tt + 1)
            p.op("vector", lambda e, tt=tt, ex=ex: e.tensor_scalar(
                out=gbuf[:, tt * JW:(tt + 1) * JW], in0=gbuf[:, tt * JW:(tt + 1) * JW],
                scalar1=mtok[:, tt * NEXP + ex: tt * NEXP + ex + 1], scalar2=None, op0=ALU.mult),
                inc=s_gs)

    def emit_transposes(ex, jw):
        gsbase = s_gs.n - 16
        trbase = s_tr.n
        trebase = s_tre.n
        trsbase = s_trs.n
        p.wait("tensor", s_ge, ge_done[(ex, jw)])
        if jw == 0:
            p.wait("tensor", s_sc, sc_done[ex - 1])
        for tt in range(16):
            p.wait("tensor", s_gs, gsbase + tt + 1)
            for j3 in range(3):
                i = tt * 3 + j3
                b = i % 4
                if i >= 4:
                    if i % 2 == 0:
                        p.wait("tensor", s_tre, trebase + (i - 4) // 2 + 1)
                    else:
                        p.wait("tensor", s_trs, trsbase + (i - 4) // 2 + 1)
                p.op("tensor", lambda e, b=b, tt=tt, j3=j3: e.transpose(
                    bkb[b][0:128, 0:128], gbuf[:, tt * JW + j3 * 128: tt * JW + j3 * 128 + 128],
                    idbs[:, :]), inc=s_tr)
                eng = "vector" if i % 2 == 0 else "scalar"
                p.wait(eng, s_tr, trbase + i + 1)
                jt = jw * 3 + j3
                if eng == "vector":
                    p.op(eng, lambda e, b=b, jt=jt, tt=tt: e.tensor_copy(
                        out=gtb[:, jt * T_OWN + tt * 128: jt * T_OWN + tt * 128 + 128],
                        in_=bkb[b][0:128, 0:128]), inc=s_tre)
                else:
                    p.op(eng, lambda e, b=b, jt=jt, tt=tt: e.activation(
                        out=gtb[:, jt * T_OWN + tt * 128: jt * T_OWN + tt * 128 + 128],
                        in_=bkb[b][0:128, 0:128], func=AF.Copy), inc=s_trs)
        tre_done[(ex, jw)] = (s_tre.n, s_trs.n)

    def emit_phase1(ex):
        ubase = s_u.n
        silbase = s_sil.n
        hmbase = s_hm.n
        for fg in range(NFG):
            slot = fg % 2
            g_glob = ex * NFG + fg
            if g_glob >= 2:
                p.wait("gpsimd", s_u, (g_glob - 1) * FG * 2 * 2)
            p.dma("gpsimd", w0b[slot].rearrange("p (f c) -> p f c", f=FG),
                  w0t[ex, fg * FG:(fg + 1) * FG, :, :].rearrange("f p c -> p f c"), inc=s_w0[slot])
            p.dma("gpsimd", w1b[slot].rearrange("p (f c) -> p f c", f=FG),
                  w1t[ex, fg * FG:(fg + 1) * FG, :, :].rearrange("f p c -> p f c"), inc=s_w1[slot])
            for fc in range(FG):
                f = fg * FG + fc
                for jw in range(2):
                    i = f * 2 + jw
                    b0, b1 = (4, 5) if i % 2 == 0 else (6, 7)
                    if fc == 0 and jw == 0:
                        p.wait("tensor", s_w0[slot], 16 * (s_w0[slot].n // 16))
                        p.wait("tensor", s_w1[slot], 16 * (s_w1[slot].n // 16))
                    if i == 0:
                        p.wait("tensor", s_ge, ge_done[(ex, 1)])
                        p.wait("tensor", s_tre, tre_done[(ex, 1)][0])
                        p.wait("tensor", s_trs, tre_done[(ex, 1)][1])
                        if ex > 0:
                            p.wait("tensor", s_ye, ye_done[ex - 1])
                            p.wait("tensor", s_sa, sa_done[ex - 1])
                    if i >= 2:
                        p.wait("tensor", s_hm, hmbase + i - 1)
                        p.wait("tensor", s_sil, silbase + i - 1)
                    for k in range(NCH):
                        p.op("tensor", lambda e, b0=b0, slot=slot, fc=fc, k=k, jw=jw: e.matmul(
                            bkh[b0][0:128, 0:JW],
                            w0b[slot][:, fc * D + k * 128: fc * D + k * 128 + 128],
                            xy[:, (jw * NCH + k) * JW: (jw * NCH + k + 1) * JW],
                            start=(k == 0), stop=(k == NCH - 1)),
                            inc=(s_u if k == NCH - 1 else None))
                    for k in range(NCH):
                        p.op("tensor", lambda e, b1=b1, slot=slot, fc=fc, k=k, jw=jw: e.matmul(
                            bkh[b1][0:128, 0:JW],
                            w1b[slot][:, fc * D + k * 128: fc * D + k * 128 + 128],
                            xy[:, (jw * NCH + k) * JW: (jw * NCH + k + 1) * JW],
                            start=(k == 0), stop=(k == NCH - 1)),
                            inc=(s_u if k == NCH - 1 else None))
                    p.wait("scalar", s_u, ubase + 2 * i + 1)
                    if i >= 2:
                        p.wait("scalar", s_hm, hmbase + i - 1)
                    elif hmbase > 0:
                        p.wait("scalar", s_hm, hmbase)
                    p.op("scalar", lambda e, b0=b0, sl2=i % 2: e.activation(
                        out=sil[sl2][:, :], in_=bkh[b0][0:128, 0:JW], func=AF.Silu), inc=s_sil)
                    p.wait("vector", s_sil, silbase + i + 1)
                    p.wait("vector", s_u, ubase + 2 * i + 2)
                    if i == 0 and ex > 0:
                        p.wait("vector", s_y, 12 * ex)
                    p.op("vector", lambda e, b1=b1, sl2=i % 2, f=f, jw=jw: e.tensor_mul(
                        out=hall[:, f * C + jw * JW: f * C + (jw + 1) * JW],
                        in0=sil[sl2][:, :], in1=bkh[b1][0:128, 0:JW]), inc=s_hm)
        u_done[ex] = s_u.n
        hm_done[ex] = s_hm.n

    def emit_phase2(ex):
        ybase = s_y.n
        p2base = s_p2.n
        yebase = s_ye.n
        for dh in range(2):
            for fg in range(NFG):
                slot = fg % 2
                gidx = dh * NFG + fg
                if gidx >= 2 and fg >= 2:
                    p.wait("gpsimd", s_p2, p2base + dh * (NFG - 1) + fg - 1)
                elif dh == 1:
                    p.wait("gpsimd", s_y, ybase + NJT)
                elif ex > 0:
                    p.wait("gpsimd", s_y, 12 * ex)
                p.dma("gpsimd", wob[slot].rearrange("q (g c) -> q g c", g=FG),
                      wo[ex, fg * FG * 128:(fg + 1) * FG * 128, dh * 512:(dh + 1) * 512]
                      .rearrange("(g q) c -> q g c", q=128), inc=s_wo[slot])
                for fc in range(FG):
                    f = fg * FG + fc
                    if fc == 0:
                        p.wait("tensor", s_wo[slot], 16 * (s_wo[slot].n // 16))
                    if gidx == 0 and fc == 0:
                        p.wait("tensor", s_hm, hm_done[ex])
                        p.wait("tensor", s_tre, tre_done[(ex, 1)][0])
                        p.wait("tensor", s_trs, tre_done[(ex, 1)][1])
                        p.wait("tensor", s_ge, ge_done[(ex, 1)])
                    if dh == 1 and fg == 0 and fc == 0:
                        p.wait("tensor", s_ye, yebase + 6)
                    for jt in range(NJT):
                        last = (fc == FG - 1 and jt == NJT - 1 and f != NDH - 1)
                        p.op("tensor", lambda e, jt=jt, slot=slot, fc=fc, f=f: e.matmul(
                            bkh[jt][0:128, 0:512],
                            hall[:, f * C + jt * 128: f * C + jt * 128 + 128],
                            wob[slot][:, fc * 512:(fc + 1) * 512],
                            start=(f == 0), stop=(f == NDH - 1)),
                            inc=(s_y if f == NDH - 1 else (s_p2 if last else None)))
            for jt in range(NJT):
                p.wait("scalar", s_y, ybase + dh * NJT + jt + 1)
                if dh == 0 and jt == 0:
                    p.wait("scalar", s_u, u_done[ex])
                p.op("scalar", lambda e, jt=jt, dh=dh: e.activation(
                    out=xy[:, jt * 1024 + dh * 512: jt * 1024 + dh * 512 + 512],
                    in_=bkh[jt][0:128, 0:512], func=AF.Copy), inc=s_ye)
        ye_done[ex] = s_ye.n

    def emit_scatter(ex):
        scbase = s_sc.n
        sabase = s_sa.n
        for w in range(4):
            for k in range(NCH):
                i = w * NCH + k
                b = 6 + (i % 2)
                if i == 0 and w == 0:
                    p.wait("tensor", s_ye, ye_done[ex])
                    p.wait("tensor", s_pfe, s_pfe.n)
                if i >= 2:
                    p.wait("tensor", s_sa, sabase + i - 1)
                for jt in range(NJT):
                    p.op("tensor", lambda e, b=b, jt=jt, k=k, w=w: e.matmul(
                        bkh[b][0:128, 0:512],
                        xy[:, jt * 1024 + k * 128: jt * 1024 + k * 128 + 128],
                        gtb[:, jt * T_OWN + w * 512: jt * T_OWN + w * 512 + 512],
                        start=(jt == 0), stop=(jt == NJT - 1)),
                        inc=(s_sc if jt == NJT - 1 else None))
                p.wait("vector", s_sc, scbase + i + 1)
                if ex == 0 and i == 0:
                    p.wait("vector", s_cn, 1)
                p.op("vector", lambda e, b=b, k=k, w=w: e.tensor_add(
                    out=yaccb[:, k * T_OWN + w * 512: k * T_OWN + w * 512 + 512],
                    in0=yaccb[:, k * T_OWN + w * 512: k * T_OWN + w * 512 + 512],
                    in1=bkh[b][0:128, 0:512]), inc=s_sa)
        sa_done[ex] = s_sa.n
        sc_done[ex] = s_sc.n

    dbg = {k: dram[k] for k in ("dbg_pm1", "dbg_xg", "dbg_gt", "dbg_yg", "dbg_yacc",
                                "dbg_mtok", "dbg_mbin", "dbg_pcols", "dbg_xntok", "dbg_hall")
           if k in dram}
    s_dbg = p.sem("s_dbg") if dbg else None
    emit_prefix(0)
    if dbg:
        p.wait("gpsimd", s_pfe, s_pfe.n)
        p.dma("gpsimd", dbg["dbg_pm1"][:, :], pm1[:, :], inc=s_dbg)
        p.dma("gpsimd", dbg["dbg_mtok"][:, :], mtok[:, :], inc=s_dbg)
        p.dma("gpsimd", dbg["dbg_mbin"][:, :], mbin[:, :], inc=s_dbg)
        p.dma("gpsimd", dbg["dbg_pcols"][:, :], pcols[:, :], inc=s_dbg)
        p.wait("gpsimd", s_xte, n_xte_total)
        p.wait("gpsimd", s_xts, n_xts_total)
        p.dma("gpsimd", dbg["dbg_xntok"][:, :], xn_tok[:, :], inc=s_dbg)
    emit_gbuild(0, 0)
    for ex in range(NEXP):
        emit_gather(ex, 0)
        emit_transposes(ex, 0)
        emit_gbuild(ex, 1)
        emit_gather(ex, 1)
        emit_transposes(ex, 1)
        if dbg and ex == 0:
            p.wait("gpsimd", s_ge, ge_done[(0, 1)])
            p.dma("gpsimd", dbg["dbg_xg"][:, :], xy[:, :], inc=s_dbg)
            p.wait("gpsimd", s_tre, tre_done[(0, 1)][0])
            p.wait("gpsimd", s_trs, tre_done[(0, 1)][1])
            p.dma("gpsimd", dbg["dbg_gt"][:, :], gtb[:, :], inc=s_dbg)
            p.wait("gpsimd", s_dbg, s_dbg.n)
        emit_phase1(ex)
        if dbg and ex == 0:
            p.wait("gpsimd", s_hm, hm_done[0])
            p.dma("gpsimd", dbg["dbg_hall"][:, :], hall[:, :], inc=s_dbg)
            p.wait("gpsimd", s_dbg, s_dbg.n)
        if ex < NEXP - 1:
            emit_prefix(ex + 1)
            emit_gbuild(ex + 1, 0)
        emit_phase2(ex)
        if dbg and ex == 0:
            p.wait("gpsimd", s_ye, ye_done[0])
            p.dma("gpsimd", dbg["dbg_yg"][:, :], xy[:, :], inc=s_dbg)
            p.wait("gpsimd", s_dbg, s_dbg.n)
        emit_scatter(ex)
    if dbg:
        p.wait("gpsimd", s_sa, s_sa.n)
        p.dma("gpsimd", dbg["dbg_yacc"][:, :], yaccb[:, :], inc=s_dbg)
        p.wait("gpsimd", s_dbg, s_dbg.n)

    n_sll = [0, 0]
    n_xll = [0, 0]
    n_cmb = 0
    tr_total = s_tr.n
    sa_total = s_sa.n
    for w in range(4):
        for o in range(NCH):
            i = w * NCH + o
            sl = i % 2
            if i == 0:
                p.wait("sync", s_tr, tr_total)
                p.wait("sync", s_xt, n_xt_total)
                p.wait("sync", s_mt, 16)
            if n_sll[sl] >= 1:
                p.wait("sync", s_cmb, 2 * (i - 2) + 1)
            p.dma("sync", ssb[sl][:, :],
                  sred[o * 128:(o + 1) * 128, w * 512:(w + 1) * 512], inc=s_sl[sl])
            n_sll[sl] += 1
            if i == 0:
                p.wait("gpsimd", s_tr, tr_total)
                p.wait("gpsimd", s_xt, n_xt_total)
                p.wait("gpsimd", s_mt, 16)
            if n_xll[sl] >= 1:
                p.wait("gpsimd", s_cmb, 2 * (i - 2) + 2)
            p.dma("gpsimd", xst[sl][:, :],
                  xT[o * 128:(o + 1) * 128, w * 512:(w + 1) * 512], inc=s_xl[sl])
            n_xll[sl] += 1
            if i == 0:
                p.wait("scalar", s_sa, sa_total)
            if i >= 2:
                p.wait("scalar", s_cmb, 2 * (i - 2) + 1)
            p.op("scalar", lambda e, o=o, w=w, sl=sl: e.activation(
                out=gsb[sl][:, :], in_=yaccb[:, o * T_OWN + w * 512: o * T_OWN + w * 512 + 512],
                func=AF.Silu), inc=s_gsb)
            p.wait("vector", s_sl[sl], 16 * n_sll[sl])
            p.wait("vector", s_xl[sl], 16 * n_xll[sl])
            p.wait("vector", s_gsb, i + 1)
            if n_cmb >= 1:
                p.wait("vector", s_cmb, n_cmb)
            csl = (n_cmb // 2) % 2
            if n_cmb >= 4:
                p.wait("vector", s_out[csl], 16 * (n_cmb // 4))
            p.op("vector", lambda e, sl=sl: e.tensor_mul(out=tmul[:, :], in0=ssb[sl][:, :],
                                                         in1=gsb[sl][:, :]), inc=s_cmb)
            n_cmb += 1
            p.wait("vector", s_cmb, n_cmb)
            p.op("vector", lambda e, sl=sl, csl=csl: e.tensor_add(out=x1s[csl][:, :],
                                                                  in0=xst[sl][:, :], in1=tmul[:, :]),
                 inc=s_cmb)
            n_cmb += 1
            p.wait("gpsimd", s_cmb, n_cmb)
            p.dma("gpsimd", x2[o * 128:(o + 1) * 128, w * 512:(w + 1) * 512], x1s[csl][:, :],
                  inc=s_out[csl])

    p.wait("gpsimd", s_out[0], s_out[0].n)
    p.wait("gpsimd", s_out[1], s_out[1].n)


C0 = 512
JW0 = 512
NJT0 = C0 // 128


def _emit_tail0s(nc, p, B, **dram):
    xT = dram['xT']
    xnin = dram['xnin']
    sred = dram['sred']
    mrows = dram['mrows']
    w1t = dram['w1t']
    b1c = dram['b1c']
    w2 = dram['w2']
    b2 = dram['b2']
    boc = dram['boc']
    x1 = dram['x1']
    sut = dram['sut']
    su16 = dram['su16']
    iot = dram['iot']
    idf = dram['idf']
    idb = dram['idb']
    C, JW, NJT = C0, JW0, NJT0
    NFG = NDH // FG

    idbs = B.sb("idbs", [128, 128], BF16)
    idfs = B.sb("idfs", [128, 128], F32)
    suts = B.sb("suts", [128, 128], F32)
    su16s = B.sb("su16s", [16, 16], F32)
    iots = B.sb("iots", [128, C], F32)
    one1f = B.sb("one1f", [1, 128], F32)
    onec = B.sb("onec", [128, 1], F32)
    mtok = B.sb("mtok", [128, 16 * NEXP], F32)
    mbin = B.sb("mbin", [128, 16], F32)
    pcols = B.sb("pcols", [128, 16], F32)
    pm1 = B.sb("pm1", [128, 16], F32)
    csb = B.sb("csb", [16, 1], F32)
    offsb = B.sb("offsb", [16, 1], F32)
    offr = B.sb("offr", [1, 16], F32)
    b1s = B.sb("b1s", [128, NDH], F32)
    rb1 = B.sb("rb1", [128, NDH], F32)
    rb1b = B.sb("rb1b", [128, NDH], BF16)
    b2s = B.sb("b2s", [NEXP, D], F32)
    sb2 = B.sb("sb2", [128, NCH], F32)
    bos = B.sb("bos", [128, NCH], F32)
    bias8 = B.sb("bias8", [128, NCH], F32)
    ctsb = B.sb("ctsb", [1, D], F32)
    ctc = B.sb("ctc", [128, NCH], F32)
    sil = [B.sb(f"sil{j}", [128, JW], BF16) for j in range(2)]
    u0 = B.mark()
    rsb = B.sb("rsb", [NEXP, T_OWN], F32)
    xnb = [B.sb(f"xnb{j}", [128, T_OWN], BF16) for j in range(2)]
    u_end = B.mark()
    B.rewind(u0)
    gbuf = B.sb("gbuf", [128, 16 * JW], BF16)
    u_end = max(u_end, B.mark())
    B.rewind(u0)
    sst = [B.sb(f"sst{j}", [128, 512], F32) for j in range(2)]
    ssb = [B.sb(f"ssb{j}", [128, 512], F32) for j in range(2)]
    gsb = [B.sb(f"gsb{j}", [128, 512], BF16) for j in range(2)]
    xst = [B.sb(f"xst{j}", [128, 512], F32) for j in range(2)]
    tmul = B.sb("tmul", [128, 512], F32)
    x1s = [B.sb(f"x1s{j}", [128, 512], F32) for j in range(2)]
    u_end = max(u_end, B.mark())
    B.seek(u_end)
    gtb = B.sb("gtb", [128, NJT * T_OWN], BF16)
    xy = B.sb("xy", [128, NCH * JW], BF16)
    hall = B.sb("hall", [128, NDH * C], BF16)
    w1b = [B.sb(f"w1b{j}", [128, FG * D], BF16) for j in range(2)]
    wob = [B.sb(f"wob{j}", [128, FG * 512], BF16) for j in range(2)]
    yaccb = B.sb("yaccb", [128, NCH * T_OWN], BF16)
    xn_tok = B.sb("xn_tok", [128, 16 * NCH * 128], BF16)

    bkh = B.banks8("bk")
    bkb = [h.bitcast(BF16) for h in bkh]

    s_ld = p.sem("s_ld")
    s_cn = p.sem("s_ic")
    s_xnl = [p.sem("s_xw0"), p.sem("s_xw1")]
    s_mt = p.sem("s_rt")
    s_mte = p.sem("s_rc")
    s_xt = p.sem("s_pp")
    s_xte = p.sem("s_qt")
    s_xts = p.sem("s_h")
    s_pfm = p.sem("s_mm")
    s_pfe = p.sem("s_ev")
    s_g = p.sem("s_lg")
    s_gt = p.sem("s_rope")
    s_gm = p.sem("s_pj")
    s_ge = p.sem("s_ge")
    s_tr = p.sem("s_smm")
    s_tre = p.sem("s_exp")
    s_trs = p.sem("s_msk")
    s_w1 = [p.sem("s_w10"), p.sem("s_w11")]
    s_wo = [p.sem("s_fx0"), p.sem("s_fx1")]
    s_b1 = p.sem("s_b10")
    s_u = p.sem("s_u")
    s_sil = p.sem("s_v")
    s_hm = p.sem("s_pvm")
    s_y = p.sem("s_y")
    s_p2 = p.sem("s_tm")
    s_ye = p.sem("s_ue")
    s_sc = p.sem("s_di")
    s_sa = p.sem("s_dr")
    s_ct = p.sem("s_sb2")
    s_sl = [p.sem("s_at0"), p.sem("s_at1")]
    s_xl = [p.sem("s_xl0"), p.sem("s_xl1")]
    s_gsb = p.sem("s_an")
    s_cmb = p.sem("s_cmb")
    s_out = [p.sem("s_out0"), p.sem("s_out1")]
    ctd = dram['ctd']

    p.dma("sync", idbs[:, :], idb[:, :], inc=s_ld)
    p.dma("sync", idfs[:, :], idf[:, :], inc=s_ld)
    p.dma("sync", suts[:, :], sut[:, :], inc=s_ld)
    p.dma("sync", su16s[:, :], su16[:, :], inc=s_ld)
    p.dma("sync", iots[:, :], iot[:, 0:C], inc=s_ld)
    p.dma("sync", rsb[:, :], mrows[:, :], inc=s_ld)
    p.dma("sync", b2s[:, :], b2[:, :], inc=s_ld)
    p.dma("sync", bos[:, :], boc[:, :], inc=s_ld)
    nld = s_ld.n
    p.op("vector", lambda e: e.memset(one1f[:, :], 1.0), inc=s_cn)
    p.op("vector", lambda e: e.memset(onec[:, :], 1.0), inc=s_cn)
    p.op("vector", lambda e: e.memset(yaccb[:, :], 0.0), inc=s_cn)
    p.op("vector", lambda e: e.memset(ctsb[:, :], 0.0), inc=s_cn)

    p.wait("tensor", s_ld, nld)
    p.wait("tensor", s_cn, 2)
    for c in range(NCH):
        if c >= 1:
            p.wait("tensor", s_pfe, c)
        p.op("tensor", lambda e, c=c: e.matmul(bkh[6][0:128, 0:1], b2s[:, c * 128:(c + 1) * 128],
                                               onec[0:NEXP, 0:1], start=True, stop=True), inc=s_pfm)
        p.wait("vector", s_pfm, s_pfm.n)
        p.op("vector", lambda e, c=c: e.tensor_copy(out=sb2[:, c:c + 1], in_=bkh[6][0:128, 0:1]),
             inc=s_pfe)

    for tt in range(16):
        b = tt % 4
        if tt >= 4:
            p.wait("tensor", s_mte, tt - 3)
        p.op("tensor", lambda e, b=b, tt=tt: e.transpose(bkh[b][0:128, 0:NEXP],
                                                         rsb[0:NEXP, tt * 128:(tt + 1) * 128],
                                                         idfs[0:NEXP, 0:NEXP]), inc=s_mt)
        p.wait("vector", s_mt, s_mt.n)
        p.op("vector", lambda e, b=b, tt=tt: e.tensor_copy(out=mtok[:, tt * NEXP:(tt + 1) * NEXP],
                                                           in_=bkh[b][0:128, 0:NEXP]), inc=s_mte)

    p.wait("tensor", s_mte, 16)
    for c in range(NCH):
        slot = c % 2
        if c >= 2:
            p.wait("sync", s_xt, 16 * (c - 1))
        p.dma("sync", xnb[slot][:, :], xnin[c * 128:(c + 1) * 128, :], inc=s_xnl[slot])
        p.wait("tensor", s_xnl[slot], 16 * (c // 2 + 1))
        for tt in range(16):
            i = c * 16 + tt
            b = i % 4
            if i >= 4:
                if i % 2 == 0:
                    p.wait("tensor", s_xte, (i - 4) // 2 + 1)
                else:
                    p.wait("tensor", s_xts, (i - 4) // 2 + 1)
            p.op("tensor", lambda e, b=b, slot=slot, tt=tt: e.transpose(
                bkb[b][0:128, 0:128], xnb[slot][:, tt * 128:(tt + 1) * 128], idbs[:, :]), inc=s_xt)
            eng = "vector" if i % 2 == 0 else "scalar"
            p.wait(eng, s_xt, s_xt.n)
            if eng == "vector":
                p.op(eng, lambda e, b=b, tt=tt, c=c: e.tensor_copy(
                    out=xn_tok[:, tt * 1024 + c * 128: tt * 1024 + c * 128 + 128],
                    in_=bkb[b][0:128, 0:128]), inc=s_xte)
            else:
                p.op(eng, lambda e, b=b, tt=tt, c=c: e.activation(
                    out=xn_tok[:, tt * 1024 + c * 128: tt * 1024 + c * 128 + 128],
                    in_=bkb[b][0:128, 0:128], func=AF.Copy), inc=s_xts)
    n_xt_total = s_xt.n
    n_xte_total = s_xte.n
    n_xts_total = s_xts.n

    mtok_v = mtok.rearrange("p (t e8) -> p e8 t", e8=NEXP)

    hm_done = {}
    u_done = {}
    ye_done = {}
    sa_done = {-1: 0}
    sc_done = {-1: 0}
    tre_done = {}
    ge_done = {}

    def emit_prefix(ex):
        if ex == 0:
            p.wait("vector", s_mte, 16)
        p.op("vector", lambda e, ex=ex: e.tensor_scalar(out=mbin[:, :], in0=mtok_v[:, ex, :],
                                                        scalar1=0.0, scalar2=None, op0=ALU.is_gt),
             inc=s_pfe)
        p.wait("tensor", s_pfe, s_pfe.n)
        p.wait("tensor", s_cn, 2)
        if ex > 0:
            p.wait("tensor", s_hm, hm_done[ex - 1])
        p.op("tensor", lambda e: e.matmul(bkh[7][0:16, 0:1], mbin[:, :], onec[:, :],
                                          start=True, stop=True), inc=s_pfm)
        p.wait("vector", s_pfm, s_pfm.n)
        p.op("vector", lambda e: e.tensor_copy(out=csb[:, :], in_=bkh[7][0:16, 0:1]), inc=s_pfe)
        p.wait("tensor", s_pfe, s_pfe.n)
        p.op("tensor", lambda e: e.matmul(bkh[7][0:16, 0:1], su16s[:, :], csb[:, :],
                                          start=True, stop=True), inc=s_pfm)
        p.wait("vector", s_pfm, s_pfm.n)
        p.op("vector", lambda e: e.tensor_copy(out=offsb[:, :], in_=bkh[7][0:16, 0:1]), inc=s_pfe)
        p.wait("tensor", s_pfe, s_pfe.n)
        p.op("tensor", lambda e: e.transpose(bkh[7][0:1, 0:16], offsb[:, :], idfs[0:16, 0:16]),
             inc=s_pfm)
        p.wait("vector", s_pfm, s_pfm.n)
        p.op("vector", lambda e: e.tensor_copy(out=offr[:, :], in_=bkh[7][0:1, 0:16]), inc=s_pfe)
        p.wait("tensor", s_pfe, s_pfe.n)
        p.op("tensor", lambda e: e.matmul(bkh[7][0:128, 0:16], suts[:, :], mbin[:, :],
                                          start=True, stop=False))
        p.op("tensor", lambda e: e.matmul(bkh[7][0:128, 0:16], one1f[:, :], offr[:, :],
                                          start=False, stop=True), inc=s_pfm)
        p.wait("vector", s_pfm, s_pfm.n)
        p.op("vector", lambda e: e.tensor_copy(out=pcols[:, :], in_=bkh[7][0:128, 0:16]), inc=s_pfe)
        p.wait("vector", s_pfe, s_pfe.n)
        p.opw("vector", lambda e: e.tensor_mul(out=pm1[:, :], in0=pcols[:, :], in1=mbin[:, :]))
        p.op("vector", lambda e: e.tensor_scalar(out=pm1[:, :], in0=pm1[:, :], scalar1=1.0,
                                                 scalar2=None, op0=ALU.subtract), inc=s_pfe)

    def emit_gbuild(ex):
        if ex == 0:
            p.wait("vector", s_ld, nld)
            p.wait("vector", s_xt, n_xt_total)
            p.wait("vector", s_mt, 16)
        p.wait("vector", s_tr, s_tr.n)
        p.wait("vector", s_pfe, s_pfe.n)
        for tt in range(16):
            p.op("vector", lambda e, tt=tt: e.tensor_scalar(
                out=gbuf[:, tt * JW:(tt + 1) * JW], in0=iots[:, 0:JW],
                scalar1=pm1[:, tt:tt + 1], scalar2=None, op0=ALU.is_equal), inc=s_g)

    def emit_gather(ex):
        gbase_t = s_gt.n
        gbase_m = s_gm.n
        for tt in range(16):
            p.wait("tensor", s_g, s_g.n - 16 + tt + 1)
            if tt == 0:
                p.wait("tensor", s_ye, ye_done.get(ex - 1, 0))
                p.wait("tensor", s_sa, sa_done[ex - 1])
                p.wait("tensor", s_ct, 2 * ex)
                p.wait("tensor", s_pfe, s_pfe.n)
                if ex == 0:
                    p.wait("tensor", s_xte, n_xte_total)
                    p.wait("tensor", s_xts, n_xts_total)
            for k in range(NCH):
                p.op("tensor", lambda e, k=k, tt=tt: e.matmul(
                    bkh[k][0:128, 0:JW],
                    xn_tok[:, tt * 1024 + k * 128: tt * 1024 + k * 128 + 128],
                    gbuf[:, tt * JW:(tt + 1) * JW],
                    start=(tt == 0), stop=(tt == 15)),
                    inc=(s_gt if k == NCH - 1 else (s_gm if tt == 15 else None)))
        for k in range(NCH):
            if k < NCH - 1:
                p.wait("scalar", s_gm, gbase_m + k + 1)
            else:
                p.wait("scalar", s_gt, gbase_t + 16)
            if ex > 0 and k == 0:
                p.wait("scalar", s_sc, sc_done[ex - 1])
            p.op("scalar", lambda e, k=k: e.activation(
                out=xy[:, k * JW: (k + 1) * JW],
                in_=bkh[k][0:128, 0:JW], func=AF.Copy), inc=s_ge)
        ge_done[ex] = s_ge.n

    def emit_transposes(ex):
        trbase = s_tr.n
        trebase = s_tre.n
        trsbase = s_trs.n
        p.wait("tensor", s_ge, ge_done[ex])
        p.wait("tensor", s_sc, sc_done[ex - 1])
        for tt in range(16):
            for j3 in range(NJT):
                i = tt * NJT + j3
                b = i % 4
                if i >= 4:
                    if i % 2 == 0:
                        p.wait("tensor", s_tre, trebase + (i - 4) // 2 + 1)
                    else:
                        p.wait("tensor", s_trs, trsbase + (i - 4) // 2 + 1)
                p.op("tensor", lambda e, b=b, tt=tt, j3=j3: e.transpose(
                    bkb[b][0:128, 0:128], gbuf[:, tt * JW + j3 * 128: tt * JW + j3 * 128 + 128],
                    idbs[:, :]), inc=s_tr)
                eng = "vector" if i % 2 == 0 else "scalar"
                p.wait(eng, s_tr, trbase + i + 1)
                if eng == "vector":
                    p.op(eng, lambda e, b=b, jt=j3, tt=tt: e.tensor_copy(
                        out=gtb[:, jt * T_OWN + tt * 128: jt * T_OWN + tt * 128 + 128],
                        in_=bkb[b][0:128, 0:128]), inc=s_tre)
                else:
                    p.op(eng, lambda e, b=b, jt=j3, tt=tt: e.activation(
                        out=gtb[:, jt * T_OWN + tt * 128: jt * T_OWN + tt * 128 + 128],
                        in_=bkb[b][0:128, 0:128], func=AF.Copy), inc=s_trs)
        tre_done[ex] = (s_tre.n, s_trs.n)

    def emit_phase1(ex):
        ubase = s_u.n
        silbase = s_sil.n
        hmbase = s_hm.n
        p.dma("scalar", b1s[:, :], b1c[ex, :, :], inc=s_b1)
        p.wait("scalar", s_b1, 16 * (ex + 1))
        if ex > 0:
            p.wait("scalar", s_hm, hmbase)
        p.op("scalar", lambda e: e.activation(out=rb1[:, :], in_=b1s[:, :], func=AF.Relu), inc=s_sil)
        p.wait("vector", s_sil, s_sil.n)
        if ex > 0:
            p.wait("vector", s_ct, 2 * ex)
            p.wait("vector", s_y, 8 * ex)
        p.op("vector", lambda e: e.tensor_copy(out=rb1b[:, :], in_=rb1[:, :]), inc=s_sil)
        silbase = s_sil.n
        for fg in range(NFG):
            slot = fg % 2
            g_glob = ex * NFG + fg
            if g_glob >= 2:
                p.wait("gpsimd", s_u, (g_glob - 1) * FG)
            p.dma("gpsimd", w1b[slot].rearrange("p (f c) -> p f c", f=FG),
                  w1t[ex, fg * FG:(fg + 1) * FG, :, :].rearrange("f p c -> p f c"), inc=s_w1[slot])
            for fc in range(FG):
                f = fg * FG + fc
                i = f
                b0 = 4 + (i % 2)
                if fc == 0:
                    p.wait("tensor", s_w1[slot], 16 * (s_w1[slot].n // 16))
                if i == 0:
                    p.wait("tensor", s_ge, ge_done[ex])
                    p.wait("tensor", s_tre, tre_done[ex][0])
                    p.wait("tensor", s_trs, tre_done[ex][1])
                    p.wait("tensor", s_sil, silbase)
                    if ex > 0:
                        p.wait("tensor", s_ye, ye_done[ex - 1])
                        p.wait("tensor", s_sa, sa_done[ex - 1])
                if i >= 2:
                    p.wait("tensor", s_hm, hmbase + i - 1)
                    p.wait("tensor", s_sil, silbase + i - 1)
                for k in range(NCH):
                    p.op("tensor", lambda e, b0=b0, slot=slot, fc=fc, k=k: e.matmul(
                        bkh[b0][0:128, 0:JW],
                        w1b[slot][:, fc * D + k * 128: fc * D + k * 128 + 128],
                        xy[:, k * JW: (k + 1) * JW],
                        start=(k == 0), stop=(k == NCH - 1)),
                        inc=(s_u if k == NCH - 1 else None))
                p.wait("scalar", s_u, ubase + i + 1)
                if i >= 2:
                    p.wait("scalar", s_hm, hmbase + i - 1)
                p.op("scalar", lambda e, b0=b0, sl2=i % 2, f=f: e.activation(
                    out=sil[sl2][:, :], in_=bkh[b0][0:128, 0:JW], func=AF.Relu,
                    bias=b1s[:, f:f + 1]), inc=s_sil)
                p.wait("vector", s_sil, silbase + i + 1)
                if i == 0 and ex > 0:
                    p.wait("vector", s_y, 8 * ex)
                p.op("vector", lambda e, sl2=i % 2, f=f: e.tensor_scalar(
                    out=hall[:, f * C: (f + 1) * C], in0=sil[sl2][:, :],
                    scalar1=rb1[:, f:f + 1], scalar2=None, op0=ALU.subtract), inc=s_hm)
        u_done[ex] = s_u.n
        hm_done[ex] = s_hm.n

    def emit_phase2(ex):
        ybase = s_y.n
        p2base = s_p2.n
        yebase = s_ye.n
        for dh in range(2):
            for fg in range(NFG):
                slot = fg % 2
                gidx = dh * NFG + fg
                if gidx >= 2 and fg >= 2:
                    p.wait("gpsimd", s_p2, p2base + dh * (NFG - 1) + fg - 1)
                elif dh == 1:
                    p.wait("gpsimd", s_y, ybase + NJT)
                elif ex > 0:
                    p.wait("gpsimd", s_y, 8 * ex)
                p.dma("gpsimd", wob[slot].rearrange("q (g c) -> q g c", g=FG),
                      w2[ex, fg * FG * 128:(fg + 1) * FG * 128, dh * 512:(dh + 1) * 512]
                      .rearrange("(g q) c -> q g c", q=128), inc=s_wo[slot])
                for fc in range(FG):
                    f = fg * FG + fc
                    if fc == 0:
                        p.wait("tensor", s_wo[slot], 16 * (s_wo[slot].n // 16))
                    if gidx == 0 and fc == 0:
                        p.wait("tensor", s_hm, hm_done[ex])
                        p.wait("tensor", s_tre, tre_done[ex][0])
                        p.wait("tensor", s_trs, tre_done[ex][1])
                        p.wait("tensor", s_ge, ge_done[ex])
                    if dh == 1 and fg == 0 and fc == 0:
                        p.wait("tensor", s_ye, yebase + NJT)
                        p.wait("tensor", s_pfe, s_pfe.n)
                    for jt in range(NJT):
                        last = (fc == FG - 1 and jt == NJT - 1 and f != NDH - 1)
                        p.op("tensor", lambda e, jt=jt, slot=slot, fc=fc, f=f: e.matmul(
                            bkh[jt][0:128, 0:512],
                            hall[:, f * C + jt * 128: f * C + jt * 128 + 128],
                            wob[slot][:, fc * 512:(fc + 1) * 512],
                            start=(f == 0), stop=(f == NDH - 1)),
                            inc=(s_y if f == NDH - 1 else (s_p2 if last else None)))
                    p.op("tensor", lambda e, slot=slot, fc=fc, f=f, dh=dh: e.matmul(
                        bkh[6][0:1, 0:512], rb1b[:, f:f + 1],
                        wob[slot][:, fc * 512:(fc + 1) * 512],
                        start=(f == 0), stop=(f == NDH - 1)),
                        inc=(s_ct if f == NDH - 1 else None))
            for jt in range(NJT):
                p.wait("scalar", s_y, ybase + dh * NJT + jt + 1)
                if dh == 0 and jt == 0:
                    p.wait("scalar", s_u, u_done[ex])
                p.op("scalar", lambda e, jt=jt, dh=dh: e.activation(
                    out=xy[:, jt * 1024 + dh * 512: jt * 1024 + dh * 512 + 512],
                    in_=bkh[jt][0:128, 0:512], func=AF.Copy), inc=s_ye)
            p.wait("vector", s_ct, 2 * ex + dh + 1)
            if ex == 0 and dh == 0:
                p.wait("vector", s_cn, 4)
            p.op("vector", lambda e, dh=dh: e.tensor_add(
                out=ctsb[0:1, dh * 512:(dh + 1) * 512], in0=ctsb[0:1, dh * 512:(dh + 1) * 512],
                in1=bkh[6][0:1, 0:512]), inc=s_pfe)
        ye_done[ex] = s_ye.n

    def emit_scatter(ex):
        scbase = s_sc.n
        sabase = s_sa.n
        for w in range(4):
            for k in range(NCH):
                i = w * NCH + k
                b = 4 + (i % 2)
                if i == 0 and w == 0:
                    p.wait("tensor", s_ye, ye_done[ex])
                    p.wait("tensor", s_hm, hm_done[ex])
                    p.wait("tensor", s_pfe, s_pfe.n)
                if i >= 2:
                    p.wait("tensor", s_sa, sabase + i - 1)
                for jt in range(NJT):
                    p.op("tensor", lambda e, b=b, jt=jt, k=k, w=w: e.matmul(
                        bkh[b][0:128, 0:512],
                        xy[:, jt * 1024 + k * 128: jt * 1024 + k * 128 + 128],
                        gtb[:, jt * T_OWN + w * 512: jt * T_OWN + w * 512 + 512],
                        start=(jt == 0), stop=(jt == NJT - 1)),
                        inc=(s_sc if jt == NJT - 1 else None))
                p.wait("vector", s_sc, scbase + i + 1)
                if ex == 0 and i == 0:
                    p.wait("vector", s_cn, 3)
                p.op("vector", lambda e, b=b, k=k, w=w: e.tensor_add(
                    out=yaccb[:, k * T_OWN + w * 512: k * T_OWN + w * 512 + 512],
                    in0=yaccb[:, k * T_OWN + w * 512: k * T_OWN + w * 512 + 512],
                    in1=bkh[b][0:128, 0:512]), inc=s_sa)
        sa_done[ex] = s_sa.n
        sc_done[ex] = s_sc.n

    emit_prefix(0)
    emit_gbuild(0)
    for ex in range(NEXP):
        emit_gather(ex)
        emit_transposes(ex)
        emit_phase1(ex)
        if ex < NEXP - 1:
            emit_prefix(ex + 1)
            emit_gbuild(ex + 1)
        emit_phase2(ex)
        emit_scatter(ex)

    ct_total = s_pfe.n
    p.wait("gpsimd", s_pfe, ct_total)
    p.dma("gpsimd", ctd[0:1, :], ctsb[0:1, :], inc=s_ct)
    p.wait("gpsimd", s_ct, s_ct.n)
    p.op("gpsimd", lambda e: e.dma_start(out=ctc[:, :],
                                         in_=ctd.rearrange("a (o q) -> q (a o)", q=128),
                                         allow_slow_non_contiguous=True), inc=s_ct, inc_by=16)
    nct = s_ct.n
    p.wait("vector", s_ct, nct)
    p.op("vector", lambda e: e.tensor_add(out=bias8[:, :], in0=sb2[:, :], in1=ctc[:, :]), inc=s_gsb)
    nbias = s_gsb.n

    n_sll = [0, 0]
    n_xll = [0, 0]
    n_cmb = 0
    tr_total = s_tr.n
    sa_total = s_sa.n
    for w in range(4):
        for o in range(NCH):
            i = w * NCH + o
            sl = i % 2
            if i == 0:
                p.wait("sync", s_tr, tr_total)
                p.wait("sync", s_xt, n_xt_total)
                p.wait("sync", s_mt, 16)
            if n_sll[sl] >= 1:
                p.wait("sync", s_cmb, 3 * (i - 2) + 2)
            p.dma("sync", sst[sl][:, :],
                  sred[o * 128:(o + 1) * 128, w * 512:(w + 1) * 512], inc=s_sl[sl])
            n_sll[sl] += 1
            if i == 0:
                p.wait("gpsimd", s_tr, tr_total)
                p.wait("gpsimd", s_xt, n_xt_total)
                p.wait("gpsimd", s_mt, 16)
            if n_xll[sl] >= 1:
                p.wait("gpsimd", s_cmb, 3 * (i - 2) + 3)
            p.dma("gpsimd", xst[sl][:, :],
                  xT[o * 128:(o + 1) * 128, w * 512:(w + 1) * 512], inc=s_xl[sl])
            n_xll[sl] += 1
            if i == 0:
                p.wait("scalar", s_sa, sa_total)
                p.wait("scalar", s_gsb, nbias)
            if i >= 2:
                p.wait("scalar", s_cmb, 3 * (i - 2) + 2)
            p.op("scalar", lambda e, o=o, w=w, sl=sl: e.activation(
                out=gsb[sl][:, :], in_=yaccb[:, o * T_OWN + w * 512: o * T_OWN + w * 512 + 512],
                func=AF.Silu, bias=bias8[:, o:o + 1]), inc=s_gsb)
            p.wait("vector", s_sl[sl], 16 * n_sll[sl])
            p.wait("vector", s_xl[sl], 16 * n_xll[sl])
            p.wait("vector", s_gsb, nbias + i + 1)
            if n_cmb >= 1:
                p.wait("vector", s_cmb, n_cmb)
            csl = (n_cmb // 3) % 2
            if n_cmb >= 6:
                p.wait("vector", s_out[csl], 16 * (n_cmb // 6))
            p.op("vector", lambda e, sl=sl, o=o: e.tensor_scalar(
                out=ssb[sl][:, :], in0=sst[sl][:, :], scalar1=bos[:, o:o + 1],
                scalar2=None, op0=ALU.add), inc=s_cmb)
            n_cmb += 1
            p.wait("vector", s_cmb, n_cmb)
            p.op("vector", lambda e, sl=sl: e.tensor_mul(out=tmul[:, :], in0=ssb[sl][:, :],
                                                         in1=gsb[sl][:, :]), inc=s_cmb)
            n_cmb += 1
            p.wait("vector", s_cmb, n_cmb)
            p.op("vector", lambda e, sl=sl, csl=csl: e.tensor_add(out=x1s[csl][:, :],
                                                                  in0=xst[sl][:, :], in1=tmul[:, :]),
                 inc=s_cmb)
            n_cmb += 1
            p.wait("gpsimd", s_cmb, n_cmb)
            p.dma("gpsimd", x1[o * 128:(o + 1) * 128, w * 512:(w + 1) * 512], x1s[csl][:, :],
                  inc=s_out[csl])

    p.wait("gpsimd", s_out[0], s_out[0].n)
    p.wait("gpsimd", s_out[1], s_out[1].n)


def build_tail0s():
    nc = bass.Bass("TRN2", target_bir_lowering=False)
    U8 = mybir.dt.uint8
    xT = nc.dram_tensor("xT", [D, T_OWN], F32, kind="ExternalInput").ap()
    xnin = nc.dram_tensor("xnin", [D, T_OWN], BF16, kind="ExternalInput").ap()
    sred = nc.dram_tensor("sred", [D, T_OWN], F32, kind="ExternalInput").ap()
    mrows = nc.dram_tensor("mrows", [NEXP, T_OWN], F32, kind="ExternalInput").ap()
    w1t = nc.dram_tensor("w1t", [NEXP, NDH, 128, D], F32, kind="ExternalInput").ap()
    b1c = nc.dram_tensor("b1c", [NEXP, 128, NDH], F32, kind="ExternalInput").ap()
    w2 = nc.dram_tensor("w2", [NEXP, DH, D], F32, kind="ExternalInput").ap()
    b2 = nc.dram_tensor("b2", [NEXP, D], F32, kind="ExternalInput").ap()
    boc = nc.dram_tensor("boc", [128, NCH], F32, kind="ExternalInput").ap()
    sut = nc.dram_tensor("sut", [128, 128], F32, kind="ExternalInput").ap()
    su16 = nc.dram_tensor("su16", [16, 16], F32, kind="ExternalInput").ap()
    iot = nc.dram_tensor("iot", [128, C1], F32, kind="ExternalInput").ap()
    idf = nc.dram_tensor("idf", [128, 128], F32, kind="ExternalInput").ap()
    idb = nc.dram_tensor("idb", [128, 128], BF16, kind="ExternalInput").ap()
    ctd = nc.dram_tensor("ctd", [1, D], F32).ap()
    x1 = nc.dram_tensor("x1", [D, T_OWN], F32, kind="ExternalOutput").ap()
    with ExitStack() as stack:
        en = stack.enter_context
        pool = en(nc.sbuf_tensor("pool", [128, _POOL_BYTES], U8))
        banks = [en(nc.psum_tensor(f"bank{j}", [128, 512], F32)) for j in range(8)]
        B = _Bufs(nc, stack, pool=pool, banks=banks)
        p = _Prog(nc, stack)
        _emit_tail0s(nc, p, B, xT=xT, xnin=xnin, sred=sred, mrows=mrows, w1t=w1t, b1c=b1c,
                     w2=w2, b2=b2, boc=boc, x1=x1, sut=sut, su16=su16, iot=iot, idf=idf,
                     idb=idb, ctd=ctd)
        block = en(nc.Block())
        p.replay(block)
    return nc


def build_tail1s(debug=False):
    nc = bass.Bass("TRN2", target_bir_lowering=False)
    U8 = mybir.dt.uint8
    xT = nc.dram_tensor("xT", [D, T_OWN], F32, kind="ExternalInput").ap()
    xnin = nc.dram_tensor("xnin", [D, T_OWN], BF16, kind="ExternalInput").ap()
    sred = nc.dram_tensor("sred", [D, T_OWN], F32, kind="ExternalInput").ap()
    wrows = nc.dram_tensor("wrows", [NEXP, T_OWN], F32, kind="ExternalInput").ap()
    w0t = nc.dram_tensor("w0t", [NEXP, NDH, 128, D], F32, kind="ExternalInput").ap()
    w1t = nc.dram_tensor("w1t", [NEXP, NDH, 128, D], F32, kind="ExternalInput").ap()
    wo = nc.dram_tensor("wo", [NEXP, DH, D], F32, kind="ExternalInput").ap()
    sut = nc.dram_tensor("sut", [128, 128], F32, kind="ExternalInput").ap()
    su16 = nc.dram_tensor("su16", [16, 16], F32, kind="ExternalInput").ap()
    iot = nc.dram_tensor("iot", [128, C1], F32, kind="ExternalInput").ap()
    idf = nc.dram_tensor("idf", [128, 128], F32, kind="ExternalInput").ap()
    idb = nc.dram_tensor("idb", [128, 128], BF16, kind="ExternalInput").ap()
    x2 = nc.dram_tensor("x2", [D, T_OWN], F32, kind="ExternalOutput").ap()
    dbg = {}
    if debug:
        dbg = {
            "dbg_pm1": nc.dram_tensor("dbg_pm1", [128, 16], F32, kind="ExternalOutput").ap(),
            "dbg_xg": nc.dram_tensor("dbg_xg", [128, 16 * JW1], BF16, kind="ExternalOutput").ap(),
            "dbg_gt": nc.dram_tensor("dbg_gt", [128, NJT1 * T_OWN], BF16, kind="ExternalOutput").ap(),
            "dbg_yg": nc.dram_tensor("dbg_yg", [128, 16 * JW1], BF16, kind="ExternalOutput").ap(),
            "dbg_yacc": nc.dram_tensor("dbg_yacc", [128, NCH * T_OWN], BF16, kind="ExternalOutput").ap(),
            "dbg_mtok": nc.dram_tensor("dbg_mtok", [128, 16 * NEXP], F32, kind="ExternalOutput").ap(),
            "dbg_mbin": nc.dram_tensor("dbg_mbin", [128, 16], F32, kind="ExternalOutput").ap(),
            "dbg_pcols": nc.dram_tensor("dbg_pcols", [128, 16], F32, kind="ExternalOutput").ap(),
            "dbg_xntok": nc.dram_tensor("dbg_xntok", [128, 16 * NCH * 128], BF16, kind="ExternalOutput").ap(),
            "dbg_hall": nc.dram_tensor("dbg_hall", [128, NDH * C1], BF16, kind="ExternalOutput").ap(),
        }
    with ExitStack() as stack:
        en = stack.enter_context
        pool = en(nc.sbuf_tensor("pool", [128, _POOL_BYTES], U8))
        banks = [en(nc.psum_tensor(f"bank{j}", [128, 512], F32)) for j in range(8)]
        B = _Bufs(nc, stack, pool=pool, banks=banks)
        p = _Prog(nc, stack)
        _emit_tail1s(nc, p, B, xT=xT, xnin=xnin, sred=sred, wrows=wrows, w0t=w0t, w1t=w1t,
                     wo=wo, x2=x2, sut=sut, su16=su16, iot=iot, idf=idf, idb=idb, **dbg)
        block = en(nc.Block())
        p.replay(block)
    return nc




_POOL_BYTES = 204 * 1024


def build_fused():
    nc = bass.Bass("TRN2", target_bir_lowering=False)
    U8 = mybir.dt.uint8
    xT = nc.dram_tensor("xT", [D, T_OWN], F32, kind="ExternalInput").ap()
    wr0 = nc.dram_tensor("wr0", [D, NEXP], F32, kind="ExternalInput").ap()
    brt0 = nc.dram_tensor("brt0", [128, NEXP], F32, kind="ExternalInput").ap()
    wq = nc.dram_tensor("wq", [D, 512], F32, kind="ExternalInput").ap()
    wk = nc.dram_tensor("wk", [D, 512], F32, kind="ExternalInput").ap()
    wv = nc.dram_tensor("wv", [D, 512], F32, kind="ExternalInput").ap()
    bq = nc.dram_tensor("bq", [128, 4], F32, kind="ExternalInput").ap()
    bk = nc.dram_tensor("bk", [128, 4], F32, kind="ExternalInput").ap()
    bv = nc.dram_tensor("bv", [1, 512], F32, kind="ExternalInput").ap()
    ta = nc.dram_tensor("ta", [128, T_FULL], BF16, kind="ExternalInput").ap()
    tb = nc.dram_tensor("tb", [128, T_FULL], BF16, kind="ExternalInput").ap()
    perm = nc.dram_tensor("perm", [128, 128], BF16, kind="ExternalInput").ap()
    wOo0 = nc.dram_tensor("wOo0", [512, D], F32, kind="ExternalInput").ap()
    l0w1t = nc.dram_tensor("l0w1t", [NEXP, NDH, 128, D], F32, kind="ExternalInput").ap()
    b1c = nc.dram_tensor("b1c", [NEXP, 128, NDH], F32, kind="ExternalInput").ap()
    l0w2 = nc.dram_tensor("l0w2", [NEXP, DH, D], F32, kind="ExternalInput").ap()
    b2 = nc.dram_tensor("b2", [NEXP, D], F32, kind="ExternalInput").ap()
    boc = nc.dram_tensor("boc", [128, NCH], F32, kind="ExternalInput").ap()
    wg1 = nc.dram_tensor("wg1", [D, NEXP], F32, kind="ExternalInput").ap()
    brt1 = nc.dram_tensor("brt1", [128, NEXP], F32, kind="ExternalInput").ap()
    wdq = nc.dram_tensor("wdq", [D, 256], F32, kind="ExternalInput").ap()
    wdk = nc.dram_tensor("wdk", [D, 256], F32, kind="ExternalInput").ap()
    wuq = nc.dram_tensor("wuq", [256, 512], F32, kind="ExternalInput").ap()
    wqr = nc.dram_tensor("wqr", [256, 256], F32, kind="ExternalInput").ap()
    wuk = nc.dram_tensor("wuk", [256, 512], F32, kind="ExternalInput").ap()
    wuv = nc.dram_tensor("wuv", [256, 512], F32, kind="ExternalInput").ap()
    wkr = nc.dram_tensor("wkr", [D, 32], F32, kind="ExternalInput").ap()
    ta1 = nc.dram_tensor("ta1", [64, T_FULL], BF16, kind="ExternalInput").ap()
    tb1 = nc.dram_tensor("tb1", [64, T_FULL], BF16, kind="ExternalInput").ap()
    prm16 = nc.dram_tensor("prm16", [64, 64], BF16, kind="ExternalInput").ap()
    wOo1 = nc.dram_tensor("wOo1", [512, D], F32, kind="ExternalInput").ap()
    l1w0t = nc.dram_tensor("l1w0t", [NEXP, NDH, 128, D], F32, kind="ExternalInput").ap()
    l1w1t = nc.dram_tensor("l1w1t", [NEXP, NDH, 128, D], F32, kind="ExternalInput").ap()
    l1wo = nc.dram_tensor("l1wo", [NEXP, DH, D], F32, kind="ExternalInput").ap()
    sutd = nc.dram_tensor("sut", [128, 128], F32, kind="ExternalInput").ap()
    su16d = nc.dram_tensor("su16", [16, 16], F32, kind="ExternalInput").ap()
    iotd = nc.dram_tensor("iot", [128, C1], F32, kind="ExternalInput").ap()
    idfd = nc.dram_tensor("idf", [128, 128], F32, kind="ExternalInput").ap()
    idbd = nc.dram_tensor("idb", [128, 128], BF16, kind="ExternalInput").ap()
    xn0o = nc.dram_tensor("xn0o", [D, T_OWN], BF16).ap()
    xn0g = nc.dram_tensor("xn0g", [2, D, T_OWN], BF16).ap()
    rows0 = nc.dram_tensor("rows0", [NEXP, T_OWN], F32).ap()
    sp0 = nc.dram_tensor("sp0", [2 * D, T_OWN], F32).ap()
    sr0 = nc.dram_tensor("sr0", [D, T_OWN], F32).ap()
    x1d = nc.dram_tensor("x1d", [D, T_OWN], F32).ap()
    ctd0 = nc.dram_tensor("ctd0", [1, D], F32).ap()
    xn1o = nc.dram_tensor("xn1o", [D, T_OWN], BF16).ap()
    xn1g = nc.dram_tensor("xn1g", [2, D, T_OWN], BF16).ap()
    rows1 = nc.dram_tensor("rows1", [NEXP, T_OWN], F32).ap()
    sp1 = nc.dram_tensor("sp1", [2 * D, T_OWN], F32).ap()
    sr1 = nc.dram_tensor("sr1", [D, T_OWN], F32).ap()
    x2 = nc.dram_tensor("x2", [D, T_OWN], F32, kind="ExternalOutput").ap()

    with ExitStack() as stack:
        en = stack.enter_context
        pool = en(nc.sbuf_tensor("pool", [128, _POOL_BYTES], U8))
        banks = [en(nc.psum_tensor(f"bank{j}", [128, 512], F32)) for j in range(8)]
        B = _Bufs(nc, stack, pool=pool, banks=banks)
        p = _Prog(nc, stack)

        p.prefix = "a_"; B.reset()
        _emit_pro(nc, p, B, 1, xT=xT, wr=wr0, brt=brt0, xn=xn0o, rows=rows0)
        p.barrier(colls=[("AllGather", ALU.bypass, xn0o.opt(),
                          xn0g.rearrange("a b c -> (a b) c").opt())])

        p.prefix = "b_"; B.reset()
        _emit_attn0(nc, p, B, xnin=xn0g, wq=wq, wk=wk, wv=wv, bq=bq, bk=bk, bv=bv,
                    ta=ta, tb=tb, perm=perm, wOo=wOo0, sp=sp0)
        p.barrier(colls=[("ReduceScatter", ALU.add, sp0.opt(), sr0.opt())])

        p.prefix = "c_"; B.reset()
        _emit_tail0s(nc, p, B, xT=xT, xnin=xn0o, sred=sr0, mrows=rows0,
                     w1t=l0w1t, b1c=b1c, w2=l0w2, b2=b2, boc=boc, x1=x1d,
                     sut=sutd, su16=su16d, iot=iotd, idf=idfd, idb=idbd, ctd=ctd0)
        p.barrier()

        p.prefix = "d_"; B.reset()
        _emit_pro(nc, p, B, 2, xT=x1d, wr=wg1, brt=brt1, xn=xn1o, rows=rows1)
        p.barrier(colls=[("AllGather", ALU.bypass, xn1o.opt(),
                          xn1g.rearrange("a b c -> (a b) c").opt())])

        p.prefix = "e_"; B.reset()
        _emit_attn1(nc, p, B, xnin=xn1g, wdq=wdq, wdk=wdk, wuq=wuq, wqr=wqr, wuk=wuk,
                    wuv=wuv, wkr=wkr, ta1=ta1, tb1=tb1, prm16=prm16, wOo=wOo1, sp=sp1)
        p.barrier(colls=[("ReduceScatter", ALU.add, sp1.opt(), sr1.opt())])

        p.prefix = "f_"; B.reset()
        _emit_tail1s(nc, p, B, xT=x1d, xnin=xn1o, sred=sr1, wrows=rows1,
                     w0t=l1w0t, w1t=l1w1t, wo=l1wo, x2=x2,
                     sut=sutd, su16=su16d, iot=iotd, idf=idfd, idb=idbd)

        block = en(nc.Block())
        p.replay(block)
    return nc


_NC_CACHE = {}


def _get_nc(key, builder, *args):
    if key not in _NC_CACHE:
        _NC_CACHE[key] = builder(*args)
    return _NC_CACHE[key]


def _run(nc, in_maps):
    res = run_bass_kernel_spmd(nc, in_maps, core_ids=list(range(NCORES)))
    return res.results


_PERM64 = np.concatenate([np.arange(0, 64, 2), np.arange(1, 64, 2)])
_PERM32 = np.concatenate([np.arange(0, 32, 2), np.arange(1, 32, 2)])


def kernel_unfused(**inputs):
    x = np.ascontiguousarray(inputs["L0_attn_x"])
    Wr = np.ascontiguousarray(inputs["L0_ffn_Wr"])
    br = np.ascontiguousarray(inputs["L0_ffn_br"])
    w_qkv = inputs["L0_attn_w_qkv"]; b_qkv = inputs["L0_attn_b_qkv"]
    w_out = inputs["L0_attn_w_out"]; b_out = inputs["L0_attn_b_out"]
    W1 = inputs["L0_ffn_W1"]; b1 = inputs["L0_ffn_b1"]
    W2 = np.ascontiguousarray(inputs["L0_ffn_W2"]); b2 = np.ascontiguousarray(inputs["L0_ffn_b2"])
    wg = np.ascontiguousarray(inputs["L1_ffn_wg"])
    w0l1 = inputs["L1_ffn_w0"]; w1l1 = inputs["L1_ffn_w1"]
    wol1 = np.ascontiguousarray(inputs["L1_ffn_wo"])
    WO1 = np.ascontiguousarray(inputs["L1_attn_W_O"])

    wT = np.ascontiguousarray(w_qkv.T)
    ta0, tb0 = _rope_tables(2, 32)
    perm0 = _perm_mat(128, 32)
    ta1, tb1 = _rope_tables(2, 16)
    prm16 = _perm_mat(64, 16)
    w1t0 = _tile_ffn_w(W1)
    b1c0 = np.ascontiguousarray(b1.reshape(NEXP, NDH, 128).transpose(0, 2, 1))
    woT0 = np.ascontiguousarray(w_out.T)
    boc0 = np.ascontiguousarray(b_out.reshape(NCH, 128).T)
    w0t1 = _tile_ffn_w(w0l1)
    w1t1 = _tile_ffn_w(w1l1)
    brt0 = np.ascontiguousarray(np.broadcast_to(br[None, :], (128, NEXP)).astype(np.float32))
    brt1 = np.zeros((128, NEXP), dtype=np.float32)
    attw = {}
    for par in range(2):
        heads = np.arange(par * 8, par * 8 + 8)
        qcols = np.concatenate([h * 64 + _PERM64 for h in heads])
        vcols = np.concatenate([h * 64 + np.arange(64) for h in heads])
        rcols = np.concatenate([h * 32 + _PERM32 for h in heads])
        attw[par] = {
            "wq": np.ascontiguousarray(wT[:, qcols]),
            "wk": np.ascontiguousarray(wT[:, 1024 + qcols]),
            "wv": np.ascontiguousarray(wT[:, 2048 + vcols]),
            "bq": np.ascontiguousarray(b_qkv[qcols].reshape(4, 128).T),
            "bk": np.ascontiguousarray(b_qkv[1024 + qcols].reshape(4, 128).T),
            "bv": np.ascontiguousarray(b_qkv[2048 + vcols][None, :]),
            "wuq": np.ascontiguousarray(inputs["L1_attn_W_UQ"][:, vcols]),
            "wqr": np.ascontiguousarray(inputs["L1_attn_W_QR"][:, rcols]),
            "wuk": np.ascontiguousarray(inputs["L1_attn_W_UK"][:, vcols]),
            "wuv": np.ascontiguousarray(inputs["L1_attn_W_UV"][:, vcols]),
        }
    wkr1 = np.ascontiguousarray(inputs["L1_attn_W_KR"][:, _PERM32])
    wdq1 = np.ascontiguousarray(inputs["L1_attn_W_DQ"])
    wdk1 = np.ascontiguousarray(inputs["L1_attn_W_DKV"])

    def xslice(arr3, c):
        seq, par = c // 2, c % 2
        return np.ascontiguousarray(arr3[seq, par * T_OWN:(par + 1) * T_OWN, :].T)

    nc = _get_nc("pro1", build_pro, 1)
    res = _run(nc, [{"xT": xslice(x, c), "wr": Wr, "brt": brt0} for c in range(NCORES)])
    xn0 = [res[c]["xn"] for c in range(NCORES)]
    rows0 = [res[c]["rows"] for c in range(NCORES)]

    nc = _get_nc("attn0", build_attn0)
    in_maps = []
    for c in range(NCORES):
        seq, par = c // 2, c % 2
        xnin = np.ascontiguousarray(np.stack([xn0[2 * seq], xn0[2 * seq + 1]], axis=0))
        m = {"xnin": xnin, "ta": ta0, "tb": tb0, "perm": perm0}
        for k2 in ("wq", "wk", "wv", "bq", "bk", "bv"):
            m[k2] = attw[par][k2]
        in_maps.append(m)
    res = _run(nc, in_maps)
    at0 = [res[c]["attn"] for c in range(NCORES)]

    nc = _get_nc("tail0", build_tail0)
    in_maps = []
    for c in range(NCORES):
        seq, par = c // 2, c % 2
        attn_full = np.concatenate([at0[2 * seq], at0[2 * seq + 1]], axis=0)
        in_maps.append({
            "xT": xslice(x, c),
            "xnin": xn0[c],
            "attn": np.ascontiguousarray(attn_full[:, par * T_OWN:(par + 1) * T_OWN]),
            "mrows": rows0[c],
            "w1t": w1t0, "b1c": b1c0, "w2": W2, "b2": b2,
            "woT": woT0, "boc": boc0,
        })
    res = _run(nc, in_maps)
    x1 = [np.ascontiguousarray(res[c]["x1"]) for c in range(NCORES)]

    nc = _get_nc("pro2", build_pro, 2)
    res = _run(nc, [{"xT": x1[c], "wr": wg, "brt": brt1} for c in range(NCORES)])
    xn1 = [res[c]["xn"] for c in range(NCORES)]
    rows1 = [res[c]["rows"] for c in range(NCORES)]

    nc = _get_nc("attn1", build_attn1)
    in_maps = []
    for c in range(NCORES):
        seq, par = c // 2, c % 2
        xnin = np.ascontiguousarray(np.stack([xn1[2 * seq], xn1[2 * seq + 1]], axis=0))
        m = {"xnin": xnin, "wdq": wdq1, "wdk": wdk1, "wkr": wkr1,
             "ta1": ta1, "tb1": tb1, "prm16": prm16}
        for k2 in ("wuq", "wqr", "wuk", "wuv"):
            m[k2] = attw[par][k2]
        in_maps.append(m)
    res = _run(nc, in_maps)
    at1 = [res[c]["attn"] for c in range(NCORES)]

    nc = _get_nc("tail1", build_tail1)
    in_maps = []
    for c in range(NCORES):
        seq, par = c // 2, c % 2
        attn_full = np.concatenate([at1[2 * seq], at1[2 * seq + 1]], axis=0)
        in_maps.append({
            "xT": x1[c],
            "xnin": xn1[c],
            "attn": np.ascontiguousarray(attn_full[:, par * T_OWN:(par + 1) * T_OWN]),
            "wrows": rows1[c],
            "w0t": w0t1, "w1t": w1t1, "wo": wol1, "wO": WO1,
        })
    res = _run(nc, in_maps)

    out = np.empty((4, T_FULL, D), dtype=np.float32)
    for c in range(NCORES):
        seq, par = c // 2, c % 2
        out[seq, par * T_OWN:(par + 1) * T_OWN, :] = res[c]["x2"].T
    return out


def kernel(**inputs):
    x = np.ascontiguousarray(inputs["L0_attn_x"])
    Wr = np.ascontiguousarray(inputs["L0_ffn_Wr"])
    br = np.ascontiguousarray(inputs["L0_ffn_br"])
    w_qkv = inputs["L0_attn_w_qkv"]; b_qkv = inputs["L0_attn_b_qkv"]
    w_out = inputs["L0_attn_w_out"]; b_out = inputs["L0_attn_b_out"]
    W1 = inputs["L0_ffn_W1"]; b1 = inputs["L0_ffn_b1"]
    W2 = np.ascontiguousarray(inputs["L0_ffn_W2"]); b2v = np.ascontiguousarray(inputs["L0_ffn_b2"])
    wg = np.ascontiguousarray(inputs["L1_ffn_wg"])

    wT = np.ascontiguousarray(w_qkv.T)
    woT = np.ascontiguousarray(w_out.T)
    WO1 = np.ascontiguousarray(inputs["L1_attn_W_O"])
    ta0, tb0 = _rope_tables(2, 32)
    perm0 = _perm_mat(128, 32)
    ta1t, tb1t = _rope_tables(2, 16)
    prm16 = _perm_mat(64, 16)
    w1t0 = _tile_ffn_w(W1)
    b1c0 = np.ascontiguousarray(b1.reshape(NEXP, NDH, 128).transpose(0, 2, 1))
    boc0 = np.ascontiguousarray(b_out.reshape(NCH, 128).T)
    w0t1 = _tile_ffn_w(inputs["L1_ffn_w0"])
    w1t1 = _tile_ffn_w(inputs["L1_ffn_w1"])
    wol1 = np.ascontiguousarray(inputs["L1_ffn_wo"])
    brt0 = np.ascontiguousarray(np.broadcast_to(br[None, :], (128, NEXP)).astype(np.float32))
    brt1 = np.zeros((128, NEXP), dtype=np.float32)
    shared = {
        "wr0": Wr, "brt0": brt0, "ta": ta0, "tb": tb0, "perm": perm0,
        "l0w1t": w1t0, "b1c": b1c0, "l0w2": W2, "b2": b2v, "boc": boc0,
        "wg1": wg, "brt1": brt1, "ta1": ta1t, "tb1": tb1t, "prm16": prm16,
        "wdq": np.ascontiguousarray(inputs["L1_attn_W_DQ"]),
        "wdk": np.ascontiguousarray(inputs["L1_attn_W_DKV"]),
        "wkr": np.ascontiguousarray(inputs["L1_attn_W_KR"][:, _PERM32]),
        "l1w0t": w0t1, "l1w1t": w1t1, "l1wo": wol1,
        "sut": np.triu(np.ones((128, 128), dtype=np.float32)),
        "su16": np.triu(np.ones((16, 16), dtype=np.float32), 1),
        "iot": np.broadcast_to(np.arange(C1, dtype=np.float32), (128, C1)).copy(),
        "idf": np.eye(128, dtype=np.float32),
        "idb": np.eye(128, dtype=np.float32).astype(BF),
    }
    perw = {}
    for par in range(2):
        heads = np.arange(par * 8, par * 8 + 8)
        qcols = np.concatenate([h * 64 + _PERM64 for h in heads])
        vcols = np.concatenate([h * 64 + np.arange(64) for h in heads])
        rcols = np.concatenate([h * 32 + _PERM32 for h in heads])
        perw[par] = {
            "wq": np.ascontiguousarray(wT[:, qcols]),
            "wk": np.ascontiguousarray(wT[:, 1024 + qcols]),
            "wv": np.ascontiguousarray(wT[:, 2048 + vcols]),
            "bq": np.ascontiguousarray(b_qkv[qcols].reshape(4, 128).T),
            "bk": np.ascontiguousarray(b_qkv[1024 + qcols].reshape(4, 128).T),
            "bv": np.ascontiguousarray(b_qkv[2048 + vcols][None, :]),
            "wOo0": np.ascontiguousarray(woT[par * 512:(par + 1) * 512, :]),
            "wuq": np.ascontiguousarray(inputs["L1_attn_W_UQ"][:, vcols]),
            "wqr": np.ascontiguousarray(inputs["L1_attn_W_QR"][:, rcols]),
            "wuk": np.ascontiguousarray(inputs["L1_attn_W_UK"][:, vcols]),
            "wuv": np.ascontiguousarray(inputs["L1_attn_W_UV"][:, vcols]),
            "wOo1": np.ascontiguousarray(WO1[par * 512:(par + 1) * 512, :]),
        }
    in_maps = []
    for c in range(NCORES):
        seq, par = c // 2, c % 2
        m = {"xT": np.ascontiguousarray(x[seq, par * T_OWN:(par + 1) * T_OWN, :].T)}
        m.update(shared)
        m.update(perw[par])
        in_maps.append(m)
    nc = _get_nc("fused", build_fused)
    res = _run(nc, in_maps)
    out = np.empty((4, T_FULL, D), dtype=np.float32)
    for c in range(NCORES):
        seq, par = c // 2, c % 2
        out[seq, par * T_OWN:(par + 1) * T_OWN, :] = res[c]["x2"].T
    return out
